# version 06
# baseline (speedup 1.0000x reference)
import collections
import os
from contextlib import ExitStack

_KPH = int(os.environ.get("KPH", "9"))
_KP4S = int(os.environ.get("KP4S", "4"))
_KP4Q = int(os.environ.get("KP4Q", "4"))
_KP4L = int(os.environ.get("KP4L", "6"))

import numpy as np
import ml_dtypes

import concourse.bass as bass
import concourse.mybir as mybir
from concourse.bass_utils import run_bass_kernel_spmd

F32 = mybir.dt.float32
BF16 = mybir.dt.bfloat16
AF = mybir.ActivationFunctionType
ALU = mybir.AluOpType
AX = mybir.AxisListType.X

B, T, C = 2, 8192, 1024
DH = 4096
SPAN = 512
NSLOT = 4
OWN = NSLOT * SPAN
ESCH = [16, 32, 48, 64]


def spans_for_class(g):
    return [g - 1, 8 - g, 7 + g, 16 - g]


class Sched:
    ENGS = ("pe", "sc", "ve", "sy", "gp")

    def __init__(self):
        self.recs = []
        self.cnt = collections.defaultdict(int)
        self.marks = {}

    def mark(self, ph):
        self.marks[ph] = len(self.recs)

    def emit(self, eng, fn, waits=None, inc=None):
        assert eng in self.ENGS
        self.recs.append((eng, dict(waits or {}), fn, inc))
        if inc is not None:
            self.cnt[inc[0]] += inc[1]
            return self.cnt[inc[0]]
        return None

    def c(self, sem):
        return self.cnt[sem]


def build_nc():
    nc = bass.Bass("TRN2", target_bir_lowering=False, debug=False,
                   detect_race_conditions=False)

    dt_in = lambda n, shp, dt=F32: nc.dram_tensor(n, shp, dt, kind="ExternalInput")
    x_own_t = dt_in("x_own", [OWN, C])
    x_batch_t = dt_in("x_batch", [T, C])
    W_attn_t = dt_in("W_attn", [C, 3 * C])
    W_proj_t = dt_in("W_proj", [C, C])
    w1t_t = dt_in("w1t", [32, 128, 1024])
    w2t_t = dt_in("w2t", [32, 128, 1024])
    h1t_t = dt_in("h1t", [32, 128, 4096])
    h2t_t = dt_in("h2t", [32, 128, 4096])
    w3t_t = dt_in("w3t", [32, 128, 1024])
    s1c_t = dt_in("s1c", [128, 32])
    s2c_t = dt_in("s2c", [128, 32])
    sh1c_t = dt_in("sh1c", [128, 32])
    sh2c_t = dt_in("sh2c", [128, 32])
    s3bc_t = dt_in("s3bc", [128, 1024])
    masks_t = dt_in("masks", [NSLOT, 16, 128, 512], BF16)
    out_t = nc.dram_tensor("out", [OWN, C], F32, kind="ExternalOutput")

    _dbg = os.environ.get("KDBG") == "1"
    dt_int = lambda n, shp, dt: nc.dram_tensor(
        n, shp, dt, kind=("ExternalOutput" if _dbg else "Internal"))
    xno_d = dt_int("xno_d", [OWN, C], BF16)
    xn_d = dt_int("xn_d", [T, C], BF16)
    kT_d = dt_int("kT_d", [C, T], BF16)
    v_d = dt_int("v_d", [T, C], BF16)
    y_d = dt_int("y_d", [C, OWN], BF16)
    h2o_d = dt_int("h2o_d", [DH, OWN], BF16)
    lnst_d = dt_int("lnst_d", [80, 128, 8], F32)
    w1b_d = dt_int("w1b_d", [32, 128, 1024], BF16)
    w2b_d = dt_int("w2b_d", [32, 128, 1024], BF16)
    h1b_d = dt_int("h1b_d", [32, 128, 4096], BF16)
    h2b_d = dt_int("h2b_d", [32, 128, 4096], BF16)

    x_own = x_own_t.ap()
    x_batch = x_batch_t.ap()
    W_attn = W_attn_t.ap()
    W_proj = W_proj_t.ap()
    out = out_t.ap()

    S = Sched()

    with ExitStack() as es:
        e = es.enter_context
        BQ = e(nc.sbuf_tensor("BQ", [128, 16384], BF16))
        BX = e(nc.sbuf_tensor("BX", [128, 16384], BF16))
        BH = e(nc.sbuf_tensor("BH", [128, 32768], BF16))
        BF = e(nc.sbuf_tensor("BF", [128, 7680], F32))
        BXW = e(nc.sbuf_tensor("BXW", [128, 4096], BF16))
        P6Y = e(nc.sbuf_tensor("P6Y", [128, 2048], BF16))
        SCL = e(nc.sbuf_tensor("SCL", [128, 152], F32))
        pb = [e(nc.psum_tensor(f"pb{i}", [128, 512], F32)) for i in range(8)]

        sem_names = ["X0", "X1", "XH0", "XH1", "XOT", "WAT", "PRO",
                     "MK0", "MK1", "KV0", "KV1", "WS0", "WS1",
                     "W30", "W31", "H20", "H21", "YG0", "YG1", "XT0", "XT1",
                     "CAST", "PE", "SC", "VE",
                     "GS0", "GS1", "KS0", "KS1", "VS0", "VS1",
                     "YS0", "YS1", "HS0", "HS1", "OS0", "OS1",
                     "ZPE", "ZSC", "ZVE"]
        sem = {n: e(nc.semaphore(n)) for n in sem_names}

        COL_S1, COL_S2, COL_SH1, COL_SH2 = 0, 32, 64, 96
        COL_B005, COL_EPS, COL_VS, COL_QS = 128, 129, 130, 131
        COL_SUM, COL_SSQ, COL_MEAN, COL_MSQ = 132, 134, 136, 138
        COL_VAR, COL_TMP, COL_RSTD = 140, 142, 144

        def scl(col):
            return SCL[:, col:col + 1]

        for col, t in ((COL_S1, s1c_t), (COL_S2, s2c_t), (COL_SH1, sh1c_t),
                       (COL_SH2, sh2c_t)):
            S.emit("sy", lambda g, c0=col, tt=t: g.dma_start(
                out=SCL[:, c0:c0 + 32], in_=tt.ap()), inc=("PRO", 16))
        pro_scales = S.c("PRO")
        S.emit("ve", lambda g: g.memset(scl(COL_B005), 0.005), inc=("VE", 1))
        S.emit("ve", lambda g: g.memset(SCL[:, 146:147], 0.25), inc=("VE", 1))
        ve_warm = S.c("VE")
        for fn_ in (AF.Sqrt, AF.Silu, AF.Gelu, AF.Sqrt):
            for rep in range(4):
                S.emit("sc", lambda g, f=fn_: g.activation(
                    SCL[:, 147:148], SCL[:, 146:147], f),
                    waits={"VE": ve_warm}, inc=("SC", 1))
        S.emit("ve", lambda g: g.memset(scl(COL_EPS), 1e-5), inc=("VE", 1))
        S.emit("ve", lambda g: g.memset(scl(COL_VS), 1e-3), inc=("VE", 1))
        S.emit("ve", lambda g: g.memset(scl(COL_QS), 0.125), inc=("VE", 1))

        CH = 16384
        st_post = 0
        for src_t, dst_t, nel in ((w1t_t, w1b_d, 32 * 128 * 1024),
                                  (w2t_t, w2b_d, 32 * 128 * 1024),
                                  (h1t_t, h1b_d, 32 * 128 * 4096),
                                  (h2t_t, h2b_d, 32 * 128 * 4096)):
            per = nel // 128
            for c0 in range(0, per, CH):
                i_ap = bass.AP(src_t, c0, [[per, 128], [1, CH]])
                o_ap = bass.AP(dst_t, c0, [[per, 128], [1, CH]])
                ld_post = S.emit("gp", lambda g, i=i_ap: g.dma_start(
                    out=BH[:, 0:CH], in_=i), waits={"CAST": st_post},
                    inc=("CAST", 16))
                st_post = S.emit("gp", lambda g, o=o_ap: g.dma_start(
                    out=o, in_=BH[:, 0:CH]), waits={"CAST": ld_post},
                    inc=("CAST", 16))
        cast_total = S.c("CAST")
        for ic in range(8):
            S.emit("gp", lambda g, ic=ic: g.dma_start(
                out=BQ[:, ic * 2048:(ic + 1) * 2048],
                in_=W_attn[ic * 128:(ic + 1) * 128, 1024:3072]), inc=("WAT", 16))
        wat_kv = S.c("WAT")

        ln_ve, ln_sc, ln_store, ln_ssq = {}, {}, {}, {}

        def ln_tile(ti, i, src_ap, dst_d):
            par = ti % 2
            xs = BF[:, par * 1024:(par + 1) * 1024]
            xnb = BH[:, par * 1024:(par + 1) * 1024]
            sq = BF[:, 2048:3072]
            xsem = "X0" if par == 0 else "X1"
            waits = {}
            if ti >= 2:
                waits = {"VE": ln_ve[ti - 2], "SC": ln_sc[ti - 2]}
            if ti == 0:
                waits["CAST"] = cast_total
            post = S.emit("sy", lambda g, a=xs, s=src_ap, i=i: g.dma_start(
                out=a, in_=s[i * 128:(i + 1) * 128, :]), waits=waits, inc=(xsem, 16))
            S.emit("ve", lambda g, a=xs, c=par: g.reduce_sum(
                SCL[:, COL_SUM + c:COL_SUM + c + 1], a, axis=AX),
                waits={xsem: post}, inc=("VE", 1))
            sqw = {xsem: post}
            if ti >= 1:
                sqw["VE"] = ln_ssq[ti - 1]
            ln_sc[ti] = S.emit("sc", lambda g, a=xs, c=par: g.activation(
                sq, a, AF.Square),
                waits=sqw, inc=("SC", 1))
            ln_ssq[ti] = S.emit("ve", lambda g, c=par: g.reduce_sum(
                SCL[:, COL_SSQ + c:COL_SSQ + c + 1], sq, axis=AX),
                waits={"SC": ln_sc[ti]}, inc=("VE", 1))
            S.emit("ve", lambda g, c=par: g.tensor_scalar_mul(
                SCL[:, COL_MEAN + c:COL_MEAN + c + 1],
                SCL[:, COL_SUM + c:COL_SUM + c + 1], 1.0 / 1024.0))
            S.emit("ve", lambda g, c=par: g.tensor_mul(
                SCL[:, COL_MSQ + c:COL_MSQ + c + 1],
                SCL[:, COL_MEAN + c:COL_MEAN + c + 1],
                SCL[:, COL_MEAN + c:COL_MEAN + c + 1]))
            ve_var = S.emit("ve", lambda g, c=par: g.scalar_tensor_tensor(
                SCL[:, COL_VAR + c:COL_VAR + c + 1],
                SCL[:, COL_SSQ + c:COL_SSQ + c + 1], 1.0 / 1024.0,
                SCL[:, COL_MSQ + c:COL_MSQ + c + 1], ALU.mult, ALU.subtract),
                inc=("VE", 1))
            sc_sqrt = S.emit("sc", lambda g, c=par: g.activation(
                SCL[:, COL_TMP + c:COL_TMP + c + 1],
                SCL[:, COL_VAR + c:COL_VAR + c + 1], AF.Sqrt,
                bias=scl(COL_EPS)), waits={"VE": ve_var}, inc=("SC", 1))
            S.emit("ve", lambda g, c=par: g.reciprocal(
                SCL[:, COL_RSTD + c:COL_RSTD + c + 1],
                SCL[:, COL_TMP + c:COL_TMP + c + 1]), waits={"SC": sc_sqrt})
            st_stage = SCL[:, 148 - par * 0:148]
            ve_dump = S.emit("ve", lambda g, c=par: g.tensor_copy(
                BF[:, 3072 + 8 * (1 + c):3072 + 8 * (1 + c) + 7],
                bass.AP(SCL, 132 + c, [[152, 128], [2, 7]])), inc=("VE", 1))
            S.emit("gp", lambda g, ti=ti, c=par: g.dma_start(
                out=bass.AP(lnst_d, ti * 128 * 8, [[8, 128], [1, 7]]),
                in_=BF[:, 3072 + 8 * (1 + c):3072 + 8 * (1 + c) + 7]),
                waits={"VE": ve_dump}, inc=("CAST", 16))
            gsem = "GS0" if par == 0 else "GS1"
            w2 = {gsem: ln_store[ti - 2]} if ti >= 2 else {}
            ln_ve[ti] = S.emit("ve", lambda g, a=xs, o=xnb, c=par: g.tensor_scalar(
                o, a, SCL[:, COL_MEAN + c:COL_MEAN + c + 1],
                SCL[:, COL_RSTD + c:COL_RSTD + c + 1], ALU.subtract, ALU.mult),
                waits=w2, inc=("VE", 1))
            ln_store[ti] = S.emit("gp", lambda g, o=xnb, d=dst_d, i=i: g.dma_start(
                out=d.ap()[i * 128:(i + 1) * 128, :], in_=o),
                waits={"VE": ln_ve[ti]}, inc=(gsem, 16))

        for i in range(16):
            ln_tile(i, i, x_own, xno_d)
        ph1_gs = {"GS0": S.c("GS0"), "GS1": S.c("GS1")}
        for i in range(64):
            ln_tile(16 + i, i, x_batch, xn_d)
        ph2_gs = {"GS0": S.c("GS0"), "GS1": S.c("GS1")}
        S.mark(1)

        prev_use = {}
        kst_prev = {}
        vst_prev = {}
        hg_last_pe = {}
        for hg in range(8):
            par = hg % 2
            xsem = "XH0" if par == 0 else "XH1"
            w = dict(ph2_gs) if hg == 0 else {}
            if hg >= 2:
                w["PE"] = hg_last_pe[hg - 2]
            xh_post = None
            for ic in range(8):
                xh_post = S.emit("sy", lambda g, hg=hg, ic=ic, p=par: g.dma_start_transpose(
                    BH[:, p * 8192 + ic * 1024: p * 8192 + (ic + 1) * 1024],
                    xn_d.ap()[hg * 1024:(hg + 1) * 1024, ic * 128:(ic + 1) * 128]),
                    waits=(w if ic == 0 else {}), inc=(xsem, 16))
            for oc in range(8):
                b0, b1 = pb[(oc % 2) * 2], pb[(oc % 2) * 2 + 1]
                kk = ("k", oc % 2)
                mmw = {xsem: xh_post, "WAT": wat_kv}
                if kk in prev_use:
                    mmw["SC"] = prev_use[kk]
                pe_post = None
                for ic in range(8):
                    lhs = BQ[:, ic * 2048 + oc * 128: ic * 2048 + (oc + 1) * 128]
                    rhs0 = BH[:, par * 8192 + ic * 1024: par * 8192 + ic * 1024 + 512]
                    rhs1 = BH[:, par * 8192 + ic * 1024 + 512: par * 8192 + (ic + 1) * 1024]
                    S.emit("pe", lambda g, o=b0, l=lhs, r=rhs0, st=(ic == 0), sp=(ic == 7):
                           g.matmul(o[:], l, r, start=st, stop=sp),
                           waits=(mmw if ic == 0 else {}))
                    pe_post = S.emit("pe", lambda g, o=b1, l=lhs, r=rhs1, st=(ic == 0), sp=(ic == 7):
                                     g.matmul(o[:], l, r, start=st, stop=sp),
                                     inc=("PE", 1) if ic == 7 else None)
                kst = BH[:, 16384 + (oc % 2) * 1024: 16384 + (oc % 2) * 1024 + 1024]
                ksem = "KS0" if oc % 2 == 0 else "KS1"
                evw = {"PE": pe_post}
                if (oc % 2) in kst_prev:
                    evw[ksem] = kst_prev[oc % 2]
                S.emit("sc", lambda g, o=kst, s=b0: g.copy(o[:, 0:512], s[:]), waits=evw)
                prev_use[kk] = S.emit("sc", lambda g, o=kst, s=b1: g.copy(
                    o[:, 512:1024], s[:]), inc=("SC", 1))
                kst_prev[oc % 2] = S.emit("sc", lambda g, o=kst, hg=hg, oc=oc: g.dma_start(
                    out=kT_d.ap()[oc * 128:(oc + 1) * 128, hg * 1024:(hg + 1) * 1024],
                    in_=o), waits={"SC": prev_use[kk]}, inc=(ksem, 16))
            for tc in range(8):
                b0, b1 = pb[4 + (tc % 2) * 2], pb[4 + (tc % 2) * 2 + 1]
                kk = ("v", tc % 2)
                mmw = {}
                if kk in prev_use:
                    mmw["SC"] = prev_use[kk]
                pe_post = None
                for ic in range(8):
                    lhs = BH[:, par * 8192 + ic * 1024 + tc * 128: par * 8192 + ic * 1024 + (tc + 1) * 128]
                    rhs0 = BQ[:, ic * 2048 + 1024: ic * 2048 + 1536]
                    rhs1 = BQ[:, ic * 2048 + 1536: ic * 2048 + 2048]
                    S.emit("pe", lambda g, o=b0, l=lhs, r=rhs0, st=(ic == 0), sp=(ic == 7):
                           g.matmul(o[:], l, r, start=st, stop=sp),
                           waits=(mmw if ic == 0 else {}))
                    pe_post = S.emit("pe", lambda g, o=b1, l=lhs, r=rhs1, st=(ic == 0), sp=(ic == 7):
                                     g.matmul(o[:], l, r, start=st, stop=sp),
                                     inc=("PE", 1) if ic == 7 else None)
                vst = BH[:, 18432 + (tc % 2) * 1024: 18432 + (tc % 2) * 1024 + 1024]
                vsem = "VS0" if tc % 2 == 0 else "VS1"
                evw = {"PE": pe_post}
                if (tc % 2) in vst_prev:
                    evw[vsem] = vst_prev[tc % 2]
                S.emit("sc", lambda g, o=vst, s=b0: g.activation(
                    o[:, 0:512], s[:], AF.Copy, scale=scl(COL_VS)), waits=evw)
                prev_use[kk] = S.emit("sc", lambda g, o=vst, s=b1: g.activation(
                    o[:, 512:1024], s[:], AF.Copy, scale=scl(COL_VS)), inc=("SC", 1))
                vst_prev[tc % 2] = S.emit("sc", lambda g, o=vst, hg=hg, tc=tc: g.dma_start(
                    out=v_d.ap()[hg * 1024 + tc * 128: hg * 1024 + (tc + 1) * 128, :],
                    in_=o), waits={"SC": prev_use[kk]}, inc=(vsem, 16))
            hg_last_pe[hg] = S.c("PE")
        ph3_pe_total = S.c("PE")
        ph3_sc_total = S.c("SC")
        ph3_kv_st = {s: S.c(s) for s in ("KS0", "KS1", "VS0", "VS1")}
        S.mark(2)

        for ic in range(8):
            S.emit("gp", lambda g, ic=ic: g.dma_start(
                out=BH[:, 16384 + ic * 1024: 16384 + (ic + 1) * 1024],
                in_=W_attn[ic * 128:(ic + 1) * 128, 0:1024]),
                waits=(dict(ph3_kv_st) if ic == 0 else {}), inc=("WAT", 16))
        wat_q = S.c("WAT")
        for ic in range(8):
            S.emit("sy", lambda g, ic=ic: g.dma_start_transpose(
                BX[:, ic * 2048:(ic + 1) * 2048],
                xno_d.ap()[:, ic * 128:(ic + 1) * 128]),
                waits=(dict(ph1_gs) if ic == 0 else {}), inc=("XOT", 16))
        xot_total = S.c("XOT")
        qevac = {}
        for oc in range(8):
            banks = [pb[0], pb[1], pb[2], pb[3]] if oc % 2 == 0 else [pb[4], pb[5], pb[6], pb[7]]
            mmw = {"XOT": xot_total, "WAT": wat_q,
                   "SC": qevac.get(oc - 2, ph3_sc_total)}
            pe_post = None
            for ic in range(8):
                lhs = BH[:, 16384 + ic * 1024 + oc * 128: 16384 + ic * 1024 + (oc + 1) * 128]
                for j in range(4):
                    rhs = BX[:, ic * 2048 + j * 512: ic * 2048 + (j + 1) * 512]
                    pe_post = S.emit("pe", lambda g, o=banks[j], l=lhs, r=rhs,
                                     st=(ic == 0), sp=(ic == 7): g.matmul(
                                         o[:], l, r, start=st, stop=sp),
                                     waits=(mmw if (ic == 0 and j == 0) else {}),
                                     inc=("PE", 1) if (ic == 7 and j == 3) else None)
            for j in range(4):
                qevac[oc] = S.emit("sc", lambda g, oc=oc, j=j, s=banks[j]: g.activation(
                    BQ[:, oc * 2048 + j * 512: oc * 2048 + (j + 1) * 512],
                    s[:], AF.Copy, scale=scl(COL_QS)),
                    waits={"PE": pe_post}, inc=("SC", 1))
        ph35_sc_total = S.c("SC")
        S.mark(3)

        pm = {}
        kv_prev = {}
        mask_prev = {}
        yev = {0: ph35_sc_total, 1: ph35_sc_total}
        yst_prev = {}
        first_kv = True
        pe_av = None
        for s in range(_KP4S):
            mpar = s % 2
            msem = "MK0" if mpar == 0 else "MK1"
            mw = {}
            if s == 0:
                mw["PE"] = ph3_pe_total
            if mpar in mask_prev:
                mw["VE"] = mask_prev[mpar]
            for j in range(16):
                S.emit("sy", lambda g, s=s, j=j, mp=mpar: g.dma_start(
                    out=BH[:, mp * 8192 + j * 512: mp * 8192 + (j + 1) * 512],
                    in_=bass.AP(masks_t, (s * 16 + j) * 128 * 512,
                                [[512, 128], [1, 512]])),
                    waits=(mw if j == 0 else {}), inc=(msem, 16))
            mask_total = S.c(msem)
            nkb = ESCH[s]
            for q in range(_KP4Q):
                kv_post = {}
                for kb in range(nkb):
                    sup, jj = kb // 4, kb % 4
                    spar = sup % 2
                    ksem = "KV0" if spar == 0 else "KV1"
                    if jj == 0:
                        lw = {}
                        if first_kv:
                            lw.update(ph3_kv_st)
                            first_kv = False
                        if spar in kv_prev:
                            lw["PE"] = kv_prev[spar]
                        for hpc in range(2):
                            S.emit("sy", lambda g, q=q, hpc=hpc, sup=sup, p=spar: g.dma_start(
                                out=BXW[:, p * 1024 + hpc * 512: p * 1024 + (hpc + 1) * 512],
                                in_=kT_d.ap()[(q * 2 + hpc) * 128:(q * 2 + hpc + 1) * 128,
                                              sup * 512:(sup + 1) * 512]),
                                waits=(lw if hpc == 0 else {}), inc=(ksem, 16))
                        for vj in range(4):
                            kv_post[sup] = S.emit("sy", lambda g, q=q, vj=vj, sup=sup, p=spar: g.dma_start(
                                out=BXW[:, 2048 + p * 1024 + vj * 256: 2048 + p * 1024 + (vj + 1) * 256],
                                in_=v_d.ap()[sup * 512 + vj * 128: sup * 512 + (vj + 1) * 128,
                                             q * 256:(q + 1) * 256]), inc=(ksem, 16))
                    for hp in range(2):
                        pp = (kb * 2 + hp) % 2
                        bA, bB = pb[pp * 2], pb[pp * 2 + 1]
                        qoff = (q * 2 + hp) * 2048 + s * 512
                        koff = spar * 1024 + hp * 512 + jj * 128
                        if _KP4L < 1:
                            continue
                        qkw = {ksem: kv_post[sup]}
                        if pp in pm:
                            qkw["SC"] = pm[pp].get("sc", 0)
                            qkw["VE"] = pm[pp].get("ve3", 0)
                        else:
                            qkw["SC"] = ph35_sc_total
                        qkw = {k: v for k, v in qkw.items() if v}
                        S.emit("pe", lambda g, o=bA, ko=koff, qo=qoff: g.matmul(
                            o[:], BXW[0:64, ko:ko + 128], BQ[0:64, qo:qo + 512],
                            start=True, stop=True, tile_position=(0, 0)), waits=qkw)
                        pe_qk = S.emit("pe", lambda g, o=bB, ko=koff, qo=qoff: g.matmul(
                            o[:], BXW[64:128, ko:ko + 128], BQ[64:128, qo:qo + 512],
                            start=True, stop=True, tile_position=(64, 0)), inc=("PE", 1))
                        RA, RB = BF[:, 0:512], BF[:, 512:1024]
                        qA = BF[:, 1024 + pp * 1024: 1536 + pp * 1024]
                        qB = BF[:, 1536 + pp * 1024: 2048 + pp * 1024]
                        tA = BF[:, 3072 + pp * 1024: 3584 + pp * 1024]
                        tB = BF[:, 3584 + pp * 1024: 4096 + pp * 1024]
                        wA = BF[:, 5120 + pp * 1024: 5632 + pp * 1024]
                        wB = BF[:, 5632 + pp * 1024: 6144 + pp * 1024]
                        AA = BH[:, 16384 + (pp * 2) * 512: 16384 + (pp * 2 + 1) * 512]
                        AB = BH[:, 16384 + (pp * 2 + 1) * 512: 16384 + (pp * 2 + 2) * 512]
                        if _KP4L < 2:
                            pm[pp] = {"pe_qk": pe_qk}
                            kv_prev[spar] = pe_qk
                            pe_av = pe_qk
                            continue
                        scw = {"PE": pe_qk}
                        if pp in pm:
                            scw["VE"] = pm[pp].get("ve5", 0)
                        scw = {k: v for k, v in scw.items() if v}
                        S.emit("sc", lambda g, o=RA, i=bA: g.activation(o, i[:], AF.Relu),
                               waits=scw)
                        S.emit("sc", lambda g, o=RB, i=bB: g.activation(o, i[:], AF.Relu))
                        S.emit("sc", lambda g, o=qA, i=RA: g.activation(
                            o, i, AF.Square, bias=scl(COL_B005)))
                        sc_p = S.emit("sc", lambda g, o=qB, i=RB: g.activation(
                            o, i, AF.Square, bias=scl(COL_B005)), inc=("SC", 1))
                        if _KP4L < 3:
                            pm[pp] = {"sc": sc_p}
                            kv_prev[spar] = pe_qk
                            pe_av = pe_qk
                            continue
                        S.emit("ve", lambda g, o=tA, i=bA: g.tensor_scalar(
                            o, i[:], -0.01, 0.999975, ALU.mult, ALU.add),
                            waits={"PE": pe_qk, "SC": sc_p})
                        ve3 = S.emit("ve", lambda g, o=tB, i=bB: g.tensor_scalar(
                            o, i[:], -0.01, 0.999975, ALU.mult, ALU.add), inc=("VE", 1))
                        if _KP4L < 4:
                            pm[pp] = {"sc": sc_p, "ve3": ve3}
                            kv_prev[spar] = pe_qk
                            pe_av = pe_qk
                            continue
                        vew = {"SC": sc_p}
                        if pp in pm:
                            vew["PE"] = pm[pp].get("pe_av", 0)
                        vew = {k: v for k, v in vew.items() if v}
                        S.emit("ve", lambda g, o=wA, a=qA, b=tA: g.tensor_tensor(
                            o, a, b, ALU.add), waits=vew)
                        S.emit("ve", lambda g, o=wB, a=qB, b=tB: g.tensor_tensor(
                            o, a, b, ALU.add))
                        S.emit("ve", lambda g, o=AA, t=tA, w=wA: g.scalar_tensor_tensor(
                            o, t, 1.999975, w, ALU.is_le, ALU.mult))
                        ve5 = S.emit("ve", lambda g, o=AB, t=tB, w=wB: g.scalar_tensor_tensor(
                            o, t, 1.999975, w, ALU.is_le, ALU.mult), inc=("VE", 1))
                        if kb >= 16 * s and _KP4L >= 5:
                            moff = mpar * 8192 + (kb - 16 * s) * 512
                            S.emit("ve", lambda g, o=AA, mo=moff: g.tensor_tensor(
                                o, o, BH[:, mo:mo + 512], ALU.mult),
                                waits={msem: mask_total})
                            ve5 = S.emit("ve", lambda g, o=AB, mo=moff: g.tensor_tensor(
                                o, o, BH[:, mo:mo + 512], ALU.mult), inc=("VE", 1))
                            mask_prev[mpar] = ve5
                        if _KP4L < 6:
                            pm[pp] = {"sc": sc_p, "ve3": ve3, "ve5": ve5}
                            kv_prev[spar] = pe_qk
                            pe_av = pe_qk
                            continue
                        voff = 2048 + spar * 1024 + jj * 256 + hp * 128
                        yb = pb[4 + hp]
                        avw = {"VE": ve5}
                        if kb == 0:
                            avw["SC"] = yev[hp]
                        S.emit("pe", lambda g, o=yb, vo=voff, a=AA, st=(kb == 0),
                               sp=(kb == nkb - 1): g.matmul(
                                   o[0:64, :], BXW[:, vo:vo + 64], a,
                                   start=st, stop=sp, tile_position=(0, 0)), waits=avw)
                        pe_av = S.emit("pe", lambda g, o=yb, vo=voff, a=AB,
                                       st=(kb == 0), sp=(kb == nkb - 1): g.matmul(
                                           o[64:128, :], BXW[:, vo + 64:vo + 128], a,
                                           start=st, stop=sp, tile_position=(0, 64),
                                           skip_group_check=True),
                                       inc=("PE", 1))
                        pm[pp] = {"sc": sc_p, "ve3": ve3, "ve5": ve5, "pe_av": pe_av}
                        kv_prev[spar] = pe_av
                for hp in range(2 if _KP4L >= 6 else 0):
                    yst = BH[:, 18432 + hp * 512: 18432 + (hp + 1) * 512]
                    ysem2 = "YS0" if hp == 0 else "YS1"
                    evw = {"PE": pe_av}
                    if hp in yst_prev:
                        evw[ysem2] = yst_prev[hp]
                    yev[hp] = S.emit("sc", lambda g, o=yst, s=pb[4 + hp]: g.copy(
                        o, s[:]), waits=evw, inc=("SC", 1))
                    yst_prev[hp] = S.emit("sc", lambda g, o=yst, q=q, hp=hp, s=s: g.dma_start(
                        out=y_d.ap()[(q * 2 + hp) * 128:(q * 2 + hp + 1) * 128,
                                     s * 512:(s + 1) * 512], in_=o),
                        waits={"SC": yev[hp]}, inc=(ysem2, 16))
        ph4_y_st = {"YS0": S.c("YS0"), "YS1": S.c("YS1")}
        ph4_pe_total = S.c("PE")
        ph4_sc_total = S.c("SC")
        S.mark(4)

        w_item = 0
        ws_prev = {}
        ffn_gate = {}
        h2_last_pe = ph4_pe_total
        h2o_sst = {}
        for sp in range(NSLOT):
            for rep in range(4):
                S.emit("sc", lambda g: g.activation(
                    SCL[:, 147:148], SCL[:, 146:147], AF.Silu), inc=("SC", 1))
            for oc in range(32):
                par = w_item % 2
                wsem = "WS0" if par == 0 else "WS1"
                lw = {}
                if sp == 0 and oc < 2:
                    lw["CAST"] = cast_total
                    lw["PE"] = ph4_pe_total
                if par in ws_prev:
                    lw["PE"] = max(lw.get("PE", 0), ws_prev[par])
                S.emit("sy", lambda g, oc=oc, p=par: g.dma_start(
                    out=BXW[:, p * 2048: p * 2048 + 1024],
                    in_=bass.AP(w1b_d, oc * 128 * 1024, [[1024, 128], [1, 1024]])),
                    waits=lw, inc=(wsem, 16))
                ws_post = S.emit("sy", lambda g, oc=oc, p=par: g.dma_start(
                    out=BXW[:, p * 2048 + 1024: p * 2048 + 2048],
                    in_=bass.AP(w2b_d, oc * 128 * 1024, [[1024, 128], [1, 1024]])),
                    inc=(wsem, 16))
                b1, b2 = pb[oc % 2], pb[2 + oc % 2]
                mmw = {wsem: ws_post, "PE": h2_last_pe}
                gk = ("w12", oc % 2)
                if gk in ffn_gate:
                    mmw["SC"] = ffn_gate[gk]["sc"]
                    mmw["VE"] = ffn_gate[gk]["ve"]
                elif sp == 0:
                    mmw["SC"] = ph4_sc_total
                for ic in range(8):
                    rhs = BX[:, ic * 2048 + sp * 512: ic * 2048 + (sp + 1) * 512]
                    S.emit("pe", lambda g, o=b1, p=par, ic=ic, r=rhs, st=(ic == 0),
                           sp2=(ic == 7): g.matmul(
                               o[:], BXW[:, p * 2048 + ic * 128: p * 2048 + (ic + 1) * 128],
                               r, start=st, stop=sp2),
                           waits=(mmw if ic == 0 else {}),
                           inc=("PE", 1) if ic == 7 else None)
                pe_w1 = S.c("PE")
                for ic in range(8):
                    rhs = BX[:, ic * 2048 + sp * 512: ic * 2048 + (sp + 1) * 512]
                    S.emit("pe", lambda g, o=b2, p=par, ic=ic, r=rhs, st=(ic == 0),
                           sp2=(ic == 7): g.matmul(
                               o[:], BXW[:, p * 2048 + 1024 + ic * 128: p * 2048 + 1024 + (ic + 1) * 128],
                               r, start=st, stop=sp2),
                           inc=("PE", 1) if ic == 7 else None)
                pe_w2 = S.c("PE")
                ws_prev[par] = pe_w2
                silu = BF[:, (oc % 2) * 512:(oc % 2) * 512 + 512]
                sc_post = S.emit("sc", lambda g, o=silu, s=b1, oc=oc: g.activation(
                    o, s[:], AF.Silu, scale=SCL[:, COL_S1 + oc:COL_S1 + oc + 1]),
                    waits={"PE": pe_w1, "PRO": pro_scales}, inc=("SC", 1))
                ve_post = S.emit("ve", lambda g, s=b2, si=silu, oc=oc: g.scalar_tensor_tensor(
                    BH[:, oc * 512:(oc + 1) * 512], s[:],
                    SCL[:, COL_S2 + oc:COL_S2 + oc + 1], si, ALU.mult, ALU.mult),
                    waits={"PE": pe_w2, "SC": sc_post}, inc=("VE", 1))
                ffn_gate[gk] = {"sc": sc_post, "ve": ve_post}
                w_item += 1
            h_ready = S.c("VE")
            for rep in range(4):
                S.emit("sc", lambda g: g.activation(
                    SCL[:, 147:148], SCL[:, 146:147], AF.Gelu), inc=("SC", 1))
            for oc2 in range(32):
                halves = []
                for half in range(2):
                    par = w_item % 2
                    wsem = "WS0" if par == 0 else "WS1"
                    lw = {}
                    if par in ws_prev:
                        lw["PE"] = ws_prev[par]
                    hp_post = S.emit("sy", lambda g, oc2=oc2, half=half, p=par: g.dma_start(
                        out=BXW[:, p * 2048:(p + 1) * 2048],
                        in_=bass.AP(h1b_d, oc2 * 128 * 4096 + half * 2048,
                                    [[4096, 128], [1, 2048]])),
                        waits=lw, inc=(wsem, 16))
                    halves.append((par, wsem, hp_post))
                    w_item += 1
                bk = pb[4 + oc2 % 2]
                gk = ("h1", oc2 % 2)
                for hc in range(32):
                    par, wsem, hp_post = halves[hc // 16]
                    mmw = {}
                    if hc in (0, 16):
                        mmw[wsem] = hp_post
                        if hc == 0:
                            mmw["VE"] = h_ready
                            mmw["SC"] = ffn_gate.get(gk, ph4_sc_total)
                    rhs = BH[:, hc * 512:(hc + 1) * 512]
                    S.emit("pe", lambda g, o=bk, p=par, hc=hc, r=rhs, st=(hc == 0),
                           sp2=(hc == 31): g.matmul(
                               o[:], BXW[:, p * 2048 + (hc % 16) * 128: p * 2048 + ((hc % 16) + 1) * 128],
                               r, start=st, stop=sp2), waits=mmw,
                           inc=("PE", 1) if hc == 31 else None)
                pe_h1 = S.c("PE")
                ws_prev[halves[0][0]] = pe_h1
                ws_prev[halves[1][0]] = pe_h1
                ffn_gate[gk] = S.emit("sc", lambda g, oc2=oc2, s=bk: g.activation(
                    BH[:, 16384 + oc2 * 512: 16384 + (oc2 + 1) * 512], s[:], AF.Gelu,
                    scale=SCL[:, COL_SH1 + oc2:COL_SH1 + oc2 + 1]),
                    waits={"PE": pe_h1}, inc=("SC", 1))
            g_ready = S.c("SC")
            for oc2 in range(32):
                halves = []
                for half in range(2):
                    par = w_item % 2
                    wsem = "WS0" if par == 0 else "WS1"
                    lw = {}
                    if par in ws_prev:
                        lw["PE"] = ws_prev[par]
                    hp_post = S.emit("sy", lambda g, oc2=oc2, half=half, p=par: g.dma_start(
                        out=BXW[:, p * 2048:(p + 1) * 2048],
                        in_=bass.AP(h2b_d, oc2 * 128 * 4096 + half * 2048,
                                    [[4096, 128], [1, 2048]])),
                        waits=lw, inc=(wsem, 16))
                    halves.append((par, wsem, hp_post))
                    w_item += 1
                bk = pb[6 + oc2 % 2]
                gk = ("h2", oc2 % 2)
                for hc in range(32):
                    par, wsem, hp_post = halves[hc // 16]
                    mmw = {}
                    if hc in (0, 16):
                        mmw[wsem] = hp_post
                        if hc == 0:
                            mmw["SC"] = max(g_ready, ffn_gate.get(gk, 0))
                    rhs = BH[:, 16384 + hc * 512: 16384 + (hc + 1) * 512]
                    S.emit("pe", lambda g, o=bk, p=par, hc=hc, r=rhs, st=(hc == 0),
                           sp2=(hc == 31): g.matmul(
                               o[:], BXW[:, p * 2048 + (hc % 16) * 128: p * 2048 + ((hc % 16) + 1) * 128],
                               r, start=st, stop=sp2), waits=mmw,
                           inc=("PE", 1) if hc == 31 else None)
                pe_h2 = S.c("PE")
                ws_prev[halves[0][0]] = pe_h2
                ws_prev[halves[1][0]] = pe_h2
                h2stg = P6Y[:, (oc2 % 2) * 512:(oc2 % 2 + 1) * 512]
                hsem2 = "HS0" if oc2 % 2 == 0 else "HS1"
                evw = {"PE": pe_h2}
                if (oc2 % 2) in h2o_sst:
                    evw[hsem2] = h2o_sst[oc2 % 2]
                ffn_gate[gk] = S.emit("sc", lambda g, o=h2stg, s=bk, oc2=oc2: g.activation(
                    o, s[:], AF.Gelu, scale=SCL[:, COL_SH2 + oc2:COL_SH2 + oc2 + 1]),
                    waits=evw, inc=("SC", 1))
                h2o_sst[oc2 % 2] = S.emit("sc", lambda g, o=h2stg, oc2=oc2, sp=sp: g.dma_start(
                    out=h2o_d.ap()[oc2 * 128:(oc2 + 1) * 128, sp * 512:(sp + 1) * 512],
                    in_=o), waits={"SC": ffn_gate[gk]}, inc=(hsem2, 16))
            h2_last_pe = S.c("PE")
        ph5_h_st = {"HS0": S.c("HS0"), "HS1": S.c("HS1")}
        ph5_pe_total = S.c("PE")
        ph5_sc_total = S.c("SC")
        ph5_ve_total = S.c("VE")
        S.mark(5)

        for cc in range(8):
            S.emit("gp", lambda g, cc=cc: g.dma_start(
                out=BQ[:, cc * 1024:(cc + 1) * 1024],
                in_=W_proj[cc * 128:(cc + 1) * 128, :]),
                waits=({"PE": ph5_pe_total} if cc == 0 else {}), inc=("WAT", 16))
        wat_total = S.c("WAT")
        S.emit("sy", lambda g: g.dma_start(out=BF[:, 2048:3072], in_=s3bc_t.ap()),
               waits={"PE": ph5_pe_total, "VE": ph5_ve_total}, inc=("PRO", 16))
        pro_total = S.c("PRO")
        w3cv = {}
        for hc in range(32):
            par = hc % 2
            wsem = "W30" if par == 0 else "W31"
            lw = {}
            if hc < 2:
                lw["PE"] = ph5_pe_total
                lw["VE"] = ph5_ve_total
            else:
                lw["VE"] = w3cv[hc - 2]
            w3_post = S.emit("sy", lambda g, hc=hc, p=par: g.dma_start(
                out=BF[:, p * 1024:(p + 1) * 1024],
                in_=bass.AP(w3t_t, hc * 128 * 1024, [[1024, 128], [1, 1024]])),
                waits=lw, inc=(wsem, 16))
            cvw = {wsem: w3_post, "PRO": pro_total}
            if hc < 2:
                cvw["PE"] = ph5_pe_total
            w3cv[hc] = S.emit("ve", lambda g, hc=hc, p=par: g.tensor_tensor(
                BH[:, hc * 1024:(hc + 1) * 1024], BF[:, p * 1024:(p + 1) * 1024],
                BF[:, 2048:3072], ALU.mult), waits=cvw, inc=("VE", 1))
        w3_ready = S.c("VE")

        h2l_prev = {}
        yg_prev = {}
        xt_prev = {}
        outstore = {}
        bank_gate = {}
        for grp in range(4):
            h2_posts = []
            for half in range(2):
                hsem = "H20" if half == 0 else "H21"
                lw = {}
                if grp == 0:
                    lw.update(ph5_h_st)
                if (grp - 1, half) in h2l_prev:
                    lw["PE"] = h2l_prev[(grp - 1, half)]
                hpost = None
                for part in range(2):
                    hcb = half * 16 + part * 8
                    i_ap = bass.AP(h2o_d, hcb * 128 * OWN + grp * 512,
                                   [[OWN, 128], [128 * OWN, 8], [1, 512]])
                    hpost = S.emit("sy", lambda g, i=i_ap, half=half, part=part: g.dma_start(
                        out=BX[:, half * 8192 + part * 4096: half * 8192 + (part + 1) * 4096],
                        in_=i), waits=(lw if part == 0 else {}), inc=(hsem, 16))
                h2_posts.append((hsem, hpost))
            ygpar = grp % 2
            ysem = "YG0" if ygpar == 0 else "YG1"
            lw = {}
            if grp == 0:
                lw.update(ph4_y_st)
            if (grp - 2, ygpar) in yg_prev:
                lw["PE"] = yg_prev[(grp - 2, ygpar)]
            y_post = None
            for part in range(2):
                i_ap = bass.AP(y_d, (part * 4) * 128 * OWN + grp * 512,
                               [[OWN, 128], [128 * OWN, 4], [1, 512]])
                y_post = S.emit("sy", lambda g, i=i_ap, p=ygpar, part=part: g.dma_start(
                    out=BQ[:, 8192 + p * 4096 + part * 2048: 8192 + p * 4096 + (part + 1) * 2048],
                    in_=i), waits=(lw if part == 0 else {}), inc=(ysem, 16))
            for half in range(2):
                hsem, hpost = h2_posts[half]
                for tc4 in range(4):
                    for oh in range(2):
                        bk = pb[tc4 * 2 + oh]
                        for hcl in range(16):
                            hc = half * 16 + hcl
                            mmw = {}
                            if hcl == 0:
                                mmw[hsem] = hpost
                                if half == 0:
                                    mmw["VE"] = max(w3_ready,
                                                    bank_gate.get((grp - 1, tc4, oh), 0))
                                    if grp == 0:
                                        mmw["SC"] = ph5_sc_total
                            lhs_off = half * 8192 + (hcl // 8) * 4096 + (hcl % 8) * 512 + tc4 * 128
                            rhs = BH[:, hc * 1024 + oh * 512: hc * 1024 + (oh + 1) * 512]
                            S.emit("pe", lambda g, o=bk, lo=lhs_off, r=rhs,
                                   st=(hc == 0): g.matmul(
                                       o[:], BX[:, lo:lo + 128], r,
                                       start=st, stop=False), waits=mmw)
            for tc4 in range(4):
                tc = grp * 4 + tc4
                xpar = tc % 2
                xsem = "XT0" if xpar == 0 else "XT1"
                lw = {}
                if tc < 2:
                    lw["PE"] = ph5_pe_total
                if (tc - 2) in xt_prev:
                    lw["VE"] = xt_prev[tc - 2]
                xpost = S.emit("sy", lambda g, tc=tc, p=xpar: g.dma_start(
                    out=BF[:, 3072 + p * 1024: 3072 + (p + 1) * 1024],
                    in_=x_own[tc * 128:(tc + 1) * 128, :]), waits=lw, inc=(xsem, 16))
                for oh in range(2):
                    bk = pb[tc4 * 2 + oh]
                    pe_fin = None
                    for cc in range(8):
                        mmw = {}
                        if cc == 0:
                            mmw = {ysem: y_post, "WAT": wat_total}
                        lhs_off = 8192 + ygpar * 4096 + (cc // 4) * 2048 + (cc % 4) * 512 + tc4 * 128
                        rhs = BQ[:, cc * 1024 + oh * 512: cc * 1024 + (oh + 1) * 512]
                        pe_fin = S.emit("pe", lambda g, o=bk, lo=lhs_off, r=rhs,
                                        sp2=(cc == 7): g.matmul(
                                            o[:], BQ[:, lo:lo + 128], r,
                                            start=False, stop=sp2), waits=mmw,
                                        inc=("PE", 1) if cc == 7 else None)
                    yg_prev[(grp, ygpar)] = pe_fin
                    h2l_prev[(grp, 0)] = pe_fin
                    h2l_prev[(grp, 1)] = pe_fin
                    obuf = BF[:, 5120 + (tc % 2) * 1024 + oh * 512: 5120 + (tc % 2) * 1024 + (oh + 1) * 512]
                    xsl = BF[:, 3072 + (tc % 2) * 1024 + oh * 512: 3072 + (tc % 2) * 1024 + (oh + 1) * 512]
                    osem = "OS0" if tc % 2 == 0 else "OS1"
                    evw = {"PE": pe_fin, xsem: xpost}
                    if (tc - 2) in outstore:
                        evw[osem] = outstore[tc - 2]
                    ov = S.emit("ve", lambda g, o=obuf, s=bk, x=xsl: g.tensor_add(
                        o, s[:], x), waits=evw, inc=("VE", 1))
                    bank_gate[(grp, tc4, oh)] = ov
                    xt_prev[tc] = ov
                tc = grp * 4 + tc4
                outstore[tc] = S.emit("gp", lambda g, tc=tc: g.dma_start(
                    out=out[tc * 128:(tc + 1) * 128, :],
                    in_=BF[:, 5120 + (tc % 2) * 1024: 5120 + (tc % 2) * 1024 + 1024]),
                    waits={"VE": xt_prev[tc]},
                    inc=("OS0" if tc % 2 == 0 else "OS1", 16))
        S.mark(6)
        if _KPH < 6 and _KPH in S.marks:
            S.recs = S.recs[:S.marks[_KPH]]
        tot = collections.defaultdict(int)
        for (_en, _w, _fn, _inc) in S.recs:
            if _inc:
                tot[_inc[0]] += _inc[1]
        fin = {k: tot[k] for k in ("GS0", "GS1", "KS0", "KS1", "VS0", "VS1",
                                   "YS0", "YS1", "HS0", "HS1", "OS0", "OS1")
               if tot[k] > 0}
        S.recs.append(("gp", fin, None, None))

        for k, v in S.cnt.items():
            assert v < 60000, (k, v)

        ksafe_engs = os.environ.get("KSAFE", "pe,sc,ve").replace(",", " ").split()
        selfsem_all = {"pe": "ZPE", "sc": "ZSC", "ve": "ZVE"}
        selfsem = {k: v for k, v in selfsem_all.items() if k in ksafe_engs}

        with nc.Block() as block:
            def replay(eng_name):
                last = {}
                zs = selfsem.get(eng_name)

                def run(g):
                    cnt = collections.defaultdict(int)
                    zn = 0
                    prev = None
                    for (en, waits, fn, inc) in S.recs:
                        if en != eng_name:
                            if inc is not None:
                                cnt[inc[0]] += inc[1]
                            continue
                        for sname, val in waits.items():
                            if last.get(sname, -1) < val:
                                g.wait_ge(sem[sname], val)
                                last[sname] = val
                        if fn is None:
                            continue
                        if zs is not None and prev is not None:
                            if last.get(prev[0], -1) < prev[1]:
                                g.wait_ge(sem[prev[0]], prev[1])
                                last[prev[0]] = prev[1]
                        inst = fn(g)
                        is_dma = getattr(inst, "is_dma", None)
                        if inc is not None:
                            cnt[inc[0]] += inc[1]
                            inst.then_inc(sem[inc[0]], inc[1])
                            if inc[1] == 1:
                                prev = (inc[0], cnt[inc[0]])
                            else:
                                prev = None
                        elif zs is not None:
                            zn += 1
                            inst.then_inc(sem[zs], 1)
                            prev = (zs, zn)
                        else:
                            prev = None
                return run

            block.tensor(replay("pe"))
            block.scalar(replay("sc"))
            block.vector(replay("ve"))
            block.sync(replay("sy"))
            block.gpsimd(replay("gp"))

    return nc


_NC_CACHE = None


def _get_nc():
    global _NC_CACHE
    if _NC_CACHE is None:
        _NC_CACHE = build_nc()
    return _NC_CACHE


def kernel(L0_attn_x, L0_attn_W_attn, L0_attn_W_proj,
           L0_ffn_w1_w, L0_ffn_w1_s, L0_ffn_w2_w, L0_ffn_w2_s,
           L0_ffn_h1_w, L0_ffn_h1_s, L0_ffn_h2_w, L0_ffn_h2_s,
           L0_ffn_w3_w, L0_ffn_w3_s):
    x = np.ascontiguousarray(np.asarray(L0_attn_x, dtype=np.float32))
    W_attn = np.ascontiguousarray(np.asarray(L0_attn_W_attn, dtype=np.float32))
    W_proj = np.ascontiguousarray(np.asarray(L0_attn_W_proj, dtype=np.float32))
    w1 = np.asarray(L0_ffn_w1_w, dtype=np.float32)
    w2 = np.asarray(L0_ffn_w2_w, dtype=np.float32)
    h1 = np.asarray(L0_ffn_h1_w, dtype=np.float32)
    h2 = np.asarray(L0_ffn_h2_w, dtype=np.float32)
    w3 = np.asarray(L0_ffn_w3_w, dtype=np.float32)
    s1 = np.asarray(L0_ffn_w1_s, dtype=np.float32)
    s2 = np.asarray(L0_ffn_w2_s, dtype=np.float32)
    sh1 = np.asarray(L0_ffn_h1_s, dtype=np.float32)
    sh2 = np.asarray(L0_ffn_h2_s, dtype=np.float32)
    s3 = np.asarray(L0_ffn_w3_s, dtype=np.float32)

    w1t = np.ascontiguousarray(
        w1.T.reshape(8, 128, 32, 128).transpose(2, 1, 0, 3).reshape(32, 128, 1024))
    w2t = np.ascontiguousarray(
        w2.T.reshape(8, 128, 32, 128).transpose(2, 1, 0, 3).reshape(32, 128, 1024))
    h1t = np.ascontiguousarray(
        h1.T.reshape(32, 128, 32, 128).transpose(2, 1, 0, 3).reshape(32, 128, 4096))
    h2t = np.ascontiguousarray(
        h2.T.reshape(32, 128, 32, 128).transpose(2, 1, 0, 3).reshape(32, 128, 4096))
    w3t = np.ascontiguousarray(w3.T.reshape(32, 128, 1024))
    s1c = np.ascontiguousarray(s1.reshape(32, 128).T)
    s2c = np.ascontiguousarray(s2.reshape(32, 128).T)
    sh1c = np.ascontiguousarray(sh1.reshape(32, 128).T)
    sh2c = np.ascontiguousarray(sh2.reshape(32, 128).T)
    s3bc = np.ascontiguousarray(np.broadcast_to(s3, (128, 1024)))

    shared = dict(W_attn=W_attn, W_proj=W_proj, w1t=w1t, w2t=w2t,
                  h1t=h1t, h2t=h2t, w3t=w3t, s1c=s1c, s2c=s2c,
                  sh1c=sh1c, sh2c=sh2c, s3bc=s3bc)

    in_maps = []
    core_spans = []
    p_idx = np.arange(128)[:, None]
    f_idx = np.arange(512)[None, :]
    for c in range(8):
        g = c // 2 + 1
        b = c % 2
        spans = spans_for_class(g)
        core_spans.append((b, spans))
        x_own = np.ascontiguousarray(
            np.concatenate([x[b, sp * SPAN:(sp + 1) * SPAN] for sp in spans], axis=0))
        m = np.zeros((NSLOT, 16, 128, 512), dtype=np.float32)
        for s in range(NSLOT):
            for j in range(16):
                kb = 16 * s + j
                m[s, j] = ((kb * 128 + p_idx) <= (spans[s] * 512 + f_idx))
        in_maps.append(dict(shared, x_own=x_own,
                            x_batch=np.ascontiguousarray(x[b]),
                            masks=m.astype(ml_dtypes.bfloat16)))

    nc = _get_nc()
    res = run_bass_kernel_spmd(nc, in_maps, core_ids=list(range(8)))

    out_full = np.empty((2, T, C), dtype=np.float32)
    for c in range(8):
        b, spans = core_spans[c]
        o = res.results[c]["out"]
        for si, sp in enumerate(spans):
            out_full[b, sp * SPAN:(sp + 1) * SPAN] = o[si * SPAN:(si + 1) * SPAN]
    return out_full


# version 07
# speedup vs baseline: 1.4448x; 1.4448x over previous
import collections
import os
from contextlib import ExitStack

_KPH = int(os.environ.get("KPH", "9"))
_KP4S = int(os.environ.get("KP4S", "4"))
_KP4Q = int(os.environ.get("KP4Q", "4"))
_KP4L = int(os.environ.get("KP4L", "6"))

import numpy as np
import ml_dtypes

import concourse.bass as bass
import concourse.mybir as mybir
from concourse.bass_utils import run_bass_kernel_spmd

F32 = mybir.dt.float32
BF16 = mybir.dt.bfloat16
AF = mybir.ActivationFunctionType
ALU = mybir.AluOpType
AX = mybir.AxisListType.X

B, T, C = 2, 8192, 1024
DH = 4096
SPAN = 512
NSLOT = 4
OWN = NSLOT * SPAN
ESCH = [16, 32, 48, 64]


def spans_for_class(g):
    return [g - 1, 8 - g, 7 + g, 16 - g]


class Sched:
    ENGS = ("pe", "sc", "ve", "sy", "gp")

    def __init__(self):
        self.recs = []
        self.cnt = collections.defaultdict(int)
        self.marks = {}

    def mark(self, ph):
        self.marks[ph] = len(self.recs)

    def emit(self, eng, fn, waits=None, inc=None):
        assert eng in self.ENGS
        self.recs.append((eng, dict(waits or {}), fn, inc))
        if inc is not None:
            self.cnt[inc[0]] += inc[1]
            return self.cnt[inc[0]]
        return None

    def c(self, sem):
        return self.cnt[sem]


def build_nc():
    nc = bass.Bass("TRN2", target_bir_lowering=False, debug=False,
                   detect_race_conditions=False)

    dt_in = lambda n, shp, dt=F32: nc.dram_tensor(n, shp, dt, kind="ExternalInput")
    x_own_t = dt_in("x_own", [OWN, C])
    x_batch_t = dt_in("x_batch", [T, C])
    W_attn_t = dt_in("W_attn", [C, 3 * C])
    W_proj_t = dt_in("W_proj", [C, C])
    w1t_t = dt_in("w1t", [32, 128, 1024])
    w2t_t = dt_in("w2t", [32, 128, 1024])
    h1t_t = dt_in("h1t", [32, 128, 4096])
    h2t_t = dt_in("h2t", [32, 128, 4096])
    w3t_t = dt_in("w3t", [32, 128, 1024])
    s1c_t = dt_in("s1c", [128, 32])
    s2c_t = dt_in("s2c", [128, 32])
    sh1c_t = dt_in("sh1c", [128, 32])
    sh2c_t = dt_in("sh2c", [128, 32])
    s3bc_t = dt_in("s3bc", [128, 1024])
    masks_t = dt_in("masks", [NSLOT, 16, 128, 512], BF16)
    out_t = nc.dram_tensor("out", [OWN, C], F32, kind="ExternalOutput")

    _dbg = os.environ.get("KDBG") == "1"
    dt_int = lambda n, shp, dt: nc.dram_tensor(
        n, shp, dt, kind=("ExternalOutput" if _dbg else "Internal"))
    xno_d = dt_int("xno_d", [OWN, C], BF16)
    xn_d = dt_int("xn_d", [T, C], BF16)
    kT_d = dt_int("kT_d", [C, T], BF16)
    v_d = dt_int("v_d", [T, C], BF16)
    y_d = dt_int("y_d", [C, OWN], BF16)
    h2o_d = dt_int("h2o_d", [DH, OWN], BF16)
    lnst_d = dt_int("lnst_d", [80, 128, 8], F32)
    w1b_d = dt_int("w1b_d", [32, 128, 1024], BF16)
    w2b_d = dt_int("w2b_d", [32, 128, 1024], BF16)
    h1b_d = dt_int("h1b_d", [32, 128, 4096], BF16)
    h2b_d = dt_int("h2b_d", [32, 128, 4096], BF16)

    x_own = x_own_t.ap()
    x_batch = x_batch_t.ap()
    W_attn = W_attn_t.ap()
    W_proj = W_proj_t.ap()
    out = out_t.ap()

    S = Sched()

    with ExitStack() as es:
        e = es.enter_context
        BQ = e(nc.sbuf_tensor("BQ", [128, 16384], BF16))
        BX = e(nc.sbuf_tensor("BX", [128, 16384], BF16))
        BH = e(nc.sbuf_tensor("BH", [128, 32768], BF16))
        BF = e(nc.sbuf_tensor("BF", [128, 7680], F32))
        BXW = e(nc.sbuf_tensor("BXW", [128, 4096], BF16))
        P6Y = e(nc.sbuf_tensor("P6Y", [128, 2048], BF16))
        SCL = e(nc.sbuf_tensor("SCL", [128, 152], F32))
        pb = [e(nc.psum_tensor(f"pb{i}", [128, 512], F32)) for i in range(8)]

        sem_names = ["X0", "X1", "XH0", "XH1", "XOT", "WAT", "PRO",
                     "MK0", "MK1", "KV0", "KV1", "WS0", "WS1",
                     "W30", "W31", "H20", "H21", "YG0", "YG1", "XT0", "XT1",
                     "CAST", "PE", "SC", "VE",
                     "GS0", "GS1", "KS0", "KS1", "VS0", "VS1",
                     "YS0", "YS1", "HS0", "HS1", "OS0", "OS1",
                     "ZPE", "ZSC", "ZVE"]
        sem = {n: e(nc.semaphore(n)) for n in sem_names}

        COL_S1, COL_S2, COL_SH1, COL_SH2 = 0, 32, 64, 96
        COL_B005, COL_EPS, COL_VS, COL_QS = 128, 129, 130, 131
        COL_SUM, COL_SSQ, COL_MEAN, COL_MSQ = 132, 134, 136, 138
        COL_VAR, COL_TMP, COL_RSTD = 140, 142, 144

        def scl(col):
            return SCL[:, col:col + 1]

        for col, t in ((COL_S1, s1c_t), (COL_S2, s2c_t), (COL_SH1, sh1c_t),
                       (COL_SH2, sh2c_t)):
            S.emit("sy", lambda g, c0=col, tt=t: g.dma_start(
                out=SCL[:, c0:c0 + 32], in_=tt.ap()), inc=("PRO", 16))
        pro_scales = S.c("PRO")
        S.emit("ve", lambda g: g.memset(scl(COL_B005), 0.005), inc=("VE", 1))
        S.emit("ve", lambda g: g.memset(SCL[:, 146:147], 0.25), inc=("VE", 1))
        ve_warm = S.c("VE")
        for fn_ in (AF.Sqrt, AF.Silu, AF.Gelu, AF.Sqrt):
            for rep in range(4):
                S.emit("sc", lambda g, f=fn_: g.activation(
                    SCL[:, 147:148], SCL[:, 146:147], f),
                    waits={"VE": ve_warm}, inc=("SC", 1))
        S.emit("ve", lambda g: g.memset(scl(COL_EPS), 1e-5), inc=("VE", 1))
        S.emit("ve", lambda g: g.memset(scl(COL_VS), 1e-3), inc=("VE", 1))
        S.emit("ve", lambda g: g.memset(scl(COL_QS), 0.125), inc=("VE", 1))

        CH = 16384
        st_post = 0
        for src_t, dst_t, nel in ((w1t_t, w1b_d, 32 * 128 * 1024),
                                  (w2t_t, w2b_d, 32 * 128 * 1024),
                                  (h1t_t, h1b_d, 32 * 128 * 4096),
                                  (h2t_t, h2b_d, 32 * 128 * 4096)):
            per = nel // 128
            for c0 in range(0, per, CH):
                i_ap = bass.AP(src_t, c0, [[per, 128], [1, CH]])
                o_ap = bass.AP(dst_t, c0, [[per, 128], [1, CH]])
                ld_post = S.emit("gp", lambda g, i=i_ap: g.dma_start(
                    out=BH[:, 0:CH], in_=i), waits={"CAST": st_post},
                    inc=("CAST", 16))
                st_post = S.emit("gp", lambda g, o=o_ap: g.dma_start(
                    out=o, in_=BH[:, 0:CH]), waits={"CAST": ld_post},
                    inc=("CAST", 16))
        cast_total = S.c("CAST")
        for ic in range(8):
            S.emit("gp", lambda g, ic=ic: g.dma_start(
                out=BQ[:, ic * 2048:(ic + 1) * 2048],
                in_=W_attn[ic * 128:(ic + 1) * 128, 1024:3072]), inc=("WAT", 16))
        wat_kv = S.c("WAT")

        ln_ve, ln_sc, ln_store, ln_ssq = {}, {}, {}, {}

        def ln_tile(ti, i, src_ap, dst_d):
            par = ti % 2
            xs = BF[:, par * 1024:(par + 1) * 1024]
            xnb = BH[:, par * 1024:(par + 1) * 1024]
            sq = BF[:, 2048:3072]
            xsem = "X0" if par == 0 else "X1"
            waits = {}
            if ti >= 2:
                waits = {"VE": ln_ve[ti - 2], "SC": ln_sc[ti - 2]}
            if ti == 0:
                waits["CAST"] = cast_total
            post = S.emit("sy", lambda g, a=xs, s=src_ap, i=i: g.dma_start(
                out=a, in_=s[i * 128:(i + 1) * 128, :]), waits=waits, inc=(xsem, 16))
            S.emit("ve", lambda g, a=xs, c=par: g.reduce_sum(
                SCL[:, COL_SUM + c:COL_SUM + c + 1], a, axis=AX),
                waits={xsem: post}, inc=("VE", 1))
            sqw = {xsem: post}
            if ti >= 1:
                sqw["VE"] = ln_ssq[ti - 1]
            ln_sc[ti] = S.emit("sc", lambda g, a=xs, c=par: g.activation(
                sq, a, AF.Square),
                waits=sqw, inc=("SC", 1))
            ln_ssq[ti] = S.emit("ve", lambda g, c=par: g.reduce_sum(
                SCL[:, COL_SSQ + c:COL_SSQ + c + 1], sq, axis=AX),
                waits={"SC": ln_sc[ti]}, inc=("VE", 1))
            S.emit("ve", lambda g, c=par: g.tensor_scalar_mul(
                SCL[:, COL_MEAN + c:COL_MEAN + c + 1],
                SCL[:, COL_SUM + c:COL_SUM + c + 1], 1.0 / 1024.0))
            S.emit("ve", lambda g, c=par: g.tensor_mul(
                SCL[:, COL_MSQ + c:COL_MSQ + c + 1],
                SCL[:, COL_MEAN + c:COL_MEAN + c + 1],
                SCL[:, COL_MEAN + c:COL_MEAN + c + 1]))
            ve_var = S.emit("ve", lambda g, c=par: g.scalar_tensor_tensor(
                SCL[:, COL_VAR + c:COL_VAR + c + 1],
                SCL[:, COL_SSQ + c:COL_SSQ + c + 1], 1.0 / 1024.0,
                SCL[:, COL_MSQ + c:COL_MSQ + c + 1], ALU.mult, ALU.subtract),
                inc=("VE", 1))
            sc_sqrt = S.emit("sc", lambda g, c=par: g.activation(
                SCL[:, COL_TMP + c:COL_TMP + c + 1],
                SCL[:, COL_VAR + c:COL_VAR + c + 1], AF.Sqrt,
                bias=scl(COL_EPS)), waits={"VE": ve_var}, inc=("SC", 1))
            S.emit("ve", lambda g, c=par: g.reciprocal(
                SCL[:, COL_RSTD + c:COL_RSTD + c + 1],
                SCL[:, COL_TMP + c:COL_TMP + c + 1]), waits={"SC": sc_sqrt})
            st_stage = SCL[:, 148 - par * 0:148]
            ve_dump = S.emit("ve", lambda g, c=par: g.tensor_copy(
                BF[:, 3072 + 8 * (1 + c):3072 + 8 * (1 + c) + 7],
                bass.AP(SCL, 132 + c, [[152, 128], [2, 7]])), inc=("VE", 1))
            S.emit("gp", lambda g, ti=ti, c=par: g.dma_start(
                out=bass.AP(lnst_d, ti * 128 * 8, [[8, 128], [1, 7]]),
                in_=BF[:, 3072 + 8 * (1 + c):3072 + 8 * (1 + c) + 7]),
                waits={"VE": ve_dump}, inc=("CAST", 16))
            gsem = "GS0" if par == 0 else "GS1"
            w2 = {gsem: ln_store[ti - 2]} if ti >= 2 else {}
            ln_ve[ti] = S.emit("ve", lambda g, a=xs, o=xnb, c=par: g.tensor_scalar(
                o, a, SCL[:, COL_MEAN + c:COL_MEAN + c + 1],
                SCL[:, COL_RSTD + c:COL_RSTD + c + 1], ALU.subtract, ALU.mult),
                waits=w2, inc=("VE", 1))
            ln_store[ti] = S.emit("gp", lambda g, o=xnb, d=dst_d, i=i: g.dma_start(
                out=d.ap()[i * 128:(i + 1) * 128, :], in_=o),
                waits={"VE": ln_ve[ti]}, inc=(gsem, 16))

        for i in range(16):
            ln_tile(i, i, x_own, xno_d)
        ph1_gs = {"GS0": S.c("GS0"), "GS1": S.c("GS1")}
        for i in range(64):
            ln_tile(16 + i, i, x_batch, xn_d)
        ph2_gs = {"GS0": S.c("GS0"), "GS1": S.c("GS1")}
        S.mark(1)

        prev_use = {}
        kst_prev = {}
        vst_prev = {}
        hg_last_pe = {}
        for hg in range(8):
            par = hg % 2
            xsem = "XH0" if par == 0 else "XH1"
            w = dict(ph2_gs) if hg == 0 else {}
            if hg >= 2:
                w["PE"] = hg_last_pe[hg - 2]
            xh_post = None
            for ic in range(8):
                xh_post = S.emit("sy", lambda g, hg=hg, ic=ic, p=par: g.dma_start_transpose(
                    BH[:, p * 8192 + ic * 1024: p * 8192 + (ic + 1) * 1024],
                    xn_d.ap()[hg * 1024:(hg + 1) * 1024, ic * 128:(ic + 1) * 128]),
                    waits=(w if ic == 0 else {}), inc=(xsem, 16))
            for oc in range(8):
                b0, b1 = pb[(oc % 2) * 2], pb[(oc % 2) * 2 + 1]
                kk = ("k", oc % 2)
                mmw = {xsem: xh_post, "WAT": wat_kv}
                if kk in prev_use:
                    mmw["SC"] = prev_use[kk]
                pe_post = None
                for ic in range(8):
                    lhs = BQ[:, ic * 2048 + oc * 128: ic * 2048 + (oc + 1) * 128]
                    rhs0 = BH[:, par * 8192 + ic * 1024: par * 8192 + ic * 1024 + 512]
                    rhs1 = BH[:, par * 8192 + ic * 1024 + 512: par * 8192 + (ic + 1) * 1024]
                    S.emit("pe", lambda g, o=b0, l=lhs, r=rhs0, st=(ic == 0), sp=(ic == 7):
                           g.matmul(o[:], l, r, start=st, stop=sp),
                           waits=(mmw if ic == 0 else {}))
                    pe_post = S.emit("pe", lambda g, o=b1, l=lhs, r=rhs1, st=(ic == 0), sp=(ic == 7):
                                     g.matmul(o[:], l, r, start=st, stop=sp),
                                     inc=("PE", 1) if ic == 7 else None)
                kst = BH[:, 16384 + (oc % 2) * 1024: 16384 + (oc % 2) * 1024 + 1024]
                ksem = "KS0" if oc % 2 == 0 else "KS1"
                evw = {"PE": pe_post}
                if (oc % 2) in kst_prev:
                    evw[ksem] = kst_prev[oc % 2]
                S.emit("sc", lambda g, o=kst, s=b0: g.copy(o[:, 0:512], s[:]), waits=evw)
                prev_use[kk] = S.emit("sc", lambda g, o=kst, s=b1: g.copy(
                    o[:, 512:1024], s[:]), inc=("SC", 1))
                kst_prev[oc % 2] = S.emit("sc", lambda g, o=kst, hg=hg, oc=oc: g.dma_start(
                    out=kT_d.ap()[oc * 128:(oc + 1) * 128, hg * 1024:(hg + 1) * 1024],
                    in_=o), waits={"SC": prev_use[kk]}, inc=(ksem, 16))
            for tc in range(8):
                b0, b1 = pb[4 + (tc % 2) * 2], pb[4 + (tc % 2) * 2 + 1]
                kk = ("v", tc % 2)
                mmw = {}
                if kk in prev_use:
                    mmw["SC"] = prev_use[kk]
                pe_post = None
                for ic in range(8):
                    lhs = BH[:, par * 8192 + ic * 1024 + tc * 128: par * 8192 + ic * 1024 + (tc + 1) * 128]
                    rhs0 = BQ[:, ic * 2048 + 1024: ic * 2048 + 1536]
                    rhs1 = BQ[:, ic * 2048 + 1536: ic * 2048 + 2048]
                    S.emit("pe", lambda g, o=b0, l=lhs, r=rhs0, st=(ic == 0), sp=(ic == 7):
                           g.matmul(o[:], l, r, start=st, stop=sp),
                           waits=(mmw if ic == 0 else {}))
                    pe_post = S.emit("pe", lambda g, o=b1, l=lhs, r=rhs1, st=(ic == 0), sp=(ic == 7):
                                     g.matmul(o[:], l, r, start=st, stop=sp),
                                     inc=("PE", 1) if ic == 7 else None)
                vst = BH[:, 18432 + (tc % 2) * 1024: 18432 + (tc % 2) * 1024 + 1024]
                vsem = "VS0" if tc % 2 == 0 else "VS1"
                evw = {"PE": pe_post}
                if (tc % 2) in vst_prev:
                    evw[vsem] = vst_prev[tc % 2]
                S.emit("sc", lambda g, o=vst, s=b0: g.activation(
                    o[:, 0:512], s[:], AF.Copy, scale=scl(COL_VS)), waits=evw)
                prev_use[kk] = S.emit("sc", lambda g, o=vst, s=b1: g.activation(
                    o[:, 512:1024], s[:], AF.Copy, scale=scl(COL_VS)), inc=("SC", 1))
                vst_prev[tc % 2] = S.emit("sc", lambda g, o=vst, hg=hg, tc=tc: g.dma_start(
                    out=v_d.ap()[hg * 1024 + tc * 128: hg * 1024 + (tc + 1) * 128, :],
                    in_=o), waits={"SC": prev_use[kk]}, inc=(vsem, 16))
            hg_last_pe[hg] = S.c("PE")
        ph3_pe_total = S.c("PE")
        ph3_sc_total = S.c("SC")
        ph3_kv_st = {s: S.c(s) for s in ("KS0", "KS1", "VS0", "VS1")}
        S.mark(2)

        for ic in range(8):
            S.emit("gp", lambda g, ic=ic: g.dma_start(
                out=BH[:, 16384 + ic * 1024: 16384 + (ic + 1) * 1024],
                in_=W_attn[ic * 128:(ic + 1) * 128, 0:1024]),
                waits=(dict(ph3_kv_st) if ic == 0 else {}), inc=("WAT", 16))
        wat_q = S.c("WAT")
        for ic in range(8):
            S.emit("sy", lambda g, ic=ic: g.dma_start_transpose(
                BX[:, ic * 2048:(ic + 1) * 2048],
                xno_d.ap()[:, ic * 128:(ic + 1) * 128]),
                waits=(dict(ph1_gs) if ic == 0 else {}), inc=("XOT", 16))
        xot_total = S.c("XOT")
        qevac = {}
        for oc in range(8):
            banks = [pb[0], pb[1], pb[2], pb[3]] if oc % 2 == 0 else [pb[4], pb[5], pb[6], pb[7]]
            mmw = {"XOT": xot_total, "WAT": wat_q,
                   "SC": qevac.get(oc - 2, ph3_sc_total)}
            pe_post = None
            for ic in range(8):
                lhs = BH[:, 16384 + ic * 1024 + oc * 128: 16384 + ic * 1024 + (oc + 1) * 128]
                for j in range(4):
                    rhs = BX[:, ic * 2048 + j * 512: ic * 2048 + (j + 1) * 512]
                    pe_post = S.emit("pe", lambda g, o=banks[j], l=lhs, r=rhs,
                                     st=(ic == 0), sp=(ic == 7): g.matmul(
                                         o[:], l, r, start=st, stop=sp),
                                     waits=(mmw if (ic == 0 and j == 0) else {}),
                                     inc=("PE", 1) if (ic == 7 and j == 3) else None)
            for j in range(4):
                qevac[oc] = S.emit("sc", lambda g, oc=oc, j=j, s=banks[j]: g.activation(
                    BQ[:, oc * 2048 + j * 512: oc * 2048 + (j + 1) * 512],
                    s[:], AF.Copy, scale=scl(COL_QS)),
                    waits={"PE": pe_post}, inc=("SC", 1))
        ph35_sc_total = S.c("SC")
        S.mark(3)

        pm = {}
        kv_prev = {}
        mask_prev = {}
        yev = {0: ph35_sc_total, 1: ph35_sc_total}
        yst_prev = {}
        first_kv = True
        pe_av = None
        for s in range(_KP4S):
            mpar = s % 2
            msem = "MK0" if mpar == 0 else "MK1"
            mw = {}
            if s == 0:
                mw["PE"] = ph3_pe_total
            if mpar in mask_prev:
                mw["VE"] = mask_prev[mpar]
            for j in range(16):
                S.emit("sy", lambda g, s=s, j=j, mp=mpar: g.dma_start(
                    out=BH[:, mp * 8192 + j * 512: mp * 8192 + (j + 1) * 512],
                    in_=bass.AP(masks_t, (s * 16 + j) * 128 * 512,
                                [[512, 128], [1, 512]])),
                    waits=(mw if j == 0 else {}), inc=(msem, 16))
            mask_total = S.c(msem)
            nkb = ESCH[s]
            for q in range(_KP4Q):
                kv_post = {}
                for kb in range(nkb):
                    sup, jj = kb // 4, kb % 4
                    spar = sup % 2
                    ksem = "KV0" if spar == 0 else "KV1"
                    if jj == 0:
                        lw = {}
                        if first_kv:
                            lw.update(ph3_kv_st)
                            first_kv = False
                        if spar in kv_prev:
                            lw["PE"] = kv_prev[spar]
                        for hpc in range(2):
                            S.emit("sy", lambda g, q=q, hpc=hpc, sup=sup, p=spar: g.dma_start(
                                out=BXW[:, p * 1024 + hpc * 512: p * 1024 + (hpc + 1) * 512],
                                in_=kT_d.ap()[(q * 2 + hpc) * 128:(q * 2 + hpc + 1) * 128,
                                              sup * 512:(sup + 1) * 512]),
                                waits=(lw if hpc == 0 else {}), inc=(ksem, 16))
                        for vj in range(4):
                            kv_post[sup] = S.emit("sy", lambda g, q=q, vj=vj, sup=sup, p=spar: g.dma_start(
                                out=BXW[:, 2048 + p * 1024 + vj * 256: 2048 + p * 1024 + (vj + 1) * 256],
                                in_=v_d.ap()[sup * 512 + vj * 128: sup * 512 + (vj + 1) * 128,
                                             q * 256:(q + 1) * 256]), inc=(ksem, 16))
                    for hp in range(2):
                        pp = (kb * 2 + hp) % 2
                        bA, bB = pb[pp * 2], pb[pp * 2 + 1]
                        qoff = (q * 2 + hp) * 2048 + s * 512
                        koff = spar * 1024 + hp * 512 + jj * 128
                        if _KP4L < 1:
                            continue
                        qkw = {ksem: kv_post[sup]}
                        if pp in pm:
                            qkw["SC"] = pm[pp].get("sc", 0)
                            qkw["VE"] = pm[pp].get("ve3", 0)
                        else:
                            qkw["SC"] = ph35_sc_total
                        qkw = {k: v for k, v in qkw.items() if v}
                        S.emit("pe", lambda g, o=bA, ko=koff, qo=qoff: g.matmul(
                            o[:], BXW[0:64, ko:ko + 128], BQ[0:64, qo:qo + 512],
                            start=True, stop=True, tile_position=(0, 0)), waits=qkw)
                        pe_qk = S.emit("pe", lambda g, o=bB, ko=koff, qo=qoff: g.matmul(
                            o[:], BXW[64:128, ko:ko + 128], BQ[64:128, qo:qo + 512],
                            start=True, stop=True, tile_position=(64, 0)), inc=("PE", 1))
                        RA, RB = BF[:, 0:512], BF[:, 512:1024]
                        qA = BF[:, 1024 + pp * 1024: 1536 + pp * 1024]
                        qB = BF[:, 1536 + pp * 1024: 2048 + pp * 1024]
                        tA = BF[:, 3072 + pp * 1024: 3584 + pp * 1024]
                        tB = BF[:, 3584 + pp * 1024: 4096 + pp * 1024]
                        wA = BF[:, 5120 + pp * 1024: 5632 + pp * 1024]
                        wB = BF[:, 5632 + pp * 1024: 6144 + pp * 1024]
                        AA = BH[:, 16384 + (pp * 2) * 512: 16384 + (pp * 2 + 1) * 512]
                        AB = BH[:, 16384 + (pp * 2 + 1) * 512: 16384 + (pp * 2 + 2) * 512]
                        if _KP4L < 2:
                            pm[pp] = {"pe_qk": pe_qk}
                            kv_prev[spar] = pe_qk
                            pe_av = pe_qk
                            continue
                        scw = {"PE": pe_qk}
                        if pp in pm:
                            scw["VE"] = pm[pp].get("ve5", 0)
                        scw = {k: v for k, v in scw.items() if v}
                        S.emit("sc", lambda g, o=RA, i=bA: g.activation(o, i[:], AF.Relu),
                               waits=scw)
                        S.emit("sc", lambda g, o=RB, i=bB: g.activation(o, i[:], AF.Relu))
                        S.emit("sc", lambda g, o=qA, i=RA: g.activation(
                            o, i, AF.Square, bias=scl(COL_B005)))
                        sc_p = S.emit("sc", lambda g, o=qB, i=RB: g.activation(
                            o, i, AF.Square, bias=scl(COL_B005)), inc=("SC", 1))
                        if _KP4L < 3:
                            pm[pp] = {"sc": sc_p}
                            kv_prev[spar] = pe_qk
                            pe_av = pe_qk
                            continue
                        S.emit("ve", lambda g, o=tA, i=bA: g.tensor_scalar(
                            o, i[:], -0.01, 0.999975, ALU.mult, ALU.add),
                            waits={"PE": pe_qk, "SC": sc_p})
                        ve3 = S.emit("ve", lambda g, o=tB, i=bB: g.tensor_scalar(
                            o, i[:], -0.01, 0.999975, ALU.mult, ALU.add), inc=("VE", 1))
                        if _KP4L < 4:
                            pm[pp] = {"sc": sc_p, "ve3": ve3}
                            kv_prev[spar] = pe_qk
                            pe_av = pe_qk
                            continue
                        vew = {"SC": sc_p}
                        if pp in pm:
                            vew["PE"] = pm[pp].get("pe_av", 0)
                        vew = {k: v for k, v in vew.items() if v}
                        S.emit("ve", lambda g, o=wA, a=qA, b=tA: g.tensor_tensor(
                            o, a, b, ALU.add), waits=vew)
                        S.emit("ve", lambda g, o=wB, a=qB, b=tB: g.tensor_tensor(
                            o, a, b, ALU.add))
                        S.emit("ve", lambda g, o=AA, t=tA, w=wA: g.scalar_tensor_tensor(
                            o, t, 1.999975, w, ALU.is_le, ALU.mult))
                        ve5 = S.emit("ve", lambda g, o=AB, t=tB, w=wB: g.scalar_tensor_tensor(
                            o, t, 1.999975, w, ALU.is_le, ALU.mult), inc=("VE", 1))
                        if kb >= 16 * s and _KP4L >= 5:
                            moff = mpar * 8192 + (kb - 16 * s) * 512
                            S.emit("ve", lambda g, o=AA, mo=moff: g.tensor_tensor(
                                o, o, BH[:, mo:mo + 512], ALU.mult),
                                waits={msem: mask_total})
                            ve5 = S.emit("ve", lambda g, o=AB, mo=moff: g.tensor_tensor(
                                o, o, BH[:, mo:mo + 512], ALU.mult), inc=("VE", 1))
                            mask_prev[mpar] = ve5
                        if _KP4L < 6:
                            pm[pp] = {"sc": sc_p, "ve3": ve3, "ve5": ve5}
                            kv_prev[spar] = pe_qk
                            pe_av = pe_qk
                            continue
                        voff = 2048 + spar * 1024 + jj * 256 + hp * 128
                        yb = pb[4 + hp]
                        avw = {"VE": ve5}
                        if kb == 0:
                            avw["SC"] = yev[hp]
                        S.emit("pe", lambda g, o=yb, vo=voff, a=AA, st=(kb == 0),
                               sp=(kb == nkb - 1): g.matmul(
                                   o[0:64, :], BXW[:, vo:vo + 64], a,
                                   start=st, stop=sp, tile_position=(0, 0)), waits=avw)
                        pe_av = S.emit("pe", lambda g, o=yb, vo=voff, a=AB,
                                       st=(kb == 0), sp=(kb == nkb - 1): g.matmul(
                                           o[64:128, :], BXW[:, vo + 64:vo + 128], a,
                                           start=st, stop=sp, tile_position=(0, 64),
                                           skip_group_check=True),
                                       inc=("PE", 1))
                        pm[pp] = {"sc": sc_p, "ve3": ve3, "ve5": ve5, "pe_av": pe_av}
                        kv_prev[spar] = pe_av
                for hp in range(2 if _KP4L >= 6 else 0):
                    yst = BH[:, 18432 + hp * 512: 18432 + (hp + 1) * 512]
                    ysem2 = "YS0" if hp == 0 else "YS1"
                    evw = {"PE": pe_av}
                    if hp in yst_prev:
                        evw[ysem2] = yst_prev[hp]
                    yev[hp] = S.emit("sc", lambda g, o=yst, s=pb[4 + hp]: g.copy(
                        o, s[:]), waits=evw, inc=("SC", 1))
                    yst_prev[hp] = S.emit("sc", lambda g, o=yst, q=q, hp=hp, s=s: g.dma_start(
                        out=y_d.ap()[(q * 2 + hp) * 128:(q * 2 + hp + 1) * 128,
                                     s * 512:(s + 1) * 512], in_=o),
                        waits={"SC": yev[hp]}, inc=(ysem2, 16))
        ph4_y_st = {"YS0": S.c("YS0"), "YS1": S.c("YS1")}
        ph4_pe_total = S.c("PE")
        ph4_sc_total = S.c("SC")
        S.mark(4)

        w_item = 0
        ws_prev = {}
        ffn_gate = {}
        h2_last_pe = ph4_pe_total
        h2o_sst = {}
        for sp in range(NSLOT):
            for rep in range(4):
                S.emit("sc", lambda g: g.activation(
                    SCL[:, 147:148], SCL[:, 146:147], AF.Silu), inc=("SC", 1))
            for oc in range(32):
                par = w_item % 2
                wsem = "WS0" if par == 0 else "WS1"
                lw = {}
                if sp == 0 and oc < 2:
                    lw["CAST"] = cast_total
                    lw["PE"] = ph4_pe_total
                if par in ws_prev:
                    lw["PE"] = max(lw.get("PE", 0), ws_prev[par])
                S.emit("sy", lambda g, oc=oc, p=par: g.dma_start(
                    out=BXW[:, p * 2048: p * 2048 + 1024],
                    in_=bass.AP(w1b_d, oc * 128 * 1024, [[1024, 128], [1, 1024]])),
                    waits=lw, inc=(wsem, 16))
                ws_post = S.emit("sy", lambda g, oc=oc, p=par: g.dma_start(
                    out=BXW[:, p * 2048 + 1024: p * 2048 + 2048],
                    in_=bass.AP(w2b_d, oc * 128 * 1024, [[1024, 128], [1, 1024]])),
                    inc=(wsem, 16))
                b1, b2 = pb[oc % 2], pb[2 + oc % 2]
                mmw = {wsem: ws_post, "PE": h2_last_pe}
                gk = ("w12", oc % 2)
                if gk in ffn_gate:
                    mmw["SC"] = ffn_gate[gk]["sc"]
                    mmw["VE"] = ffn_gate[gk]["ve"]
                elif sp == 0:
                    mmw["SC"] = ph4_sc_total
                for ic in range(8):
                    rhs = BX[:, ic * 2048 + sp * 512: ic * 2048 + (sp + 1) * 512]
                    S.emit("pe", lambda g, o=b1, p=par, ic=ic, r=rhs, st=(ic == 0),
                           sp2=(ic == 7): g.matmul(
                               o[:], BXW[:, p * 2048 + ic * 128: p * 2048 + (ic + 1) * 128],
                               r, start=st, stop=sp2),
                           waits=(mmw if ic == 0 else {}),
                           inc=("PE", 1) if ic == 7 else None)
                pe_w1 = S.c("PE")
                for ic in range(8):
                    rhs = BX[:, ic * 2048 + sp * 512: ic * 2048 + (sp + 1) * 512]
                    S.emit("pe", lambda g, o=b2, p=par, ic=ic, r=rhs, st=(ic == 0),
                           sp2=(ic == 7): g.matmul(
                               o[:], BXW[:, p * 2048 + 1024 + ic * 128: p * 2048 + 1024 + (ic + 1) * 128],
                               r, start=st, stop=sp2),
                           inc=("PE", 1) if ic == 7 else None)
                pe_w2 = S.c("PE")
                ws_prev[par] = pe_w2
                silu = BF[:, (oc % 2) * 512:(oc % 2) * 512 + 512]
                sc_post = S.emit("sc", lambda g, o=silu, s=b1, oc=oc: g.activation(
                    o, s[:], AF.Silu, scale=SCL[:, COL_S1 + oc:COL_S1 + oc + 1]),
                    waits={"PE": pe_w1, "PRO": pro_scales}, inc=("SC", 1))
                ve_post = S.emit("ve", lambda g, s=b2, si=silu, oc=oc: g.scalar_tensor_tensor(
                    BH[:, oc * 512:(oc + 1) * 512], s[:],
                    SCL[:, COL_S2 + oc:COL_S2 + oc + 1], si, ALU.mult, ALU.mult),
                    waits={"PE": pe_w2, "SC": sc_post}, inc=("VE", 1))
                ffn_gate[gk] = {"sc": sc_post, "ve": ve_post}
                w_item += 1
            h_ready = S.c("VE")
            for rep in range(4):
                S.emit("sc", lambda g: g.activation(
                    SCL[:, 147:148], SCL[:, 146:147], AF.Gelu), inc=("SC", 1))
            for oc2 in range(32):
                halves = []
                for half in range(2):
                    par = w_item % 2
                    wsem = "WS0" if par == 0 else "WS1"
                    lw = {}
                    if par in ws_prev:
                        lw["PE"] = ws_prev[par]
                    hp_post = S.emit("sy", lambda g, oc2=oc2, half=half, p=par: g.dma_start(
                        out=BXW[:, p * 2048:(p + 1) * 2048],
                        in_=bass.AP(h1b_d, oc2 * 128 * 4096 + half * 2048,
                                    [[4096, 128], [1, 2048]])),
                        waits=lw, inc=(wsem, 16))
                    halves.append((par, wsem, hp_post))
                    w_item += 1
                bk = pb[4 + oc2 % 2]
                gk = ("h1", oc2 % 2)
                for hc in range(32):
                    par, wsem, hp_post = halves[hc // 16]
                    mmw = {}
                    if hc in (0, 16):
                        mmw[wsem] = hp_post
                        if hc == 0:
                            mmw["VE"] = h_ready
                            mmw["SC"] = ffn_gate.get(gk, ph4_sc_total)
                    rhs = BH[:, hc * 512:(hc + 1) * 512]
                    S.emit("pe", lambda g, o=bk, p=par, hc=hc, r=rhs, st=(hc == 0),
                           sp2=(hc == 31): g.matmul(
                               o[:], BXW[:, p * 2048 + (hc % 16) * 128: p * 2048 + ((hc % 16) + 1) * 128],
                               r, start=st, stop=sp2), waits=mmw,
                           inc=("PE", 1) if hc == 31 else None)
                pe_h1 = S.c("PE")
                ws_prev[halves[0][0]] = pe_h1
                ws_prev[halves[1][0]] = pe_h1
                ffn_gate[gk] = S.emit("sc", lambda g, oc2=oc2, s=bk: g.activation(
                    BH[:, 16384 + oc2 * 512: 16384 + (oc2 + 1) * 512], s[:], AF.Gelu,
                    scale=SCL[:, COL_SH1 + oc2:COL_SH1 + oc2 + 1]),
                    waits={"PE": pe_h1}, inc=("SC", 1))
            g_ready = S.c("SC")
            for oc2 in range(32):
                halves = []
                for half in range(2):
                    par = w_item % 2
                    wsem = "WS0" if par == 0 else "WS1"
                    lw = {}
                    if par in ws_prev:
                        lw["PE"] = ws_prev[par]
                    hp_post = S.emit("sy", lambda g, oc2=oc2, half=half, p=par: g.dma_start(
                        out=BXW[:, p * 2048:(p + 1) * 2048],
                        in_=bass.AP(h2b_d, oc2 * 128 * 4096 + half * 2048,
                                    [[4096, 128], [1, 2048]])),
                        waits=lw, inc=(wsem, 16))
                    halves.append((par, wsem, hp_post))
                    w_item += 1
                bk = pb[6 + oc2 % 2]
                gk = ("h2", oc2 % 2)
                for hc in range(32):
                    par, wsem, hp_post = halves[hc // 16]
                    mmw = {}
                    if hc in (0, 16):
                        mmw[wsem] = hp_post
                        if hc == 0:
                            mmw["SC"] = max(g_ready, ffn_gate.get(gk, 0))
                    rhs = BH[:, 16384 + hc * 512: 16384 + (hc + 1) * 512]
                    S.emit("pe", lambda g, o=bk, p=par, hc=hc, r=rhs, st=(hc == 0),
                           sp2=(hc == 31): g.matmul(
                               o[:], BXW[:, p * 2048 + (hc % 16) * 128: p * 2048 + ((hc % 16) + 1) * 128],
                               r, start=st, stop=sp2), waits=mmw,
                           inc=("PE", 1) if hc == 31 else None)
                pe_h2 = S.c("PE")
                ws_prev[halves[0][0]] = pe_h2
                ws_prev[halves[1][0]] = pe_h2
                h2stg = P6Y[:, (oc2 % 2) * 512:(oc2 % 2 + 1) * 512]
                hsem2 = "HS0" if oc2 % 2 == 0 else "HS1"
                evw = {"PE": pe_h2}
                if (oc2 % 2) in h2o_sst:
                    evw[hsem2] = h2o_sst[oc2 % 2]
                ffn_gate[gk] = S.emit("sc", lambda g, o=h2stg, s=bk, oc2=oc2: g.activation(
                    o, s[:], AF.Gelu, scale=SCL[:, COL_SH2 + oc2:COL_SH2 + oc2 + 1]),
                    waits=evw, inc=("SC", 1))
                h2o_sst[oc2 % 2] = S.emit("sc", lambda g, o=h2stg, oc2=oc2, sp=sp: g.dma_start(
                    out=h2o_d.ap()[oc2 * 128:(oc2 + 1) * 128, sp * 512:(sp + 1) * 512],
                    in_=o), waits={"SC": ffn_gate[gk]}, inc=(hsem2, 16))
            h2_last_pe = S.c("PE")
        ph5_h_st = {"HS0": S.c("HS0"), "HS1": S.c("HS1")}
        ph5_pe_total = S.c("PE")
        ph5_sc_total = S.c("SC")
        ph5_ve_total = S.c("VE")
        S.mark(5)

        for cc in range(8):
            S.emit("gp", lambda g, cc=cc: g.dma_start(
                out=BQ[:, cc * 1024:(cc + 1) * 1024],
                in_=W_proj[cc * 128:(cc + 1) * 128, :]),
                waits=({"PE": ph5_pe_total} if cc == 0 else {}), inc=("WAT", 16))
        wat_total = S.c("WAT")
        S.emit("sy", lambda g: g.dma_start(out=BF[:, 2048:3072], in_=s3bc_t.ap()),
               waits={"PE": ph5_pe_total, "VE": ph5_ve_total}, inc=("PRO", 16))
        pro_total = S.c("PRO")
        w3cv = {}
        for hc in range(32):
            par = hc % 2
            wsem = "W30" if par == 0 else "W31"
            lw = {}
            if hc < 2:
                lw["PE"] = ph5_pe_total
                lw["VE"] = ph5_ve_total
            else:
                lw["VE"] = w3cv[hc - 2]
            w3_post = S.emit("sy", lambda g, hc=hc, p=par: g.dma_start(
                out=BF[:, p * 1024:(p + 1) * 1024],
                in_=bass.AP(w3t_t, hc * 128 * 1024, [[1024, 128], [1, 1024]])),
                waits=lw, inc=(wsem, 16))
            cvw = {wsem: w3_post, "PRO": pro_total}
            if hc < 2:
                cvw["PE"] = ph5_pe_total
            w3cv[hc] = S.emit("ve", lambda g, hc=hc, p=par: g.tensor_tensor(
                BH[:, hc * 1024:(hc + 1) * 1024], BF[:, p * 1024:(p + 1) * 1024],
                BF[:, 2048:3072], ALU.mult), waits=cvw, inc=("VE", 1))
        w3_ready = S.c("VE")

        h2l_prev = {}
        yg_prev = {}
        xt_prev = {}
        outstore = {}
        bank_gate = {}
        for grp in range(4):
            h2_posts = []
            for half in range(2):
                hsem = "H20" if half == 0 else "H21"
                lw = {}
                if grp == 0:
                    lw.update(ph5_h_st)
                if (grp - 1, half) in h2l_prev:
                    lw["PE"] = h2l_prev[(grp - 1, half)]
                hpost = None
                for part in range(2):
                    hcb = half * 16 + part * 8
                    i_ap = bass.AP(h2o_d, hcb * 128 * OWN + grp * 512,
                                   [[OWN, 128], [128 * OWN, 8], [1, 512]])
                    hpost = S.emit("sy", lambda g, i=i_ap, half=half, part=part: g.dma_start(
                        out=BX[:, half * 8192 + part * 4096: half * 8192 + (part + 1) * 4096],
                        in_=i), waits=(lw if part == 0 else {}), inc=(hsem, 16))
                h2_posts.append((hsem, hpost))
            ygpar = grp % 2
            ysem = "YG0" if ygpar == 0 else "YG1"
            lw = {}
            if grp == 0:
                lw.update(ph4_y_st)
            if (grp - 2, ygpar) in yg_prev:
                lw["PE"] = yg_prev[(grp - 2, ygpar)]
            y_post = None
            for part in range(2):
                i_ap = bass.AP(y_d, (part * 4) * 128 * OWN + grp * 512,
                               [[OWN, 128], [128 * OWN, 4], [1, 512]])
                y_post = S.emit("sy", lambda g, i=i_ap, p=ygpar, part=part: g.dma_start(
                    out=BQ[:, 8192 + p * 4096 + part * 2048: 8192 + p * 4096 + (part + 1) * 2048],
                    in_=i), waits=(lw if part == 0 else {}), inc=(ysem, 16))
            for half in range(2):
                hsem, hpost = h2_posts[half]
                for tc4 in range(4):
                    for oh in range(2):
                        bk = pb[tc4 * 2 + oh]
                        for hcl in range(16):
                            hc = half * 16 + hcl
                            mmw = {}
                            if hcl == 0:
                                mmw[hsem] = hpost
                                if half == 0:
                                    mmw["VE"] = max(w3_ready,
                                                    bank_gate.get((grp - 1, tc4, oh), 0))
                                    if grp == 0:
                                        mmw["SC"] = ph5_sc_total
                            lhs_off = half * 8192 + (hcl // 8) * 4096 + (hcl % 8) * 512 + tc4 * 128
                            rhs = BH[:, hc * 1024 + oh * 512: hc * 1024 + (oh + 1) * 512]
                            S.emit("pe", lambda g, o=bk, lo=lhs_off, r=rhs,
                                   st=(hc == 0): g.matmul(
                                       o[:], BX[:, lo:lo + 128], r,
                                       start=st, stop=False), waits=mmw)
            for tc4 in range(4):
                tc = grp * 4 + tc4
                xpar = tc % 2
                xsem = "XT0" if xpar == 0 else "XT1"
                lw = {}
                if tc < 2:
                    lw["PE"] = ph5_pe_total
                if (tc - 2) in xt_prev:
                    lw["VE"] = xt_prev[tc - 2]
                xpost = S.emit("sy", lambda g, tc=tc, p=xpar: g.dma_start(
                    out=BF[:, 3072 + p * 1024: 3072 + (p + 1) * 1024],
                    in_=x_own[tc * 128:(tc + 1) * 128, :]), waits=lw, inc=(xsem, 16))
                for oh in range(2):
                    bk = pb[tc4 * 2 + oh]
                    pe_fin = None
                    for cc in range(8):
                        mmw = {}
                        if cc == 0:
                            mmw = {ysem: y_post, "WAT": wat_total}
                        lhs_off = 8192 + ygpar * 4096 + (cc // 4) * 2048 + (cc % 4) * 512 + tc4 * 128
                        rhs = BQ[:, cc * 1024 + oh * 512: cc * 1024 + (oh + 1) * 512]
                        pe_fin = S.emit("pe", lambda g, o=bk, lo=lhs_off, r=rhs,
                                        sp2=(cc == 7): g.matmul(
                                            o[:], BQ[:, lo:lo + 128], r,
                                            start=False, stop=sp2), waits=mmw,
                                        inc=("PE", 1) if cc == 7 else None)
                    yg_prev[(grp, ygpar)] = pe_fin
                    h2l_prev[(grp, 0)] = pe_fin
                    h2l_prev[(grp, 1)] = pe_fin
                    obuf = BF[:, 5120 + (tc % 2) * 1024 + oh * 512: 5120 + (tc % 2) * 1024 + (oh + 1) * 512]
                    xsl = BF[:, 3072 + (tc % 2) * 1024 + oh * 512: 3072 + (tc % 2) * 1024 + (oh + 1) * 512]
                    osem = "OS0" if tc % 2 == 0 else "OS1"
                    evw = {"PE": pe_fin, xsem: xpost}
                    if (tc - 2) in outstore:
                        evw[osem] = outstore[tc - 2]
                    ov = S.emit("ve", lambda g, o=obuf, s=bk, x=xsl: g.tensor_add(
                        o, s[:], x), waits=evw, inc=("VE", 1))
                    bank_gate[(grp, tc4, oh)] = ov
                    xt_prev[tc] = ov
                tc = grp * 4 + tc4
                outstore[tc] = S.emit("gp", lambda g, tc=tc: g.dma_start(
                    out=out[tc * 128:(tc + 1) * 128, :],
                    in_=BF[:, 5120 + (tc % 2) * 1024: 5120 + (tc % 2) * 1024 + 1024]),
                    waits={"VE": xt_prev[tc]},
                    inc=("OS0" if tc % 2 == 0 else "OS1", 16))
        S.mark(6)
        if _KPH < 6 and _KPH in S.marks:
            S.recs = S.recs[:S.marks[_KPH]]
        tot = collections.defaultdict(int)
        for (_en, _w, _fn, _inc) in S.recs:
            if _inc:
                tot[_inc[0]] += _inc[1]
        fin = {k: tot[k] for k in ("GS0", "GS1", "KS0", "KS1", "VS0", "VS1",
                                   "YS0", "YS1", "HS0", "HS1", "OS0", "OS1")
               if tot[k] > 0}
        S.recs.append(("gp", fin, None, None))

        for k, v in S.cnt.items():
            assert v < 60000, (k, v)

        ksafe_engs = os.environ.get("KSAFE", "sc,ve").replace(",", " ").split()
        selfsem_all = {"pe": "ZPE", "sc": "ZSC", "ve": "ZVE"}
        selfsem = {k: v for k, v in selfsem_all.items() if k in ksafe_engs}

        with nc.Block() as block:
            def replay(eng_name):
                last = {}
                zs = selfsem.get(eng_name)

                def run(g):
                    cnt = collections.defaultdict(int)
                    zn = 0
                    prev = None
                    for (en, waits, fn, inc) in S.recs:
                        if en != eng_name:
                            if inc is not None:
                                cnt[inc[0]] += inc[1]
                            continue
                        for sname, val in waits.items():
                            if last.get(sname, -1) < val:
                                g.wait_ge(sem[sname], val)
                                last[sname] = val
                        if fn is None:
                            continue
                        if zs is not None and prev is not None:
                            if last.get(prev[0], -1) < prev[1]:
                                g.wait_ge(sem[prev[0]], prev[1])
                                last[prev[0]] = prev[1]
                        inst = fn(g)
                        is_dma = getattr(inst, "is_dma", None)
                        if inc is not None:
                            cnt[inc[0]] += inc[1]
                            inst.then_inc(sem[inc[0]], inc[1])
                            if inc[1] == 1:
                                prev = (inc[0], cnt[inc[0]])
                            else:
                                prev = None
                        elif zs is not None:
                            zn += 1
                            inst.then_inc(sem[zs], 1)
                            prev = (zs, zn)
                        else:
                            prev = None
                return run

            block.tensor(replay("pe"))
            block.scalar(replay("sc"))
            block.vector(replay("ve"))
            block.sync(replay("sy"))
            block.gpsimd(replay("gp"))

    return nc


_NC_CACHE = None


def _get_nc():
    global _NC_CACHE
    if _NC_CACHE is None:
        _NC_CACHE = build_nc()
    return _NC_CACHE


def kernel(L0_attn_x, L0_attn_W_attn, L0_attn_W_proj,
           L0_ffn_w1_w, L0_ffn_w1_s, L0_ffn_w2_w, L0_ffn_w2_s,
           L0_ffn_h1_w, L0_ffn_h1_s, L0_ffn_h2_w, L0_ffn_h2_s,
           L0_ffn_w3_w, L0_ffn_w3_s):
    x = np.ascontiguousarray(np.asarray(L0_attn_x, dtype=np.float32))
    W_attn = np.ascontiguousarray(np.asarray(L0_attn_W_attn, dtype=np.float32))
    W_proj = np.ascontiguousarray(np.asarray(L0_attn_W_proj, dtype=np.float32))
    w1 = np.asarray(L0_ffn_w1_w, dtype=np.float32)
    w2 = np.asarray(L0_ffn_w2_w, dtype=np.float32)
    h1 = np.asarray(L0_ffn_h1_w, dtype=np.float32)
    h2 = np.asarray(L0_ffn_h2_w, dtype=np.float32)
    w3 = np.asarray(L0_ffn_w3_w, dtype=np.float32)
    s1 = np.asarray(L0_ffn_w1_s, dtype=np.float32)
    s2 = np.asarray(L0_ffn_w2_s, dtype=np.float32)
    sh1 = np.asarray(L0_ffn_h1_s, dtype=np.float32)
    sh2 = np.asarray(L0_ffn_h2_s, dtype=np.float32)
    s3 = np.asarray(L0_ffn_w3_s, dtype=np.float32)

    w1t = np.ascontiguousarray(
        w1.T.reshape(8, 128, 32, 128).transpose(2, 1, 0, 3).reshape(32, 128, 1024))
    w2t = np.ascontiguousarray(
        w2.T.reshape(8, 128, 32, 128).transpose(2, 1, 0, 3).reshape(32, 128, 1024))
    h1t = np.ascontiguousarray(
        h1.T.reshape(32, 128, 32, 128).transpose(2, 1, 0, 3).reshape(32, 128, 4096))
    h2t = np.ascontiguousarray(
        h2.T.reshape(32, 128, 32, 128).transpose(2, 1, 0, 3).reshape(32, 128, 4096))
    w3t = np.ascontiguousarray(w3.T.reshape(32, 128, 1024))
    s1c = np.ascontiguousarray(s1.reshape(32, 128).T)
    s2c = np.ascontiguousarray(s2.reshape(32, 128).T)
    sh1c = np.ascontiguousarray(sh1.reshape(32, 128).T)
    sh2c = np.ascontiguousarray(sh2.reshape(32, 128).T)
    s3bc = np.ascontiguousarray(np.broadcast_to(s3, (128, 1024)))

    shared = dict(W_attn=W_attn, W_proj=W_proj, w1t=w1t, w2t=w2t,
                  h1t=h1t, h2t=h2t, w3t=w3t, s1c=s1c, s2c=s2c,
                  sh1c=sh1c, sh2c=sh2c, s3bc=s3bc)

    in_maps = []
    core_spans = []
    p_idx = np.arange(128)[:, None]
    f_idx = np.arange(512)[None, :]
    for c in range(8):
        g = c // 2 + 1
        b = c % 2
        spans = spans_for_class(g)
        core_spans.append((b, spans))
        x_own = np.ascontiguousarray(
            np.concatenate([x[b, sp * SPAN:(sp + 1) * SPAN] for sp in spans], axis=0))
        m = np.zeros((NSLOT, 16, 128, 512), dtype=np.float32)
        for s in range(NSLOT):
            for j in range(16):
                kb = 16 * s + j
                m[s, j] = ((kb * 128 + p_idx) <= (spans[s] * 512 + f_idx))
        in_maps.append(dict(shared, x_own=x_own,
                            x_batch=np.ascontiguousarray(x[b]),
                            masks=m.astype(ml_dtypes.bfloat16)))

    nc = _get_nc()
    res = run_bass_kernel_spmd(nc, in_maps, core_ids=list(range(8)))

    out_full = np.empty((2, T, C), dtype=np.float32)
    for c in range(8):
        b, spans = core_spans[c]
        o = res.results[c]["out"]
        for si, sp in enumerate(spans):
            out_full[b, sp * SPAN:(sp + 1) * SPAN] = o[si * SPAN:(si + 1) * SPAN]
    return out_full


# version 08
# speedup vs baseline: 1.5302x; 1.0591x over previous
import collections
import os
from contextlib import ExitStack

_KPH = int(os.environ.get("KPH", "9"))
_KP4S = int(os.environ.get("KP4S", "4"))
_KP4Q = int(os.environ.get("KP4Q", "4"))
_KP4L = int(os.environ.get("KP4L", "6"))

import numpy as np
import ml_dtypes

import concourse.bass as bass
import concourse.mybir as mybir
from concourse.bass_utils import run_bass_kernel_spmd

F32 = mybir.dt.float32
BF16 = mybir.dt.bfloat16
AF = mybir.ActivationFunctionType
ALU = mybir.AluOpType
AX = mybir.AxisListType.X

B, T, C = 2, 8192, 1024
DH = 4096
SPAN = 512
NSLOT = 4
OWN = NSLOT * SPAN
ESCH = [16, 32, 48, 64]


def spans_for_class(g):
    return [g - 1, 8 - g, 7 + g, 16 - g]


class Sched:
    ENGS = ("pe", "sc", "ve", "sy", "gp")

    def __init__(self):
        self.recs = []
        self.cnt = collections.defaultdict(int)
        self.marks = {}

    def mark(self, ph):
        self.marks[ph] = len(self.recs)

    def emit(self, eng, fn, waits=None, inc=None):
        assert eng in self.ENGS
        self.recs.append((eng, dict(waits or {}), fn, inc))
        if inc is not None:
            self.cnt[inc[0]] += inc[1]
            return self.cnt[inc[0]]
        return None

    def c(self, sem):
        return self.cnt[sem]


def build_nc():
    nc = bass.Bass("TRN2", target_bir_lowering=False, debug=False,
                   detect_race_conditions=False)

    dt_in = lambda n, shp, dt=F32: nc.dram_tensor(n, shp, dt, kind="ExternalInput")
    x_own_t = dt_in("x_own", [OWN, C])
    x_batch_t = dt_in("x_batch", [T, C])
    W_attn_t = dt_in("W_attn", [C, 3 * C])
    W_proj_t = dt_in("W_proj", [C, C])
    w1t_t = dt_in("w1t", [32, 128, 1024])
    w2t_t = dt_in("w2t", [32, 128, 1024])
    h1t_t = dt_in("h1t", [32, 128, 4096])
    h2t_t = dt_in("h2t", [32, 128, 4096])
    w3t_t = dt_in("w3t", [32, 128, 1024])
    s1c_t = dt_in("s1c", [128, 32])
    s2c_t = dt_in("s2c", [128, 32])
    sh1c_t = dt_in("sh1c", [128, 32])
    sh2c_t = dt_in("sh2c", [128, 32])
    s3bc_t = dt_in("s3bc", [128, 1024])
    masks_t = dt_in("masks", [NSLOT, 16, 128, 512], BF16)
    out_t = nc.dram_tensor("out", [OWN, C], F32, kind="ExternalOutput")

    _dbg = os.environ.get("KDBG") == "1"
    dt_int = lambda n, shp, dt: nc.dram_tensor(
        n, shp, dt, kind=("ExternalOutput" if _dbg else "Internal"))
    xno_d = dt_int("xno_d", [OWN, C], BF16)
    xn_d = dt_int("xn_d", [T, C], BF16)
    kT_d = dt_int("kT_d", [C, T], BF16)
    v_d = dt_int("v_d", [T, C], BF16)
    y_d = dt_int("y_d", [C, OWN], BF16)
    h2o_d = dt_int("h2o_d", [DH, OWN], BF16)
    lnst_d = dt_int("lnst_d", [80, 128, 8], F32)
    w1b_d = dt_int("w1b_d", [32, 128, 1024], BF16)
    w2b_d = dt_int("w2b_d", [32, 128, 1024], BF16)
    h1b_d = dt_int("h1b_d", [32, 128, 4096], BF16)
    h2b_d = dt_int("h2b_d", [32, 128, 4096], BF16)

    x_own = x_own_t.ap()
    x_batch = x_batch_t.ap()
    W_attn = W_attn_t.ap()
    W_proj = W_proj_t.ap()
    out = out_t.ap()

    S = Sched()

    with ExitStack() as es:
        e = es.enter_context
        BQ = e(nc.sbuf_tensor("BQ", [128, 16384], BF16))
        BX = e(nc.sbuf_tensor("BX", [128, 16384], BF16))
        BH = e(nc.sbuf_tensor("BH", [128, 32768], BF16))
        BF = e(nc.sbuf_tensor("BF", [128, 7680], F32))
        BXW = e(nc.sbuf_tensor("BXW", [128, 4096], BF16))
        P6Y = e(nc.sbuf_tensor("P6Y", [128, 2048], BF16))
        SCL = e(nc.sbuf_tensor("SCL", [128, 152], F32))
        pb = [e(nc.psum_tensor(f"pb{i}", [128, 512], F32)) for i in range(8)]

        sem_names = ["X0", "X1", "XH0", "XH1", "XOT", "WAT", "PRO",
                     "MK0", "MK1", "KV0", "KV1", "WS0", "WS1",
                     "W30", "W31", "H20", "H21", "YG0", "YG1", "XT0", "XT1",
                     "CAST", "PE", "SC", "VE",
                     "GS0", "GS1", "KS0", "KS1", "VS0", "VS1",
                     "YS0", "YS1", "HS0", "HS1", "OS0", "OS1",
                     "ZPE", "ZSC", "ZVE"]
        sem = {n: e(nc.semaphore(n)) for n in sem_names}

        COL_S1, COL_S2, COL_SH1, COL_SH2 = 0, 32, 64, 96
        COL_B005, COL_EPS, COL_VS, COL_QS = 128, 129, 130, 131
        COL_SUM, COL_SSQ, COL_MEAN, COL_MSQ = 132, 134, 136, 138
        COL_VAR, COL_TMP, COL_RSTD = 140, 142, 144

        def scl(col):
            return SCL[:, col:col + 1]

        for col, t in ((COL_S1, s1c_t), (COL_S2, s2c_t), (COL_SH1, sh1c_t),
                       (COL_SH2, sh2c_t)):
            S.emit("sy", lambda g, c0=col, tt=t: g.dma_start(
                out=SCL[:, c0:c0 + 32], in_=tt.ap()), inc=("PRO", 16))
        pro_scales = S.c("PRO")
        S.emit("ve", lambda g: g.memset(scl(COL_B005), 0.005), inc=("VE", 1))
        S.emit("ve", lambda g: g.memset(SCL[:, 146:147], 0.25), inc=("VE", 1))
        ve_warm = S.c("VE")
        for fn_ in (AF.Sqrt, AF.Silu, AF.Gelu, AF.Sqrt):
            for rep in range(4):
                S.emit("sc", lambda g, f=fn_: g.activation(
                    SCL[:, 147:148], SCL[:, 146:147], f),
                    waits={"VE": ve_warm}, inc=("SC", 1))
        S.emit("ve", lambda g: g.memset(scl(COL_EPS), 1e-5), inc=("VE", 1))
        S.emit("ve", lambda g: g.memset(scl(COL_VS), 1e-3), inc=("VE", 1))
        S.emit("ve", lambda g: g.memset(scl(COL_QS), 0.125), inc=("VE", 1))

        CH = 16384
        st_post = 0
        for src_t, dst_t, nel in ((w1t_t, w1b_d, 32 * 128 * 1024),
                                  (w2t_t, w2b_d, 32 * 128 * 1024),
                                  (h1t_t, h1b_d, 32 * 128 * 4096),
                                  (h2t_t, h2b_d, 32 * 128 * 4096)):
            per = nel // 128
            for c0 in range(0, per, CH):
                i_ap = bass.AP(src_t, c0, [[per, 128], [1, CH]])
                o_ap = bass.AP(dst_t, c0, [[per, 128], [1, CH]])
                ld_post = S.emit("gp", lambda g, i=i_ap: g.dma_start(
                    out=BH[:, 0:CH], in_=i), waits={"CAST": st_post},
                    inc=("CAST", 16))
                st_post = S.emit("gp", lambda g, o=o_ap: g.dma_start(
                    out=o, in_=BH[:, 0:CH]), waits={"CAST": ld_post},
                    inc=("CAST", 16))
        cast_total = S.c("CAST")
        for ic in range(8):
            S.emit("gp", lambda g, ic=ic: g.dma_start(
                out=BQ[:, ic * 2048:(ic + 1) * 2048],
                in_=W_attn[ic * 128:(ic + 1) * 128, 1024:3072]), inc=("WAT", 16))
        wat_kv = S.c("WAT")

        ln_ve, ln_sc, ln_store, ln_ssq = {}, {}, {}, {}

        def ln_tile(ti, i, src_ap, dst_d):
            par = ti % 2
            xs = BF[:, par * 1024:(par + 1) * 1024]
            xnb = BH[:, par * 1024:(par + 1) * 1024]
            sq = BF[:, 2048:3072]
            xsem = "X0" if par == 0 else "X1"
            waits = {}
            if ti >= 2:
                waits = {"VE": ln_ve[ti - 2], "SC": ln_sc[ti - 2]}
            if ti == 0:
                waits["CAST"] = cast_total
            post = S.emit("sy", lambda g, a=xs, s=src_ap, i=i: g.dma_start(
                out=a, in_=s[i * 128:(i + 1) * 128, :]), waits=waits, inc=(xsem, 16))
            S.emit("ve", lambda g, a=xs, c=par: g.reduce_sum(
                SCL[:, COL_SUM + c:COL_SUM + c + 1], a, axis=AX),
                waits={xsem: post}, inc=("VE", 1))
            sqw = {xsem: post}
            if ti >= 1:
                sqw["VE"] = ln_ssq[ti - 1]
            ln_sc[ti] = S.emit("sc", lambda g, a=xs, c=par: g.activation(
                sq, a, AF.Square),
                waits=sqw, inc=("SC", 1))
            ln_ssq[ti] = S.emit("ve", lambda g, c=par: g.reduce_sum(
                SCL[:, COL_SSQ + c:COL_SSQ + c + 1], sq, axis=AX),
                waits={"SC": ln_sc[ti]}, inc=("VE", 1))
            S.emit("ve", lambda g, c=par: g.tensor_scalar_mul(
                SCL[:, COL_MEAN + c:COL_MEAN + c + 1],
                SCL[:, COL_SUM + c:COL_SUM + c + 1], 1.0 / 1024.0))
            S.emit("ve", lambda g, c=par: g.tensor_mul(
                SCL[:, COL_MSQ + c:COL_MSQ + c + 1],
                SCL[:, COL_MEAN + c:COL_MEAN + c + 1],
                SCL[:, COL_MEAN + c:COL_MEAN + c + 1]))
            ve_var = S.emit("ve", lambda g, c=par: g.scalar_tensor_tensor(
                SCL[:, COL_VAR + c:COL_VAR + c + 1],
                SCL[:, COL_SSQ + c:COL_SSQ + c + 1], 1.0 / 1024.0,
                SCL[:, COL_MSQ + c:COL_MSQ + c + 1], ALU.mult, ALU.subtract),
                inc=("VE", 1))
            sc_sqrt = S.emit("sc", lambda g, c=par: g.activation(
                SCL[:, COL_TMP + c:COL_TMP + c + 1],
                SCL[:, COL_VAR + c:COL_VAR + c + 1], AF.Sqrt,
                bias=scl(COL_EPS)), waits={"VE": ve_var}, inc=("SC", 1))
            S.emit("ve", lambda g, c=par: g.reciprocal(
                SCL[:, COL_RSTD + c:COL_RSTD + c + 1],
                SCL[:, COL_TMP + c:COL_TMP + c + 1]), waits={"SC": sc_sqrt})
            st_stage = SCL[:, 148 - par * 0:148]
            ve_dump = S.emit("ve", lambda g, c=par: g.tensor_copy(
                BF[:, 3072 + 8 * (1 + c):3072 + 8 * (1 + c) + 7],
                bass.AP(SCL, 132 + c, [[152, 128], [2, 7]])), inc=("VE", 1))
            S.emit("gp", lambda g, ti=ti, c=par: g.dma_start(
                out=bass.AP(lnst_d, ti * 128 * 8, [[8, 128], [1, 7]]),
                in_=BF[:, 3072 + 8 * (1 + c):3072 + 8 * (1 + c) + 7]),
                waits={"VE": ve_dump}, inc=("CAST", 16))
            gsem = "GS0" if par == 0 else "GS1"
            w2 = {gsem: ln_store[ti - 2]} if ti >= 2 else {}
            ln_ve[ti] = S.emit("ve", lambda g, a=xs, o=xnb, c=par: g.tensor_scalar(
                o, a, SCL[:, COL_MEAN + c:COL_MEAN + c + 1],
                SCL[:, COL_RSTD + c:COL_RSTD + c + 1], ALU.subtract, ALU.mult),
                waits=w2, inc=("VE", 1))
            ln_store[ti] = S.emit("gp", lambda g, o=xnb, d=dst_d, i=i: g.dma_start(
                out=d.ap()[i * 128:(i + 1) * 128, :], in_=o),
                waits={"VE": ln_ve[ti]}, inc=(gsem, 16))

        for i in range(16):
            ln_tile(i, i, x_own, xno_d)
        ph1_gs = {"GS0": S.c("GS0"), "GS1": S.c("GS1")}
        for i in range(64):
            ln_tile(16 + i, i, x_batch, xn_d)
        ph2_gs = {"GS0": S.c("GS0"), "GS1": S.c("GS1")}
        S.mark(1)

        prev_use = {}
        kst_prev = {}
        vst_prev = {}
        hg_last_pe = {}
        for hg in range(8):
            par = hg % 2
            xsem = "XH0" if par == 0 else "XH1"
            w = dict(ph2_gs) if hg == 0 else {}
            if hg >= 2:
                w["PE"] = hg_last_pe[hg - 2]
            xh_post = None
            for ic in range(8):
                xh_post = S.emit("sy", lambda g, hg=hg, ic=ic, p=par: g.dma_start_transpose(
                    BH[:, p * 8192 + ic * 1024: p * 8192 + (ic + 1) * 1024],
                    xn_d.ap()[hg * 1024:(hg + 1) * 1024, ic * 128:(ic + 1) * 128]),
                    waits=(w if ic == 0 else {}), inc=(xsem, 16))
            for oc in range(8):
                b0, b1 = pb[(oc % 2) * 2], pb[(oc % 2) * 2 + 1]
                kk = ("k", oc % 2)
                mmw = {xsem: xh_post, "WAT": wat_kv}
                if kk in prev_use:
                    mmw["SC"] = prev_use[kk]
                pe_post = None
                for ic in range(8):
                    lhs = BQ[:, ic * 2048 + oc * 128: ic * 2048 + (oc + 1) * 128]
                    rhs0 = BH[:, par * 8192 + ic * 1024: par * 8192 + ic * 1024 + 512]
                    rhs1 = BH[:, par * 8192 + ic * 1024 + 512: par * 8192 + (ic + 1) * 1024]
                    S.emit("pe", lambda g, o=b0, l=lhs, r=rhs0, st=(ic == 0), sp=(ic == 7):
                           g.matmul(o[:], l, r, start=st, stop=sp),
                           waits=(mmw if ic == 0 else {}))
                    pe_post = S.emit("pe", lambda g, o=b1, l=lhs, r=rhs1, st=(ic == 0), sp=(ic == 7):
                                     g.matmul(o[:], l, r, start=st, stop=sp),
                                     inc=("PE", 1) if ic == 7 else None)
                kst = BH[:, 16384 + (oc % 2) * 1024: 16384 + (oc % 2) * 1024 + 1024]
                ksem = "KS0" if oc % 2 == 0 else "KS1"
                evw = {"PE": pe_post}
                if (oc % 2) in kst_prev:
                    evw[ksem] = kst_prev[oc % 2]
                S.emit("sc", lambda g, o=kst, s=b0: g.copy(o[:, 0:512], s[:]), waits=evw)
                prev_use[kk] = S.emit("sc", lambda g, o=kst, s=b1: g.copy(
                    o[:, 512:1024], s[:]), inc=("SC", 1))
                kst_prev[oc % 2] = S.emit("sc", lambda g, o=kst, hg=hg, oc=oc: g.dma_start(
                    out=kT_d.ap()[oc * 128:(oc + 1) * 128, hg * 1024:(hg + 1) * 1024],
                    in_=o), waits={"SC": prev_use[kk]}, inc=(ksem, 16))
            for tc in range(8):
                b0, b1 = pb[4 + (tc % 2) * 2], pb[4 + (tc % 2) * 2 + 1]
                kk = ("v", tc % 2)
                mmw = {}
                if kk in prev_use:
                    mmw["SC"] = prev_use[kk]
                pe_post = None
                for ic in range(8):
                    lhs = BH[:, par * 8192 + ic * 1024 + tc * 128: par * 8192 + ic * 1024 + (tc + 1) * 128]
                    rhs0 = BQ[:, ic * 2048 + 1024: ic * 2048 + 1536]
                    rhs1 = BQ[:, ic * 2048 + 1536: ic * 2048 + 2048]
                    S.emit("pe", lambda g, o=b0, l=lhs, r=rhs0, st=(ic == 0), sp=(ic == 7):
                           g.matmul(o[:], l, r, start=st, stop=sp),
                           waits=(mmw if ic == 0 else {}))
                    pe_post = S.emit("pe", lambda g, o=b1, l=lhs, r=rhs1, st=(ic == 0), sp=(ic == 7):
                                     g.matmul(o[:], l, r, start=st, stop=sp),
                                     inc=("PE", 1) if ic == 7 else None)
                vst = BH[:, 18432 + (tc % 2) * 1024: 18432 + (tc % 2) * 1024 + 1024]
                vsem = "VS0" if tc % 2 == 0 else "VS1"
                evw = {"PE": pe_post}
                if (tc % 2) in vst_prev:
                    evw[vsem] = vst_prev[tc % 2]
                S.emit("sc", lambda g, o=vst, s=b0: g.activation(
                    o[:, 0:512], s[:], AF.Copy, scale=scl(COL_VS)), waits=evw)
                prev_use[kk] = S.emit("sc", lambda g, o=vst, s=b1: g.activation(
                    o[:, 512:1024], s[:], AF.Copy, scale=scl(COL_VS)), inc=("SC", 1))
                vst_prev[tc % 2] = S.emit("sc", lambda g, o=vst, hg=hg, tc=tc: g.dma_start(
                    out=v_d.ap()[hg * 1024 + tc * 128: hg * 1024 + (tc + 1) * 128, :],
                    in_=o), waits={"SC": prev_use[kk]}, inc=(vsem, 16))
            hg_last_pe[hg] = S.c("PE")
        ph3_pe_total = S.c("PE")
        ph3_sc_total = S.c("SC")
        ph3_kv_st = {s: S.c(s) for s in ("KS0", "KS1", "VS0", "VS1")}
        S.mark(2)

        for ic in range(8):
            S.emit("gp", lambda g, ic=ic: g.dma_start(
                out=BH[:, 16384 + ic * 1024: 16384 + (ic + 1) * 1024],
                in_=W_attn[ic * 128:(ic + 1) * 128, 0:1024]),
                waits=(dict(ph3_kv_st) if ic == 0 else {}), inc=("WAT", 16))
        wat_q = S.c("WAT")
        for ic in range(8):
            S.emit("sy", lambda g, ic=ic: g.dma_start_transpose(
                BX[:, ic * 2048:(ic + 1) * 2048],
                xno_d.ap()[:, ic * 128:(ic + 1) * 128]),
                waits=(dict(ph1_gs) if ic == 0 else {}), inc=("XOT", 16))
        xot_total = S.c("XOT")
        qevac = {}
        for oc in range(8):
            banks = [pb[0], pb[1], pb[2], pb[3]] if oc % 2 == 0 else [pb[4], pb[5], pb[6], pb[7]]
            mmw = {"XOT": xot_total, "WAT": wat_q,
                   "SC": qevac.get(oc - 2, ph3_sc_total)}
            pe_post = None
            for ic in range(8):
                lhs = BH[:, 16384 + ic * 1024 + oc * 128: 16384 + ic * 1024 + (oc + 1) * 128]
                for j in range(4):
                    rhs = BX[:, ic * 2048 + j * 512: ic * 2048 + (j + 1) * 512]
                    pe_post = S.emit("pe", lambda g, o=banks[j], l=lhs, r=rhs,
                                     st=(ic == 0), sp=(ic == 7): g.matmul(
                                         o[:], l, r, start=st, stop=sp),
                                     waits=(mmw if (ic == 0 and j == 0) else {}),
                                     inc=("PE", 1) if (ic == 7 and j == 3) else None)
            for j in range(4):
                qevac[oc] = S.emit("sc", lambda g, oc=oc, j=j, s=banks[j]: g.activation(
                    BQ[:, oc * 2048 + j * 512: oc * 2048 + (j + 1) * 512],
                    s[:], AF.Copy, scale=scl(COL_QS)),
                    waits={"PE": pe_post}, inc=("SC", 1))
        ph35_sc_total = S.c("SC")
        S.mark(3)

        pm = {}
        kv_prev = {}
        mask_prev = {}
        yev = {0: ph35_sc_total, 1: ph35_sc_total}
        yst_prev = {}
        first_kv = True
        pe_av = None
        for s in range(_KP4S):
            mpar = s % 2
            msem = "MK0" if mpar == 0 else "MK1"
            mw = {}
            if s == 0:
                mw["PE"] = ph3_pe_total
            if mpar in mask_prev:
                mw["VE"] = mask_prev[mpar]
            for j in range(16):
                S.emit("sy", lambda g, s=s, j=j, mp=mpar: g.dma_start(
                    out=BH[:, mp * 8192 + j * 512: mp * 8192 + (j + 1) * 512],
                    in_=bass.AP(masks_t, (s * 16 + j) * 128 * 512,
                                [[512, 128], [1, 512]])),
                    waits=(mw if j == 0 else {}), inc=(msem, 16))
            mask_total = S.c(msem)
            nkb = ESCH[s]
            for q in range(_KP4Q):
                kv_post = {}
                for kb in range(nkb):
                    sup, jj = kb // 4, kb % 4
                    spar = sup % 2
                    ksem = "KV0" if spar == 0 else "KV1"
                    if jj == 0:
                        lw = {}
                        if first_kv:
                            lw.update(ph3_kv_st)
                            first_kv = False
                        if spar in kv_prev:
                            lw["PE"] = kv_prev[spar]
                        for hpc in range(2):
                            S.emit("sy", lambda g, q=q, hpc=hpc, sup=sup, p=spar: g.dma_start(
                                out=BXW[:, p * 1024 + hpc * 512: p * 1024 + (hpc + 1) * 512],
                                in_=kT_d.ap()[(q * 2 + hpc) * 128:(q * 2 + hpc + 1) * 128,
                                              sup * 512:(sup + 1) * 512]),
                                waits=(lw if hpc == 0 else {}), inc=(ksem, 16))
                        for vj in range(4):
                            kv_post[sup] = S.emit("sy", lambda g, q=q, vj=vj, sup=sup, p=spar: g.dma_start(
                                out=BXW[:, 2048 + p * 1024 + vj * 256: 2048 + p * 1024 + (vj + 1) * 256],
                                in_=v_d.ap()[sup * 512 + vj * 128: sup * 512 + (vj + 1) * 128,
                                             q * 256:(q + 1) * 256]), inc=(ksem, 16))
                    for hp in range(2):
                        pp = (kb * 2 + hp) % 2
                        bA, bB = pb[pp * 2], pb[pp * 2 + 1]
                        qoff = (q * 2 + hp) * 2048 + s * 512
                        koff = spar * 1024 + hp * 512 + jj * 128
                        if _KP4L < 1:
                            continue
                        qkw = {ksem: kv_post[sup]}
                        if pp in pm:
                            qkw["SC"] = pm[pp].get("sc", 0)
                            qkw["VE"] = pm[pp].get("ve3", 0)
                        else:
                            qkw["SC"] = ph35_sc_total
                        qkw = {k: v for k, v in qkw.items() if v}
                        S.emit("pe", lambda g, o=bA, ko=koff, qo=qoff: g.matmul(
                            o[:], BXW[0:64, ko:ko + 128], BQ[0:64, qo:qo + 512],
                            start=True, stop=True, tile_position=(0, 0)), waits=qkw)
                        pe_qk = S.emit("pe", lambda g, o=bB, ko=koff, qo=qoff: g.matmul(
                            o[:], BXW[64:128, ko:ko + 128], BQ[64:128, qo:qo + 512],
                            start=True, stop=True, tile_position=(64, 0)), inc=("PE", 1))
                        RA, RB = BF[:, 0:512], BF[:, 512:1024]
                        qA = BF[:, 1024 + pp * 1024: 1536 + pp * 1024]
                        qB = BF[:, 1536 + pp * 1024: 2048 + pp * 1024]
                        tA = BF[:, 3072 + pp * 1024: 3584 + pp * 1024]
                        tB = BF[:, 3584 + pp * 1024: 4096 + pp * 1024]
                        wA = BF[:, 5120 + pp * 1024: 5632 + pp * 1024]
                        wB = BF[:, 5632 + pp * 1024: 6144 + pp * 1024]
                        AA = BH[:, 16384 + (pp * 2) * 512: 16384 + (pp * 2 + 1) * 512]
                        AB = BH[:, 16384 + (pp * 2 + 1) * 512: 16384 + (pp * 2 + 2) * 512]
                        if _KP4L < 2:
                            pm[pp] = {"pe_qk": pe_qk}
                            kv_prev[spar] = pe_qk
                            pe_av = pe_qk
                            continue
                        scw = {"PE": pe_qk}
                        if pp in pm:
                            scw["VE"] = pm[pp].get("ve5", 0)
                        scw = {k: v for k, v in scw.items() if v}
                        S.emit("sc", lambda g, o=RA, i=bA: g.activation(o, i[:], AF.Relu),
                               waits=scw)
                        S.emit("sc", lambda g, o=RB, i=bB: g.activation(o, i[:], AF.Relu))
                        S.emit("sc", lambda g, o=qA, i=RA: g.activation(
                            o, i, AF.Square, bias=scl(COL_B005)))
                        sc_p = S.emit("sc", lambda g, o=qB, i=RB: g.activation(
                            o, i, AF.Square, bias=scl(COL_B005)), inc=("SC", 1))
                        if _KP4L < 3:
                            pm[pp] = {"sc": sc_p}
                            kv_prev[spar] = pe_qk
                            pe_av = pe_qk
                            continue
                        S.emit("ve", lambda g, o=tA, i=bA: g.tensor_scalar(
                            o, i[:], -0.01, 0.999975, ALU.mult, ALU.add),
                            waits={"PE": pe_qk, "SC": sc_p})
                        ve3 = S.emit("ve", lambda g, o=tB, i=bB: g.tensor_scalar(
                            o, i[:], -0.01, 0.999975, ALU.mult, ALU.add), inc=("VE", 1))
                        if _KP4L < 4:
                            pm[pp] = {"sc": sc_p, "ve3": ve3}
                            kv_prev[spar] = pe_qk
                            pe_av = pe_qk
                            continue
                        vew = {"SC": sc_p}
                        if pp in pm:
                            vew["PE"] = pm[pp].get("pe_av", 0)
                        vew = {k: v for k, v in vew.items() if v}
                        S.emit("ve", lambda g, o=wA, a=qA, b=tA: g.tensor_tensor(
                            o, a, b, ALU.add), waits=vew)
                        S.emit("ve", lambda g, o=wB, a=qB, b=tB: g.tensor_tensor(
                            o, a, b, ALU.add))
                        S.emit("ve", lambda g, o=AA, t=tA, w=wA: g.scalar_tensor_tensor(
                            o, t, 1.999975, w, ALU.is_le, ALU.mult))
                        ve5 = S.emit("ve", lambda g, o=AB, t=tB, w=wB: g.scalar_tensor_tensor(
                            o, t, 1.999975, w, ALU.is_le, ALU.mult), inc=("VE", 1))
                        if kb >= 16 * s and _KP4L >= 5:
                            moff = mpar * 8192 + (kb - 16 * s) * 512
                            S.emit("ve", lambda g, o=AA, mo=moff: g.tensor_tensor(
                                o, o, BH[:, mo:mo + 512], ALU.mult),
                                waits={msem: mask_total})
                            ve5 = S.emit("ve", lambda g, o=AB, mo=moff: g.tensor_tensor(
                                o, o, BH[:, mo:mo + 512], ALU.mult), inc=("VE", 1))
                            mask_prev[mpar] = ve5
                        if _KP4L < 6:
                            pm[pp] = {"sc": sc_p, "ve3": ve3, "ve5": ve5}
                            kv_prev[spar] = pe_qk
                            pe_av = pe_qk
                            continue
                        voff = 2048 + spar * 1024 + jj * 256 + hp * 128
                        yb = pb[4 + hp]
                        avw = {"VE": ve5}
                        if kb == 0:
                            avw["SC"] = yev[hp]
                        S.emit("pe", lambda g, o=yb, vo=voff, a=AA, st=(kb == 0),
                               sp=(kb == nkb - 1): g.matmul(
                                   o[0:64, :], BXW[:, vo:vo + 64], a,
                                   start=st, stop=sp, tile_position=(0, 0)), waits=avw)
                        pe_av = S.emit("pe", lambda g, o=yb, vo=voff, a=AB,
                                       st=(kb == 0), sp=(kb == nkb - 1): g.matmul(
                                           o[64:128, :], BXW[:, vo + 64:vo + 128], a,
                                           start=st, stop=sp, tile_position=(0, 64),
                                           skip_group_check=True),
                                       inc=("PE", 1))
                        pm[pp] = {"sc": sc_p, "ve3": ve3, "ve5": ve5, "pe_av": pe_av}
                        kv_prev[spar] = pe_av
                for hp in range(2 if _KP4L >= 6 else 0):
                    yst = BH[:, 18432 + hp * 512: 18432 + (hp + 1) * 512]
                    ysem2 = "YS0" if hp == 0 else "YS1"
                    evw = {"PE": pe_av}
                    if hp in yst_prev:
                        evw[ysem2] = yst_prev[hp]
                    yev[hp] = S.emit("sc", lambda g, o=yst, s=pb[4 + hp]: g.copy(
                        o, s[:]), waits=evw, inc=("SC", 1))
                    yst_prev[hp] = S.emit("sc", lambda g, o=yst, q=q, hp=hp, s=s: g.dma_start(
                        out=y_d.ap()[(q * 2 + hp) * 128:(q * 2 + hp + 1) * 128,
                                     s * 512:(s + 1) * 512], in_=o),
                        waits={"SC": yev[hp]}, inc=(ysem2, 16))
        ph4_y_st = {"YS0": S.c("YS0"), "YS1": S.c("YS1")}
        ph4_pe_total = S.c("PE")
        ph4_sc_total = S.c("SC")
        S.mark(4)

        w_item = 0
        ws_prev = {}
        ffn_gate = {}
        h2_last_pe = ph4_pe_total
        h2o_sst = {}
        for sp in range(NSLOT):
            for rep in range(4):
                S.emit("sc", lambda g: g.activation(
                    SCL[:, 147:148], SCL[:, 146:147], AF.Silu), inc=("SC", 1))
            for oc in range(32):
                par = w_item % 2
                wsem = "WS0" if par == 0 else "WS1"
                lw = {}
                if sp == 0 and oc < 2:
                    lw["CAST"] = cast_total
                    lw["PE"] = ph4_pe_total
                if par in ws_prev:
                    lw["PE"] = max(lw.get("PE", 0), ws_prev[par])
                S.emit("sy", lambda g, oc=oc, p=par: g.dma_start(
                    out=BXW[:, p * 2048: p * 2048 + 1024],
                    in_=bass.AP(w1b_d, oc * 128 * 1024, [[1024, 128], [1, 1024]])),
                    waits=lw, inc=(wsem, 16))
                ws_post = S.emit("sy", lambda g, oc=oc, p=par: g.dma_start(
                    out=BXW[:, p * 2048 + 1024: p * 2048 + 2048],
                    in_=bass.AP(w2b_d, oc * 128 * 1024, [[1024, 128], [1, 1024]])),
                    inc=(wsem, 16))
                b1, b2 = pb[oc % 2], pb[2 + oc % 2]
                mmw = {wsem: ws_post, "PE": h2_last_pe}
                gk = ("w12", oc % 2)
                if gk in ffn_gate:
                    mmw["SC"] = ffn_gate[gk]["sc"]
                    mmw["VE"] = ffn_gate[gk]["ve"]
                elif sp == 0:
                    mmw["SC"] = ph4_sc_total
                for ic in range(8):
                    rhs = BX[:, ic * 2048 + sp * 512: ic * 2048 + (sp + 1) * 512]
                    S.emit("pe", lambda g, o=b1, p=par, ic=ic, r=rhs, st=(ic == 0),
                           sp2=(ic == 7): g.matmul(
                               o[:], BXW[:, p * 2048 + ic * 128: p * 2048 + (ic + 1) * 128],
                               r, start=st, stop=sp2),
                           waits=(mmw if ic == 0 else {}),
                           inc=("PE", 1) if ic == 7 else None)
                pe_w1 = S.c("PE")
                for ic in range(8):
                    rhs = BX[:, ic * 2048 + sp * 512: ic * 2048 + (sp + 1) * 512]
                    S.emit("pe", lambda g, o=b2, p=par, ic=ic, r=rhs, st=(ic == 0),
                           sp2=(ic == 7): g.matmul(
                               o[:], BXW[:, p * 2048 + 1024 + ic * 128: p * 2048 + 1024 + (ic + 1) * 128],
                               r, start=st, stop=sp2),
                           inc=("PE", 1) if ic == 7 else None)
                pe_w2 = S.c("PE")
                ws_prev[par] = pe_w2
                silu = BF[:, (oc % 2) * 512:(oc % 2) * 512 + 512]
                sc_post = S.emit("sc", lambda g, o=silu, s=b1, oc=oc: g.activation(
                    o, s[:], AF.Silu, scale=SCL[:, COL_S1 + oc:COL_S1 + oc + 1]),
                    waits={"PE": pe_w1, "PRO": pro_scales}, inc=("SC", 1))
                ve_post = S.emit("ve", lambda g, s=b2, si=silu, oc=oc: g.scalar_tensor_tensor(
                    BH[:, oc * 512:(oc + 1) * 512], s[:],
                    SCL[:, COL_S2 + oc:COL_S2 + oc + 1], si, ALU.mult, ALU.mult),
                    waits={"PE": pe_w2, "SC": sc_post}, inc=("VE", 1))
                ffn_gate[gk] = {"sc": sc_post, "ve": ve_post}
                w_item += 1
            h_ready = S.c("VE")
            for rep in range(4):
                S.emit("sc", lambda g: g.activation(
                    SCL[:, 147:148], SCL[:, 146:147], AF.Gelu), inc=("SC", 1))
            for oc2 in range(32):
                halves = []
                for half in range(2):
                    par = w_item % 2
                    wsem = "WS0" if par == 0 else "WS1"
                    lw = {}
                    if par in ws_prev:
                        lw["PE"] = ws_prev[par]
                    hp_post = S.emit("sy", lambda g, oc2=oc2, half=half, p=par: g.dma_start(
                        out=BXW[:, p * 2048:(p + 1) * 2048],
                        in_=bass.AP(h1b_d, oc2 * 128 * 4096 + half * 2048,
                                    [[4096, 128], [1, 2048]])),
                        waits=lw, inc=(wsem, 16))
                    halves.append((par, wsem, hp_post))
                    w_item += 1
                bk = pb[4 + oc2 % 2]
                gk = ("h1", oc2 % 2)
                for hc in range(32):
                    par, wsem, hp_post = halves[hc // 16]
                    mmw = {}
                    if hc in (0, 16):
                        mmw[wsem] = hp_post
                        if hc == 0:
                            mmw["VE"] = h_ready
                            mmw["SC"] = ffn_gate.get(gk, ph4_sc_total)
                    rhs = BH[:, hc * 512:(hc + 1) * 512]
                    S.emit("pe", lambda g, o=bk, p=par, hc=hc, r=rhs, st=(hc == 0),
                           sp2=(hc == 31): g.matmul(
                               o[:], BXW[:, p * 2048 + (hc % 16) * 128: p * 2048 + ((hc % 16) + 1) * 128],
                               r, start=st, stop=sp2), waits=mmw,
                           inc=("PE", 1) if hc == 31 else None)
                pe_h1 = S.c("PE")
                ws_prev[halves[0][0]] = pe_h1
                ws_prev[halves[1][0]] = pe_h1
                ffn_gate[gk] = S.emit("sc", lambda g, oc2=oc2, s=bk: g.activation(
                    BH[:, 16384 + oc2 * 512: 16384 + (oc2 + 1) * 512], s[:], AF.Gelu,
                    scale=SCL[:, COL_SH1 + oc2:COL_SH1 + oc2 + 1]),
                    waits={"PE": pe_h1}, inc=("SC", 1))
            g_ready = S.c("SC")
            for oc2 in range(32):
                halves = []
                for half in range(2):
                    par = w_item % 2
                    wsem = "WS0" if par == 0 else "WS1"
                    lw = {}
                    if par in ws_prev:
                        lw["PE"] = ws_prev[par]
                    hp_post = S.emit("sy", lambda g, oc2=oc2, half=half, p=par: g.dma_start(
                        out=BXW[:, p * 2048:(p + 1) * 2048],
                        in_=bass.AP(h2b_d, oc2 * 128 * 4096 + half * 2048,
                                    [[4096, 128], [1, 2048]])),
                        waits=lw, inc=(wsem, 16))
                    halves.append((par, wsem, hp_post))
                    w_item += 1
                bk = pb[6 + oc2 % 2]
                gk = ("h2", oc2 % 2)
                for hc in range(32):
                    par, wsem, hp_post = halves[hc // 16]
                    mmw = {}
                    if hc in (0, 16):
                        mmw[wsem] = hp_post
                        if hc == 0:
                            mmw["SC"] = max(g_ready, ffn_gate.get(gk, 0))
                    rhs = BH[:, 16384 + hc * 512: 16384 + (hc + 1) * 512]
                    S.emit("pe", lambda g, o=bk, p=par, hc=hc, r=rhs, st=(hc == 0),
                           sp2=(hc == 31): g.matmul(
                               o[:], BXW[:, p * 2048 + (hc % 16) * 128: p * 2048 + ((hc % 16) + 1) * 128],
                               r, start=st, stop=sp2), waits=mmw,
                           inc=("PE", 1) if hc == 31 else None)
                pe_h2 = S.c("PE")
                ws_prev[halves[0][0]] = pe_h2
                ws_prev[halves[1][0]] = pe_h2
                h2stg = P6Y[:, (oc2 % 2) * 512:(oc2 % 2 + 1) * 512]
                hsem2 = "HS0" if oc2 % 2 == 0 else "HS1"
                evw = {"PE": pe_h2}
                if (oc2 % 2) in h2o_sst:
                    evw[hsem2] = h2o_sst[oc2 % 2]
                ffn_gate[gk] = S.emit("sc", lambda g, o=h2stg, s=bk, oc2=oc2: g.activation(
                    o, s[:], AF.Gelu, scale=SCL[:, COL_SH2 + oc2:COL_SH2 + oc2 + 1]),
                    waits=evw, inc=("SC", 1))
                h2o_sst[oc2 % 2] = S.emit("sc", lambda g, o=h2stg, oc2=oc2, sp=sp: g.dma_start(
                    out=h2o_d.ap()[oc2 * 128:(oc2 + 1) * 128, sp * 512:(sp + 1) * 512],
                    in_=o), waits={"SC": ffn_gate[gk]}, inc=(hsem2, 16))
            h2_last_pe = S.c("PE")
        ph5_h_st = {"HS0": S.c("HS0"), "HS1": S.c("HS1")}
        ph5_pe_total = S.c("PE")
        ph5_sc_total = S.c("SC")
        ph5_ve_total = S.c("VE")
        S.mark(5)

        for cc in range(8):
            S.emit("gp", lambda g, cc=cc: g.dma_start(
                out=BQ[:, cc * 1024:(cc + 1) * 1024],
                in_=W_proj[cc * 128:(cc + 1) * 128, :]),
                waits=({"PE": ph5_pe_total} if cc == 0 else {}), inc=("WAT", 16))
        wat_total = S.c("WAT")
        S.emit("sy", lambda g: g.dma_start(out=BF[:, 2048:3072], in_=s3bc_t.ap()),
               waits={"PE": ph5_pe_total, "VE": ph5_ve_total}, inc=("PRO", 16))
        pro_total = S.c("PRO")
        w3cv = {}
        for hc in range(32):
            par = hc % 2
            wsem = "W30" if par == 0 else "W31"
            lw = {}
            if hc < 2:
                lw["PE"] = ph5_pe_total
                lw["VE"] = ph5_ve_total
            else:
                lw["VE"] = w3cv[hc - 2]
            w3_post = S.emit("sy", lambda g, hc=hc, p=par: g.dma_start(
                out=BF[:, p * 1024:(p + 1) * 1024],
                in_=bass.AP(w3t_t, hc * 128 * 1024, [[1024, 128], [1, 1024]])),
                waits=lw, inc=(wsem, 16))
            cvw = {wsem: w3_post, "PRO": pro_total}
            if hc < 2:
                cvw["PE"] = ph5_pe_total
            w3cv[hc] = S.emit("ve", lambda g, hc=hc, p=par: g.tensor_tensor(
                BH[:, hc * 1024:(hc + 1) * 1024], BF[:, p * 1024:(p + 1) * 1024],
                BF[:, 2048:3072], ALU.mult), waits=cvw, inc=("VE", 1))
        w3_ready = S.c("VE")

        h2l_prev = {}
        yg_prev = {}
        xt_prev = {}
        outstore = {}
        bank_gate = {}
        for grp in range(4):
            h2_posts = []
            for half in range(2):
                hsem = "H20" if half == 0 else "H21"
                lw = {}
                if grp == 0:
                    lw.update(ph5_h_st)
                if (grp - 1, half) in h2l_prev:
                    lw["PE"] = h2l_prev[(grp - 1, half)]
                hpost = None
                for part in range(2):
                    hcb = half * 16 + part * 8
                    i_ap = bass.AP(h2o_d, hcb * 128 * OWN + grp * 512,
                                   [[OWN, 128], [128 * OWN, 8], [1, 512]])
                    hpost = S.emit("sy", lambda g, i=i_ap, half=half, part=part: g.dma_start(
                        out=BX[:, half * 8192 + part * 4096: half * 8192 + (part + 1) * 4096],
                        in_=i), waits=(lw if part == 0 else {}), inc=(hsem, 16))
                h2_posts.append((hsem, hpost))
            ygpar = grp % 2
            ysem = "YG0" if ygpar == 0 else "YG1"
            lw = {}
            if grp == 0:
                lw.update(ph4_y_st)
            if (grp - 2, ygpar) in yg_prev:
                lw["PE"] = yg_prev[(grp - 2, ygpar)]
            y_post = None
            for part in range(2):
                i_ap = bass.AP(y_d, (part * 4) * 128 * OWN + grp * 512,
                               [[OWN, 128], [128 * OWN, 4], [1, 512]])
                y_post = S.emit("sy", lambda g, i=i_ap, p=ygpar, part=part: g.dma_start(
                    out=BQ[:, 8192 + p * 4096 + part * 2048: 8192 + p * 4096 + (part + 1) * 2048],
                    in_=i), waits=(lw if part == 0 else {}), inc=(ysem, 16))
            for half in range(2):
                hsem, hpost = h2_posts[half]
                for tc4 in range(4):
                    for oh in range(2):
                        bk = pb[tc4 * 2 + oh]
                        for hcl in range(16):
                            hc = half * 16 + hcl
                            mmw = {}
                            if hcl == 0:
                                mmw[hsem] = hpost
                                if half == 0:
                                    mmw["VE"] = max(w3_ready,
                                                    bank_gate.get((grp - 1, tc4, oh), 0))
                                    if grp == 0:
                                        mmw["SC"] = ph5_sc_total
                            lhs_off = half * 8192 + (hcl // 8) * 4096 + (hcl % 8) * 512 + tc4 * 128
                            rhs = BH[:, hc * 1024 + oh * 512: hc * 1024 + (oh + 1) * 512]
                            S.emit("pe", lambda g, o=bk, lo=lhs_off, r=rhs,
                                   st=(hc == 0): g.matmul(
                                       o[:], BX[:, lo:lo + 128], r,
                                       start=st, stop=False), waits=mmw)
            for tc4 in range(4):
                tc = grp * 4 + tc4
                xpar = tc % 2
                xsem = "XT0" if xpar == 0 else "XT1"
                lw = {}
                if tc < 2:
                    lw["PE"] = ph5_pe_total
                if (tc - 2) in xt_prev:
                    lw["VE"] = xt_prev[tc - 2]
                xpost = S.emit("sy", lambda g, tc=tc, p=xpar: g.dma_start(
                    out=BF[:, 3072 + p * 1024: 3072 + (p + 1) * 1024],
                    in_=x_own[tc * 128:(tc + 1) * 128, :]), waits=lw, inc=(xsem, 16))
                for oh in range(2):
                    bk = pb[tc4 * 2 + oh]
                    pe_fin = None
                    for cc in range(8):
                        mmw = {}
                        if cc == 0:
                            mmw = {ysem: y_post, "WAT": wat_total}
                        lhs_off = 8192 + ygpar * 4096 + (cc // 4) * 2048 + (cc % 4) * 512 + tc4 * 128
                        rhs = BQ[:, cc * 1024 + oh * 512: cc * 1024 + (oh + 1) * 512]
                        pe_fin = S.emit("pe", lambda g, o=bk, lo=lhs_off, r=rhs,
                                        sp2=(cc == 7): g.matmul(
                                            o[:], BQ[:, lo:lo + 128], r,
                                            start=False, stop=sp2), waits=mmw,
                                        inc=("PE", 1) if cc == 7 else None)
                    yg_prev[(grp, ygpar)] = pe_fin
                    h2l_prev[(grp, 0)] = pe_fin
                    h2l_prev[(grp, 1)] = pe_fin
                    obuf = BF[:, 5120 + (tc % 2) * 1024 + oh * 512: 5120 + (tc % 2) * 1024 + (oh + 1) * 512]
                    xsl = BF[:, 3072 + (tc % 2) * 1024 + oh * 512: 3072 + (tc % 2) * 1024 + (oh + 1) * 512]
                    osem = "OS0" if tc % 2 == 0 else "OS1"
                    evw = {"PE": pe_fin, xsem: xpost}
                    if (tc - 2) in outstore:
                        evw[osem] = outstore[tc - 2]
                    ov = S.emit("ve", lambda g, o=obuf, s=bk, x=xsl: g.tensor_add(
                        o, s[:], x), waits=evw, inc=("VE", 1))
                    bank_gate[(grp, tc4, oh)] = ov
                    xt_prev[tc] = ov
                tc = grp * 4 + tc4
                outstore[tc] = S.emit("gp", lambda g, tc=tc: g.dma_start(
                    out=out[tc * 128:(tc + 1) * 128, :],
                    in_=BF[:, 5120 + (tc % 2) * 1024: 5120 + (tc % 2) * 1024 + 1024]),
                    waits={"VE": xt_prev[tc]},
                    inc=("OS0" if tc % 2 == 0 else "OS1", 16))
        S.mark(6)
        if _KPH < 6 and _KPH in S.marks:
            S.recs = S.recs[:S.marks[_KPH]]
        tot = collections.defaultdict(int)
        for (_en, _w, _fn, _inc) in S.recs:
            if _inc:
                tot[_inc[0]] += _inc[1]
        fin = {k: tot[k] for k in ("GS0", "GS1", "KS0", "KS1", "VS0", "VS1",
                                   "YS0", "YS1", "HS0", "HS1", "OS0", "OS1")
               if tot[k] > 0}
        S.recs.append(("gp", fin, None, None))

        for k, v in S.cnt.items():
            assert v < 60000, (k, v)

        ksafe_engs = os.environ.get("KSAFE", "ve").replace(",", " ").split()
        selfsem_all = {"pe": "ZPE", "sc": "ZSC", "ve": "ZVE"}
        selfsem = {k: v for k, v in selfsem_all.items() if k in ksafe_engs}

        with nc.Block() as block:
            def replay(eng_name):
                last = {}
                zs = selfsem.get(eng_name)

                def run(g):
                    cnt = collections.defaultdict(int)
                    zn = 0
                    prev = None
                    for (en, waits, fn, inc) in S.recs:
                        if en != eng_name:
                            if inc is not None:
                                cnt[inc[0]] += inc[1]
                            continue
                        for sname, val in waits.items():
                            if last.get(sname, -1) < val:
                                g.wait_ge(sem[sname], val)
                                last[sname] = val
                        if fn is None:
                            continue
                        if zs is not None and prev is not None:
                            if last.get(prev[0], -1) < prev[1]:
                                g.wait_ge(sem[prev[0]], prev[1])
                                last[prev[0]] = prev[1]
                        inst = fn(g)
                        is_dma = getattr(inst, "is_dma", None)
                        if inc is not None:
                            cnt[inc[0]] += inc[1]
                            inst.then_inc(sem[inc[0]], inc[1])
                            if inc[1] == 1:
                                prev = (inc[0], cnt[inc[0]])
                            else:
                                prev = None
                        elif zs is not None:
                            zn += 1
                            inst.then_inc(sem[zs], 1)
                            prev = (zs, zn)
                        else:
                            prev = None
                return run

            block.tensor(replay("pe"))
            block.scalar(replay("sc"))
            block.vector(replay("ve"))
            block.sync(replay("sy"))
            block.gpsimd(replay("gp"))

    return nc


_NC_CACHE = None


def _get_nc():
    global _NC_CACHE
    if _NC_CACHE is None:
        _NC_CACHE = build_nc()
    return _NC_CACHE


def kernel(L0_attn_x, L0_attn_W_attn, L0_attn_W_proj,
           L0_ffn_w1_w, L0_ffn_w1_s, L0_ffn_w2_w, L0_ffn_w2_s,
           L0_ffn_h1_w, L0_ffn_h1_s, L0_ffn_h2_w, L0_ffn_h2_s,
           L0_ffn_w3_w, L0_ffn_w3_s):
    x = np.ascontiguousarray(np.asarray(L0_attn_x, dtype=np.float32))
    W_attn = np.ascontiguousarray(np.asarray(L0_attn_W_attn, dtype=np.float32))
    W_proj = np.ascontiguousarray(np.asarray(L0_attn_W_proj, dtype=np.float32))
    w1 = np.asarray(L0_ffn_w1_w, dtype=np.float32)
    w2 = np.asarray(L0_ffn_w2_w, dtype=np.float32)
    h1 = np.asarray(L0_ffn_h1_w, dtype=np.float32)
    h2 = np.asarray(L0_ffn_h2_w, dtype=np.float32)
    w3 = np.asarray(L0_ffn_w3_w, dtype=np.float32)
    s1 = np.asarray(L0_ffn_w1_s, dtype=np.float32)
    s2 = np.asarray(L0_ffn_w2_s, dtype=np.float32)
    sh1 = np.asarray(L0_ffn_h1_s, dtype=np.float32)
    sh2 = np.asarray(L0_ffn_h2_s, dtype=np.float32)
    s3 = np.asarray(L0_ffn_w3_s, dtype=np.float32)

    w1t = np.ascontiguousarray(
        w1.T.reshape(8, 128, 32, 128).transpose(2, 1, 0, 3).reshape(32, 128, 1024))
    w2t = np.ascontiguousarray(
        w2.T.reshape(8, 128, 32, 128).transpose(2, 1, 0, 3).reshape(32, 128, 1024))
    h1t = np.ascontiguousarray(
        h1.T.reshape(32, 128, 32, 128).transpose(2, 1, 0, 3).reshape(32, 128, 4096))
    h2t = np.ascontiguousarray(
        h2.T.reshape(32, 128, 32, 128).transpose(2, 1, 0, 3).reshape(32, 128, 4096))
    w3t = np.ascontiguousarray(w3.T.reshape(32, 128, 1024))
    s1c = np.ascontiguousarray(s1.reshape(32, 128).T)
    s2c = np.ascontiguousarray(s2.reshape(32, 128).T)
    sh1c = np.ascontiguousarray(sh1.reshape(32, 128).T)
    sh2c = np.ascontiguousarray(sh2.reshape(32, 128).T)
    s3bc = np.ascontiguousarray(np.broadcast_to(s3, (128, 1024)))

    shared = dict(W_attn=W_attn, W_proj=W_proj, w1t=w1t, w2t=w2t,
                  h1t=h1t, h2t=h2t, w3t=w3t, s1c=s1c, s2c=s2c,
                  sh1c=sh1c, sh2c=sh2c, s3bc=s3bc)

    in_maps = []
    core_spans = []
    p_idx = np.arange(128)[:, None]
    f_idx = np.arange(512)[None, :]
    for c in range(8):
        g = c // 2 + 1
        b = c % 2
        spans = spans_for_class(g)
        core_spans.append((b, spans))
        x_own = np.ascontiguousarray(
            np.concatenate([x[b, sp * SPAN:(sp + 1) * SPAN] for sp in spans], axis=0))
        m = np.zeros((NSLOT, 16, 128, 512), dtype=np.float32)
        for s in range(NSLOT):
            for j in range(16):
                kb = 16 * s + j
                m[s, j] = ((kb * 128 + p_idx) <= (spans[s] * 512 + f_idx))
        in_maps.append(dict(shared, x_own=x_own,
                            x_batch=np.ascontiguousarray(x[b]),
                            masks=m.astype(ml_dtypes.bfloat16)))

    nc = _get_nc()
    res = run_bass_kernel_spmd(nc, in_maps, core_ids=list(range(8)))

    out_full = np.empty((2, T, C), dtype=np.float32)
    for c in range(8):
        b, spans = core_spans[c]
        o = res.results[c]["out"]
        for si, sp in enumerate(spans):
            out_full[b, sp * SPAN:(sp + 1) * SPAN] = o[si * SPAN:(si + 1) * SPAN]
    return out_full


# version 09
# speedup vs baseline: 1.6900x; 1.1044x over previous
import collections
import os
from contextlib import ExitStack

_KPH = int(os.environ.get("KPH", "9"))
_KP4S = int(os.environ.get("KP4S", "4"))
_KP4Q = int(os.environ.get("KP4Q", "4"))
_KP4L = int(os.environ.get("KP4L", "6"))

import numpy as np
import ml_dtypes

import concourse.bass as bass
import concourse.mybir as mybir
from concourse.bass_utils import run_bass_kernel_spmd

F32 = mybir.dt.float32
BF16 = mybir.dt.bfloat16
AF = mybir.ActivationFunctionType
ALU = mybir.AluOpType
AX = mybir.AxisListType.X

B, T, C = 2, 8192, 1024
DH = 4096
SPAN = 512
NSLOT = 4
OWN = NSLOT * SPAN
ESCH = [16, 32, 48, 64]


def spans_for_class(g):
    return [g - 1, 8 - g, 7 + g, 16 - g]


class Sched:
    ENGS = ("pe", "sc", "ve", "sy", "gp")

    def __init__(self):
        self.recs = []
        self.cnt = collections.defaultdict(int)
        self.marks = {}

    def mark(self, ph):
        self.marks[ph] = len(self.recs)

    def emit(self, eng, fn, waits=None, inc=None, tiny=False):
        assert eng in self.ENGS
        self.recs.append((eng, dict(waits or {}), fn, inc, tiny))
        if inc is not None:
            self.cnt[inc[0]] += inc[1]
            return self.cnt[inc[0]]
        return None

    def c(self, sem):
        return self.cnt[sem]


def build_nc():
    nc = bass.Bass("TRN2", target_bir_lowering=False, debug=False,
                   detect_race_conditions=False)

    dt_in = lambda n, shp, dt=F32: nc.dram_tensor(n, shp, dt, kind="ExternalInput")
    x_own_t = dt_in("x_own", [OWN, C])
    x_batch_t = dt_in("x_batch", [T, C])
    W_attn_t = dt_in("W_attn", [C, 3 * C])
    W_proj_t = dt_in("W_proj", [C, C])
    w1t_t = dt_in("w1t", [32, 128, 1024])
    w2t_t = dt_in("w2t", [32, 128, 1024])
    h1t_t = dt_in("h1t", [32, 128, 4096])
    h2t_t = dt_in("h2t", [32, 128, 4096])
    w3t_t = dt_in("w3t", [32, 128, 1024])
    s1c_t = dt_in("s1c", [128, 32])
    s2c_t = dt_in("s2c", [128, 32])
    sh1c_t = dt_in("sh1c", [128, 32])
    sh2c_t = dt_in("sh2c", [128, 32])
    s3bc_t = dt_in("s3bc", [128, 1024])
    masks_t = dt_in("masks", [NSLOT, 16, 128, 512], BF16)
    out_t = nc.dram_tensor("out", [OWN, C], F32, kind="ExternalOutput")

    _dbg = os.environ.get("KDBG") == "1"
    dt_int = lambda n, shp, dt: nc.dram_tensor(
        n, shp, dt, kind=("ExternalOutput" if _dbg else "Internal"))
    xno_d = dt_int("xno_d", [OWN, C], BF16)
    xn_d = dt_int("xn_d", [T, C], BF16)
    kT_d = dt_int("kT_d", [C, T], BF16)
    v_d = dt_int("v_d", [T, C], BF16)
    y_d = dt_int("y_d", [C, OWN], BF16)
    h2o_d = dt_int("h2o_d", [DH, OWN], BF16)
    lnst_d = dt_int("lnst_d", [80, 128, 8], F32)
    w1b_d = dt_int("w1b_d", [32, 128, 1024], BF16)
    w2b_d = dt_int("w2b_d", [32, 128, 1024], BF16)
    h1b_d = dt_int("h1b_d", [32, 128, 4096], BF16)
    h2b_d = dt_int("h2b_d", [32, 128, 4096], BF16)

    x_own = x_own_t.ap()
    x_batch = x_batch_t.ap()
    W_attn = W_attn_t.ap()
    W_proj = W_proj_t.ap()
    out = out_t.ap()

    S = Sched()

    with ExitStack() as es:
        e = es.enter_context
        BQ = e(nc.sbuf_tensor("BQ", [128, 16384], BF16))
        BX = e(nc.sbuf_tensor("BX", [128, 16384], BF16))
        BH = e(nc.sbuf_tensor("BH", [128, 32768], BF16))
        BF = e(nc.sbuf_tensor("BF", [128, 7680], F32))
        BXW = e(nc.sbuf_tensor("BXW", [128, 4096], BF16))
        P6Y = e(nc.sbuf_tensor("P6Y", [128, 2048], BF16))
        SCL = e(nc.sbuf_tensor("SCL", [128, 152], F32))
        pb = [e(nc.psum_tensor(f"pb{i}", [128, 512], F32)) for i in range(8)]

        sem_names = ["X0", "X1", "XH0", "XH1", "XOT", "WAT", "PRO",
                     "MK0", "MK1", "KV0", "KV1", "WS0", "WS1",
                     "W30", "W31", "H20", "H21", "YG0", "YG1", "XT0", "XT1",
                     "CAST", "PE", "SC", "VE",
                     "GS0", "GS1", "KS0", "KS1", "VS0", "VS1",
                     "YS0", "YS1", "HS0", "HS1", "OS0", "OS1",
                     "ZPE", "ZSC", "ZVE"]
        sem = {n: e(nc.semaphore(n)) for n in sem_names}

        COL_S1, COL_S2, COL_SH1, COL_SH2 = 0, 32, 64, 96
        COL_B005, COL_EPS, COL_VS, COL_QS = 128, 129, 130, 131
        COL_SUM, COL_SSQ, COL_MEAN, COL_MSQ = 132, 134, 136, 138
        COL_VAR, COL_TMP, COL_RSTD = 140, 142, 144

        def scl(col):
            return SCL[:, col:col + 1]

        for col, t in ((COL_S1, s1c_t), (COL_S2, s2c_t), (COL_SH1, sh1c_t),
                       (COL_SH2, sh2c_t)):
            S.emit("sy", lambda g, c0=col, tt=t: g.dma_start(
                out=SCL[:, c0:c0 + 32], in_=tt.ap()), inc=("PRO", 16))
        pro_scales = S.c("PRO")
        S.emit("ve", lambda g: g.memset(scl(COL_B005), 0.005), inc=("VE", 1))
        S.emit("ve", lambda g: g.memset(SCL[:, 146:147], 0.25), inc=("VE", 1))
        ve_warm = S.c("VE")
        for fn_ in (AF.Sqrt, AF.Silu, AF.Gelu, AF.Sqrt):
            for rep in range(4):
                S.emit("sc", lambda g, f=fn_: g.activation(
                    SCL[:, 147:148], SCL[:, 146:147], f),
                    waits={"VE": ve_warm}, inc=("SC", 1))
        S.emit("ve", lambda g: g.memset(scl(COL_EPS), 1e-5), inc=("VE", 1))
        S.emit("ve", lambda g: g.memset(scl(COL_VS), 1e-3), inc=("VE", 1))
        S.emit("ve", lambda g: g.memset(scl(COL_QS), 0.125), inc=("VE", 1))

        CH = 16384
        st_post = 0
        for src_t, dst_t, nel in ((w1t_t, w1b_d, 32 * 128 * 1024),
                                  (w2t_t, w2b_d, 32 * 128 * 1024),
                                  (h1t_t, h1b_d, 32 * 128 * 4096),
                                  (h2t_t, h2b_d, 32 * 128 * 4096)):
            per = nel // 128
            for c0 in range(0, per, CH):
                i_ap = bass.AP(src_t, c0, [[per, 128], [1, CH]])
                o_ap = bass.AP(dst_t, c0, [[per, 128], [1, CH]])
                ld_post = S.emit("gp", lambda g, i=i_ap: g.dma_start(
                    out=BH[:, 0:CH], in_=i), waits={"CAST": st_post},
                    inc=("CAST", 16))
                st_post = S.emit("gp", lambda g, o=o_ap: g.dma_start(
                    out=o, in_=BH[:, 0:CH]), waits={"CAST": ld_post},
                    inc=("CAST", 16))
        cast_total = S.c("CAST")
        for ic in range(8):
            S.emit("gp", lambda g, ic=ic: g.dma_start(
                out=BQ[:, ic * 2048:(ic + 1) * 2048],
                in_=W_attn[ic * 128:(ic + 1) * 128, 1024:3072]), inc=("WAT", 16))
        wat_kv = S.c("WAT")

        ln_ve, ln_sc, ln_store, ln_ssq = {}, {}, {}, {}

        def ln_tile(ti, i, src_ap, dst_d):
            par = ti % 2
            xs = BF[:, par * 1024:(par + 1) * 1024]
            xnb = BH[:, par * 1024:(par + 1) * 1024]
            sq = BF[:, 2048:3072]
            xsem = "X0" if par == 0 else "X1"
            waits = {}
            if ti >= 2:
                waits = {"VE": ln_ve[ti - 2], "SC": ln_sc[ti - 2]}
            if ti == 0:
                waits["CAST"] = cast_total
            post = S.emit("sy", lambda g, a=xs, s=src_ap, i=i: g.dma_start(
                out=a, in_=s[i * 128:(i + 1) * 128, :]), waits=waits, inc=(xsem, 16))
            S.emit("ve", lambda g, a=xs, c=par: g.reduce_sum(
                SCL[:, COL_SUM + c:COL_SUM + c + 1], a, axis=AX),
                waits={xsem: post}, inc=("VE", 1), tiny=True)
            sqw = {xsem: post}
            if ti >= 1:
                sqw["VE"] = ln_ssq[ti - 1]
            ln_sc[ti] = S.emit("sc", lambda g, a=xs, c=par: g.activation(
                sq, a, AF.Square),
                waits=sqw, inc=("SC", 1))
            ln_ssq[ti] = S.emit("ve", lambda g, c=par: g.reduce_sum(
                SCL[:, COL_SSQ + c:COL_SSQ + c + 1], sq, axis=AX),
                waits={"SC": ln_sc[ti]}, inc=("VE", 1), tiny=True)
            S.emit("ve", lambda g, c=par: g.tensor_scalar_mul(
                SCL[:, COL_MEAN + c:COL_MEAN + c + 1],
                SCL[:, COL_SUM + c:COL_SUM + c + 1], 1.0 / 1024.0), tiny=True)
            S.emit("ve", lambda g, c=par: g.tensor_mul(
                SCL[:, COL_MSQ + c:COL_MSQ + c + 1],
                SCL[:, COL_MEAN + c:COL_MEAN + c + 1],
                SCL[:, COL_MEAN + c:COL_MEAN + c + 1]), tiny=True)
            ve_var = S.emit("ve", lambda g, c=par: g.scalar_tensor_tensor(
                SCL[:, COL_VAR + c:COL_VAR + c + 1],
                SCL[:, COL_SSQ + c:COL_SSQ + c + 1], 1.0 / 1024.0,
                SCL[:, COL_MSQ + c:COL_MSQ + c + 1], ALU.mult, ALU.subtract),
                inc=("VE", 1), tiny=True)
            sc_sqrt = S.emit("sc", lambda g, c=par: g.activation(
                SCL[:, COL_TMP + c:COL_TMP + c + 1],
                SCL[:, COL_VAR + c:COL_VAR + c + 1], AF.Sqrt,
                bias=scl(COL_EPS)), waits={"VE": ve_var}, inc=("SC", 1))
            S.emit("ve", lambda g, c=par: g.reciprocal(
                SCL[:, COL_RSTD + c:COL_RSTD + c + 1],
                SCL[:, COL_TMP + c:COL_TMP + c + 1]), waits={"SC": sc_sqrt},
                tiny=True)
            st_stage = SCL[:, 148 - par * 0:148]
            ve_dump = S.emit("ve", lambda g, c=par: g.tensor_copy(
                BF[:, 3072 + 8 * (1 + c):3072 + 8 * (1 + c) + 7],
                bass.AP(SCL, 132 + c, [[152, 128], [2, 7]])), inc=("VE", 1),
                tiny=True)
            S.emit("gp", lambda g, ti=ti, c=par: g.dma_start(
                out=bass.AP(lnst_d, ti * 128 * 8, [[8, 128], [1, 7]]),
                in_=BF[:, 3072 + 8 * (1 + c):3072 + 8 * (1 + c) + 7]),
                waits={"VE": ve_dump}, inc=("CAST", 16))
            gsem = "GS0" if par == 0 else "GS1"
            w2 = {gsem: ln_store[ti - 2]} if ti >= 2 else {}
            ln_ve[ti] = S.emit("ve", lambda g, a=xs, o=xnb, c=par: g.tensor_scalar(
                o, a, SCL[:, COL_MEAN + c:COL_MEAN + c + 1],
                SCL[:, COL_RSTD + c:COL_RSTD + c + 1], ALU.subtract, ALU.mult),
                waits=w2, inc=("VE", 1), tiny=True)
            ln_store[ti] = S.emit("gp", lambda g, o=xnb, d=dst_d, i=i: g.dma_start(
                out=d.ap()[i * 128:(i + 1) * 128, :], in_=o),
                waits={"VE": ln_ve[ti]}, inc=(gsem, 16))

        for i in range(16):
            ln_tile(i, i, x_own, xno_d)
        ph1_gs = {"GS0": S.c("GS0"), "GS1": S.c("GS1")}
        for i in range(64):
            ln_tile(16 + i, i, x_batch, xn_d)
        ph2_gs = {"GS0": S.c("GS0"), "GS1": S.c("GS1")}
        S.mark(1)

        prev_use = {}
        kst_prev = {}
        vst_prev = {}
        hg_last_pe = {}
        for hg in range(8):
            par = hg % 2
            xsem = "XH0" if par == 0 else "XH1"
            w = dict(ph2_gs) if hg == 0 else {}
            if hg >= 2:
                w["PE"] = hg_last_pe[hg - 2]
            xh_post = None
            for ic in range(8):
                xh_post = S.emit("sy", lambda g, hg=hg, ic=ic, p=par: g.dma_start_transpose(
                    BH[:, p * 8192 + ic * 1024: p * 8192 + (ic + 1) * 1024],
                    xn_d.ap()[hg * 1024:(hg + 1) * 1024, ic * 128:(ic + 1) * 128]),
                    waits=(w if ic == 0 else {}), inc=(xsem, 16))
            for oc in range(8):
                b0, b1 = pb[(oc % 2) * 2], pb[(oc % 2) * 2 + 1]
                kk = ("k", oc % 2)
                mmw = {xsem: xh_post, "WAT": wat_kv}
                if kk in prev_use:
                    mmw["SC"] = prev_use[kk]
                pe_post = None
                for ic in range(8):
                    lhs = BQ[:, ic * 2048 + oc * 128: ic * 2048 + (oc + 1) * 128]
                    rhs0 = BH[:, par * 8192 + ic * 1024: par * 8192 + ic * 1024 + 512]
                    rhs1 = BH[:, par * 8192 + ic * 1024 + 512: par * 8192 + (ic + 1) * 1024]
                    S.emit("pe", lambda g, o=b0, l=lhs, r=rhs0, st=(ic == 0), sp=(ic == 7):
                           g.matmul(o[:], l, r, start=st, stop=sp),
                           waits=(mmw if ic == 0 else {}))
                    pe_post = S.emit("pe", lambda g, o=b1, l=lhs, r=rhs1, st=(ic == 0), sp=(ic == 7):
                                     g.matmul(o[:], l, r, start=st, stop=sp),
                                     inc=("PE", 1) if ic == 7 else None)
                kst = BH[:, 16384 + (oc % 2) * 1024: 16384 + (oc % 2) * 1024 + 1024]
                ksem = "KS0" if oc % 2 == 0 else "KS1"
                evw = {"PE": pe_post}
                if (oc % 2) in kst_prev:
                    evw[ksem] = kst_prev[oc % 2]
                S.emit("sc", lambda g, o=kst, s=b0: g.copy(o[:, 0:512], s[:]), waits=evw)
                prev_use[kk] = S.emit("sc", lambda g, o=kst, s=b1: g.copy(
                    o[:, 512:1024], s[:]), inc=("SC", 1))
                kst_prev[oc % 2] = S.emit("sc", lambda g, o=kst, hg=hg, oc=oc: g.dma_start(
                    out=kT_d.ap()[oc * 128:(oc + 1) * 128, hg * 1024:(hg + 1) * 1024],
                    in_=o), waits={"SC": prev_use[kk]}, inc=(ksem, 16))
            for tc in range(8):
                b0, b1 = pb[4 + (tc % 2) * 2], pb[4 + (tc % 2) * 2 + 1]
                kk = ("v", tc % 2)
                mmw = {}
                if kk in prev_use:
                    mmw["SC"] = prev_use[kk]
                pe_post = None
                for ic in range(8):
                    lhs = BH[:, par * 8192 + ic * 1024 + tc * 128: par * 8192 + ic * 1024 + (tc + 1) * 128]
                    rhs0 = BQ[:, ic * 2048 + 1024: ic * 2048 + 1536]
                    rhs1 = BQ[:, ic * 2048 + 1536: ic * 2048 + 2048]
                    S.emit("pe", lambda g, o=b0, l=lhs, r=rhs0, st=(ic == 0), sp=(ic == 7):
                           g.matmul(o[:], l, r, start=st, stop=sp),
                           waits=(mmw if ic == 0 else {}))
                    pe_post = S.emit("pe", lambda g, o=b1, l=lhs, r=rhs1, st=(ic == 0), sp=(ic == 7):
                                     g.matmul(o[:], l, r, start=st, stop=sp),
                                     inc=("PE", 1) if ic == 7 else None)
                vst = BH[:, 18432 + (tc % 2) * 1024: 18432 + (tc % 2) * 1024 + 1024]
                vsem = "VS0" if tc % 2 == 0 else "VS1"
                evw = {"PE": pe_post}
                if (tc % 2) in vst_prev:
                    evw[vsem] = vst_prev[tc % 2]
                S.emit("sc", lambda g, o=vst, s=b0: g.activation(
                    o[:, 0:512], s[:], AF.Copy, scale=scl(COL_VS)), waits=evw)
                prev_use[kk] = S.emit("sc", lambda g, o=vst, s=b1: g.activation(
                    o[:, 512:1024], s[:], AF.Copy, scale=scl(COL_VS)), inc=("SC", 1))
                vst_prev[tc % 2] = S.emit("sc", lambda g, o=vst, hg=hg, tc=tc: g.dma_start(
                    out=v_d.ap()[hg * 1024 + tc * 128: hg * 1024 + (tc + 1) * 128, :],
                    in_=o), waits={"SC": prev_use[kk]}, inc=(vsem, 16))
            hg_last_pe[hg] = S.c("PE")
        ph3_pe_total = S.c("PE")
        ph3_sc_total = S.c("SC")
        ph3_kv_st = {s: S.c(s) for s in ("KS0", "KS1", "VS0", "VS1")}
        S.mark(2)

        for ic in range(8):
            S.emit("gp", lambda g, ic=ic: g.dma_start(
                out=BH[:, 16384 + ic * 1024: 16384 + (ic + 1) * 1024],
                in_=W_attn[ic * 128:(ic + 1) * 128, 0:1024]),
                waits=(dict(ph3_kv_st) if ic == 0 else {}), inc=("WAT", 16))
        wat_q = S.c("WAT")
        for ic in range(8):
            S.emit("sy", lambda g, ic=ic: g.dma_start_transpose(
                BX[:, ic * 2048:(ic + 1) * 2048],
                xno_d.ap()[:, ic * 128:(ic + 1) * 128]),
                waits=(dict(ph1_gs) if ic == 0 else {}), inc=("XOT", 16))
        xot_total = S.c("XOT")
        qevac = {}
        for oc in range(8):
            banks = [pb[0], pb[1], pb[2], pb[3]] if oc % 2 == 0 else [pb[4], pb[5], pb[6], pb[7]]
            mmw = {"XOT": xot_total, "WAT": wat_q,
                   "SC": qevac.get(oc - 2, ph3_sc_total)}
            pe_post = None
            for ic in range(8):
                lhs = BH[:, 16384 + ic * 1024 + oc * 128: 16384 + ic * 1024 + (oc + 1) * 128]
                for j in range(4):
                    rhs = BX[:, ic * 2048 + j * 512: ic * 2048 + (j + 1) * 512]
                    pe_post = S.emit("pe", lambda g, o=banks[j], l=lhs, r=rhs,
                                     st=(ic == 0), sp=(ic == 7): g.matmul(
                                         o[:], l, r, start=st, stop=sp),
                                     waits=(mmw if (ic == 0 and j == 0) else {}),
                                     inc=("PE", 1) if (ic == 7 and j == 3) else None)
            for j in range(4):
                qevac[oc] = S.emit("sc", lambda g, oc=oc, j=j, s=banks[j]: g.activation(
                    BQ[:, oc * 2048 + j * 512: oc * 2048 + (j + 1) * 512],
                    s[:], AF.Copy, scale=scl(COL_QS)),
                    waits={"PE": pe_post}, inc=("SC", 1))
        ph35_sc_total = S.c("SC")
        S.mark(3)

        pm = {}
        kv_prev = {}
        mask_prev = {}
        yev = {0: ph35_sc_total, 1: ph35_sc_total}
        yst_prev = {}
        first_kv = True
        pe_av = None
        for s in range(_KP4S):
            mpar = s % 2
            msem = "MK0" if mpar == 0 else "MK1"
            mw = {}
            if s == 0:
                mw["PE"] = ph3_pe_total
            if mpar in mask_prev:
                mw["VE"] = mask_prev[mpar]
            for j in range(16):
                S.emit("sy", lambda g, s=s, j=j, mp=mpar: g.dma_start(
                    out=BH[:, mp * 8192 + j * 512: mp * 8192 + (j + 1) * 512],
                    in_=bass.AP(masks_t, (s * 16 + j) * 128 * 512,
                                [[512, 128], [1, 512]])),
                    waits=(mw if j == 0 else {}), inc=(msem, 16))
            mask_total = S.c(msem)
            nkb = ESCH[s]
            for q in range(_KP4Q):
                kv_post = {}
                for kb in range(nkb):
                    sup, jj = kb // 4, kb % 4
                    spar = sup % 2
                    ksem = "KV0" if spar == 0 else "KV1"
                    if jj == 0:
                        lw = {}
                        if first_kv:
                            lw.update(ph3_kv_st)
                            first_kv = False
                        if spar in kv_prev:
                            lw["PE"] = kv_prev[spar]
                        for hpc in range(2):
                            S.emit("sy", lambda g, q=q, hpc=hpc, sup=sup, p=spar: g.dma_start(
                                out=BXW[:, p * 1024 + hpc * 512: p * 1024 + (hpc + 1) * 512],
                                in_=kT_d.ap()[(q * 2 + hpc) * 128:(q * 2 + hpc + 1) * 128,
                                              sup * 512:(sup + 1) * 512]),
                                waits=(lw if hpc == 0 else {}), inc=(ksem, 16))
                        for vj in range(4):
                            kv_post[sup] = S.emit("sy", lambda g, q=q, vj=vj, sup=sup, p=spar: g.dma_start(
                                out=BXW[:, 2048 + p * 1024 + vj * 256: 2048 + p * 1024 + (vj + 1) * 256],
                                in_=v_d.ap()[sup * 512 + vj * 128: sup * 512 + (vj + 1) * 128,
                                             q * 256:(q + 1) * 256]), inc=(ksem, 16))
                    for hp in range(2):
                        pp = (kb * 2 + hp) % 2
                        bA, bB = pb[pp * 2], pb[pp * 2 + 1]
                        qoff = (q * 2 + hp) * 2048 + s * 512
                        koff = spar * 1024 + hp * 512 + jj * 128
                        if _KP4L < 1:
                            continue
                        qkw = {ksem: kv_post[sup]}
                        if pp in pm:
                            qkw["SC"] = pm[pp].get("sc", 0)
                            qkw["VE"] = pm[pp].get("ve3", 0)
                        else:
                            qkw["SC"] = ph35_sc_total
                        qkw = {k: v for k, v in qkw.items() if v}
                        S.emit("pe", lambda g, o=bA, ko=koff, qo=qoff: g.matmul(
                            o[:], BXW[0:64, ko:ko + 128], BQ[0:64, qo:qo + 512],
                            start=True, stop=True, tile_position=(0, 0)), waits=qkw)
                        pe_qk = S.emit("pe", lambda g, o=bB, ko=koff, qo=qoff: g.matmul(
                            o[:], BXW[64:128, ko:ko + 128], BQ[64:128, qo:qo + 512],
                            start=True, stop=True, tile_position=(64, 0)), inc=("PE", 1))
                        RA, RB = BF[:, 0:512], BF[:, 512:1024]
                        qA = BF[:, 1024 + pp * 1024: 1536 + pp * 1024]
                        qB = BF[:, 1536 + pp * 1024: 2048 + pp * 1024]
                        tA = BF[:, 3072 + pp * 1024: 3584 + pp * 1024]
                        tB = BF[:, 3584 + pp * 1024: 4096 + pp * 1024]
                        wA = BF[:, 5120 + pp * 1024: 5632 + pp * 1024]
                        wB = BF[:, 5632 + pp * 1024: 6144 + pp * 1024]
                        AA = BH[:, 16384 + (pp * 2) * 512: 16384 + (pp * 2 + 1) * 512]
                        AB = BH[:, 16384 + (pp * 2 + 1) * 512: 16384 + (pp * 2 + 2) * 512]
                        if _KP4L < 2:
                            pm[pp] = {"pe_qk": pe_qk}
                            kv_prev[spar] = pe_qk
                            pe_av = pe_qk
                            continue
                        scw = {"PE": pe_qk}
                        if pp in pm:
                            scw["VE"] = pm[pp].get("ve5", 0)
                        scw = {k: v for k, v in scw.items() if v}
                        S.emit("sc", lambda g, o=RA, i=bA: g.activation(o, i[:], AF.Relu),
                               waits=scw)
                        S.emit("sc", lambda g, o=RB, i=bB: g.activation(o, i[:], AF.Relu))
                        S.emit("sc", lambda g, o=qA, i=RA: g.activation(
                            o, i, AF.Square, bias=scl(COL_B005)))
                        sc_p = S.emit("sc", lambda g, o=qB, i=RB: g.activation(
                            o, i, AF.Square, bias=scl(COL_B005)), inc=("SC", 1))
                        if _KP4L < 3:
                            pm[pp] = {"sc": sc_p}
                            kv_prev[spar] = pe_qk
                            pe_av = pe_qk
                            continue
                        S.emit("ve", lambda g, o=tA, i=bA: g.tensor_scalar(
                            o, i[:], -0.01, 0.999975, ALU.mult, ALU.add),
                            waits={"PE": pe_qk, "SC": sc_p})
                        ve3 = S.emit("ve", lambda g, o=tB, i=bB: g.tensor_scalar(
                            o, i[:], -0.01, 0.999975, ALU.mult, ALU.add), inc=("VE", 1))
                        if _KP4L < 4:
                            pm[pp] = {"sc": sc_p, "ve3": ve3}
                            kv_prev[spar] = pe_qk
                            pe_av = pe_qk
                            continue
                        vew = {"SC": sc_p}
                        if pp in pm:
                            vew["PE"] = pm[pp].get("pe_av", 0)
                        vew = {k: v for k, v in vew.items() if v}
                        S.emit("ve", lambda g, o=wA, a=qA, b=tA: g.tensor_tensor(
                            o, a, b, ALU.add), waits=vew)
                        S.emit("ve", lambda g, o=wB, a=qB, b=tB: g.tensor_tensor(
                            o, a, b, ALU.add))
                        S.emit("ve", lambda g, o=AA, t=tA, w=wA: g.scalar_tensor_tensor(
                            o, t, 1.999975, w, ALU.is_le, ALU.mult))
                        ve5 = S.emit("ve", lambda g, o=AB, t=tB, w=wB: g.scalar_tensor_tensor(
                            o, t, 1.999975, w, ALU.is_le, ALU.mult), inc=("VE", 1))
                        if kb >= 16 * s and _KP4L >= 5:
                            moff = mpar * 8192 + (kb - 16 * s) * 512
                            S.emit("ve", lambda g, o=AA, mo=moff: g.tensor_tensor(
                                o, o, BH[:, mo:mo + 512], ALU.mult),
                                waits={msem: mask_total})
                            ve5 = S.emit("ve", lambda g, o=AB, mo=moff: g.tensor_tensor(
                                o, o, BH[:, mo:mo + 512], ALU.mult), inc=("VE", 1))
                            mask_prev[mpar] = ve5
                        if _KP4L < 6:
                            pm[pp] = {"sc": sc_p, "ve3": ve3, "ve5": ve5}
                            kv_prev[spar] = pe_qk
                            pe_av = pe_qk
                            continue
                        voff = 2048 + spar * 1024 + jj * 256 + hp * 128
                        yb = pb[4 + hp]
                        avw = {"VE": ve5}
                        if kb == 0:
                            avw["SC"] = yev[hp]
                        S.emit("pe", lambda g, o=yb, vo=voff, a=AA, st=(kb == 0),
                               sp=(kb == nkb - 1): g.matmul(
                                   o[0:64, :], BXW[:, vo:vo + 64], a,
                                   start=st, stop=sp, tile_position=(0, 0)), waits=avw)
                        pe_av = S.emit("pe", lambda g, o=yb, vo=voff, a=AB,
                                       st=(kb == 0), sp=(kb == nkb - 1): g.matmul(
                                           o[64:128, :], BXW[:, vo + 64:vo + 128], a,
                                           start=st, stop=sp, tile_position=(0, 64),
                                           skip_group_check=True),
                                       inc=("PE", 1))
                        pm[pp] = {"sc": sc_p, "ve3": ve3, "ve5": ve5, "pe_av": pe_av}
                        kv_prev[spar] = pe_av
                for hp in range(2 if _KP4L >= 6 else 0):
                    yst = BH[:, 18432 + hp * 512: 18432 + (hp + 1) * 512]
                    ysem2 = "YS0" if hp == 0 else "YS1"
                    evw = {"PE": pe_av}
                    if hp in yst_prev:
                        evw[ysem2] = yst_prev[hp]
                    yev[hp] = S.emit("sc", lambda g, o=yst, s=pb[4 + hp]: g.copy(
                        o, s[:]), waits=evw, inc=("SC", 1))
                    yst_prev[hp] = S.emit("sc", lambda g, o=yst, q=q, hp=hp, s=s: g.dma_start(
                        out=y_d.ap()[(q * 2 + hp) * 128:(q * 2 + hp + 1) * 128,
                                     s * 512:(s + 1) * 512], in_=o),
                        waits={"SC": yev[hp]}, inc=(ysem2, 16))
        ph4_y_st = {"YS0": S.c("YS0"), "YS1": S.c("YS1")}
        ph4_pe_total = S.c("PE")
        ph4_sc_total = S.c("SC")
        S.mark(4)

        w_item = 0
        ws_prev = {}
        ffn_gate = {}
        h2_last_pe = ph4_pe_total
        h2o_sst = {}
        for sp in range(NSLOT):
            for rep in range(4):
                S.emit("sc", lambda g: g.activation(
                    SCL[:, 147:148], SCL[:, 146:147], AF.Silu), inc=("SC", 1))
            for oc in range(32):
                par = w_item % 2
                wsem = "WS0" if par == 0 else "WS1"
                lw = {}
                if sp == 0 and oc < 2:
                    lw["CAST"] = cast_total
                    lw["PE"] = ph4_pe_total
                if par in ws_prev:
                    lw["PE"] = max(lw.get("PE", 0), ws_prev[par])
                S.emit("sy", lambda g, oc=oc, p=par: g.dma_start(
                    out=BXW[:, p * 2048: p * 2048 + 1024],
                    in_=bass.AP(w1b_d, oc * 128 * 1024, [[1024, 128], [1, 1024]])),
                    waits=lw, inc=(wsem, 16))
                ws_post = S.emit("sy", lambda g, oc=oc, p=par: g.dma_start(
                    out=BXW[:, p * 2048 + 1024: p * 2048 + 2048],
                    in_=bass.AP(w2b_d, oc * 128 * 1024, [[1024, 128], [1, 1024]])),
                    inc=(wsem, 16))
                b1, b2 = pb[oc % 2], pb[2 + oc % 2]
                mmw = {wsem: ws_post, "PE": h2_last_pe}
                gk = ("w12", oc % 2)
                if gk in ffn_gate:
                    mmw["SC"] = ffn_gate[gk]["sc"]
                    mmw["VE"] = ffn_gate[gk]["ve"]
                elif sp == 0:
                    mmw["SC"] = ph4_sc_total
                for ic in range(8):
                    rhs = BX[:, ic * 2048 + sp * 512: ic * 2048 + (sp + 1) * 512]
                    S.emit("pe", lambda g, o=b1, p=par, ic=ic, r=rhs, st=(ic == 0),
                           sp2=(ic == 7): g.matmul(
                               o[:], BXW[:, p * 2048 + ic * 128: p * 2048 + (ic + 1) * 128],
                               r, start=st, stop=sp2),
                           waits=(mmw if ic == 0 else {}),
                           inc=("PE", 1) if ic == 7 else None)
                pe_w1 = S.c("PE")
                for ic in range(8):
                    rhs = BX[:, ic * 2048 + sp * 512: ic * 2048 + (sp + 1) * 512]
                    S.emit("pe", lambda g, o=b2, p=par, ic=ic, r=rhs, st=(ic == 0),
                           sp2=(ic == 7): g.matmul(
                               o[:], BXW[:, p * 2048 + 1024 + ic * 128: p * 2048 + 1024 + (ic + 1) * 128],
                               r, start=st, stop=sp2),
                           inc=("PE", 1) if ic == 7 else None)
                pe_w2 = S.c("PE")
                ws_prev[par] = pe_w2
                silu = BF[:, (oc % 2) * 512:(oc % 2) * 512 + 512]
                sc_post = S.emit("sc", lambda g, o=silu, s=b1, oc=oc: g.activation(
                    o, s[:], AF.Silu, scale=SCL[:, COL_S1 + oc:COL_S1 + oc + 1]),
                    waits={"PE": pe_w1, "PRO": pro_scales}, inc=("SC", 1))
                ve_post = S.emit("ve", lambda g, s=b2, si=silu, oc=oc: g.scalar_tensor_tensor(
                    BH[:, oc * 512:(oc + 1) * 512], s[:],
                    SCL[:, COL_S2 + oc:COL_S2 + oc + 1], si, ALU.mult, ALU.mult),
                    waits={"PE": pe_w2, "SC": sc_post}, inc=("VE", 1))
                ffn_gate[gk] = {"sc": sc_post, "ve": ve_post}
                w_item += 1
            h_ready = S.c("VE")
            for rep in range(4):
                S.emit("sc", lambda g: g.activation(
                    SCL[:, 147:148], SCL[:, 146:147], AF.Gelu), inc=("SC", 1))
            for oc2 in range(32):
                halves = []
                for half in range(2):
                    par = w_item % 2
                    wsem = "WS0" if par == 0 else "WS1"
                    lw = {}
                    if par in ws_prev:
                        lw["PE"] = ws_prev[par]
                    hp_post = S.emit("sy", lambda g, oc2=oc2, half=half, p=par: g.dma_start(
                        out=BXW[:, p * 2048:(p + 1) * 2048],
                        in_=bass.AP(h1b_d, oc2 * 128 * 4096 + half * 2048,
                                    [[4096, 128], [1, 2048]])),
                        waits=lw, inc=(wsem, 16))
                    halves.append((par, wsem, hp_post))
                    w_item += 1
                bk = pb[4 + oc2 % 2]
                gk = ("h1", oc2 % 2)
                for hc in range(32):
                    par, wsem, hp_post = halves[hc // 16]
                    mmw = {}
                    if hc in (0, 16):
                        mmw[wsem] = hp_post
                        if hc == 0:
                            mmw["VE"] = h_ready
                            mmw["SC"] = ffn_gate.get(gk, ph4_sc_total)
                    rhs = BH[:, hc * 512:(hc + 1) * 512]
                    S.emit("pe", lambda g, o=bk, p=par, hc=hc, r=rhs, st=(hc == 0),
                           sp2=(hc == 31): g.matmul(
                               o[:], BXW[:, p * 2048 + (hc % 16) * 128: p * 2048 + ((hc % 16) + 1) * 128],
                               r, start=st, stop=sp2), waits=mmw,
                           inc=("PE", 1) if hc == 31 else None)
                pe_h1 = S.c("PE")
                ws_prev[halves[0][0]] = pe_h1
                ws_prev[halves[1][0]] = pe_h1
                ffn_gate[gk] = S.emit("sc", lambda g, oc2=oc2, s=bk: g.activation(
                    BH[:, 16384 + oc2 * 512: 16384 + (oc2 + 1) * 512], s[:], AF.Gelu,
                    scale=SCL[:, COL_SH1 + oc2:COL_SH1 + oc2 + 1]),
                    waits={"PE": pe_h1}, inc=("SC", 1))
            g_ready = S.c("SC")
            for oc2 in range(32):
                halves = []
                for half in range(2):
                    par = w_item % 2
                    wsem = "WS0" if par == 0 else "WS1"
                    lw = {}
                    if par in ws_prev:
                        lw["PE"] = ws_prev[par]
                    hp_post = S.emit("sy", lambda g, oc2=oc2, half=half, p=par: g.dma_start(
                        out=BXW[:, p * 2048:(p + 1) * 2048],
                        in_=bass.AP(h2b_d, oc2 * 128 * 4096 + half * 2048,
                                    [[4096, 128], [1, 2048]])),
                        waits=lw, inc=(wsem, 16))
                    halves.append((par, wsem, hp_post))
                    w_item += 1
                bk = pb[6 + oc2 % 2]
                gk = ("h2", oc2 % 2)
                for hc in range(32):
                    par, wsem, hp_post = halves[hc // 16]
                    mmw = {}
                    if hc in (0, 16):
                        mmw[wsem] = hp_post
                        if hc == 0:
                            mmw["SC"] = max(g_ready, ffn_gate.get(gk, 0))
                    rhs = BH[:, 16384 + hc * 512: 16384 + (hc + 1) * 512]
                    S.emit("pe", lambda g, o=bk, p=par, hc=hc, r=rhs, st=(hc == 0),
                           sp2=(hc == 31): g.matmul(
                               o[:], BXW[:, p * 2048 + (hc % 16) * 128: p * 2048 + ((hc % 16) + 1) * 128],
                               r, start=st, stop=sp2), waits=mmw,
                           inc=("PE", 1) if hc == 31 else None)
                pe_h2 = S.c("PE")
                ws_prev[halves[0][0]] = pe_h2
                ws_prev[halves[1][0]] = pe_h2
                h2stg = P6Y[:, (oc2 % 2) * 512:(oc2 % 2 + 1) * 512]
                hsem2 = "HS0" if oc2 % 2 == 0 else "HS1"
                evw = {"PE": pe_h2}
                if (oc2 % 2) in h2o_sst:
                    evw[hsem2] = h2o_sst[oc2 % 2]
                ffn_gate[gk] = S.emit("sc", lambda g, o=h2stg, s=bk, oc2=oc2: g.activation(
                    o, s[:], AF.Gelu, scale=SCL[:, COL_SH2 + oc2:COL_SH2 + oc2 + 1]),
                    waits=evw, inc=("SC", 1))
                h2o_sst[oc2 % 2] = S.emit("sc", lambda g, o=h2stg, oc2=oc2, sp=sp: g.dma_start(
                    out=h2o_d.ap()[oc2 * 128:(oc2 + 1) * 128, sp * 512:(sp + 1) * 512],
                    in_=o), waits={"SC": ffn_gate[gk]}, inc=(hsem2, 16))
            h2_last_pe = S.c("PE")
        ph5_h_st = {"HS0": S.c("HS0"), "HS1": S.c("HS1")}
        ph5_pe_total = S.c("PE")
        ph5_sc_total = S.c("SC")
        ph5_ve_total = S.c("VE")
        S.mark(5)

        for cc in range(8):
            S.emit("gp", lambda g, cc=cc: g.dma_start(
                out=BQ[:, cc * 1024:(cc + 1) * 1024],
                in_=W_proj[cc * 128:(cc + 1) * 128, :]),
                waits=({"PE": ph5_pe_total} if cc == 0 else {}), inc=("WAT", 16))
        wat_total = S.c("WAT")
        S.emit("sy", lambda g: g.dma_start(out=BF[:, 2048:3072], in_=s3bc_t.ap()),
               waits={"PE": ph5_pe_total, "VE": ph5_ve_total}, inc=("PRO", 16))
        pro_total = S.c("PRO")
        w3cv = {}
        for hc in range(32):
            par = hc % 2
            wsem = "W30" if par == 0 else "W31"
            lw = {}
            if hc < 2:
                lw["PE"] = ph5_pe_total
                lw["VE"] = ph5_ve_total
            else:
                lw["VE"] = w3cv[hc - 2]
            w3_post = S.emit("sy", lambda g, hc=hc, p=par: g.dma_start(
                out=BF[:, p * 1024:(p + 1) * 1024],
                in_=bass.AP(w3t_t, hc * 128 * 1024, [[1024, 128], [1, 1024]])),
                waits=lw, inc=(wsem, 16))
            cvw = {wsem: w3_post, "PRO": pro_total}
            if hc < 2:
                cvw["PE"] = ph5_pe_total
            w3cv[hc] = S.emit("ve", lambda g, hc=hc, p=par: g.tensor_tensor(
                BH[:, hc * 1024:(hc + 1) * 1024], BF[:, p * 1024:(p + 1) * 1024],
                BF[:, 2048:3072], ALU.mult), waits=cvw, inc=("VE", 1))
        w3_ready = S.c("VE")

        h2l_prev = {}
        yg_prev = {}
        xt_prev = {}
        outstore = {}
        bank_gate = {}
        for grp in range(4):
            h2_posts = []
            for half in range(2):
                hsem = "H20" if half == 0 else "H21"
                lw = {}
                if grp == 0:
                    lw.update(ph5_h_st)
                if (grp - 1, half) in h2l_prev:
                    lw["PE"] = h2l_prev[(grp - 1, half)]
                hpost = None
                for part in range(2):
                    hcb = half * 16 + part * 8
                    i_ap = bass.AP(h2o_d, hcb * 128 * OWN + grp * 512,
                                   [[OWN, 128], [128 * OWN, 8], [1, 512]])
                    hpost = S.emit("sy", lambda g, i=i_ap, half=half, part=part: g.dma_start(
                        out=BX[:, half * 8192 + part * 4096: half * 8192 + (part + 1) * 4096],
                        in_=i), waits=(lw if part == 0 else {}), inc=(hsem, 16))
                h2_posts.append((hsem, hpost))
            ygpar = grp % 2
            ysem = "YG0" if ygpar == 0 else "YG1"
            lw = {}
            if grp == 0:
                lw.update(ph4_y_st)
            if (grp - 2, ygpar) in yg_prev:
                lw["PE"] = yg_prev[(grp - 2, ygpar)]
            y_post = None
            for part in range(2):
                i_ap = bass.AP(y_d, (part * 4) * 128 * OWN + grp * 512,
                               [[OWN, 128], [128 * OWN, 4], [1, 512]])
                y_post = S.emit("sy", lambda g, i=i_ap, p=ygpar, part=part: g.dma_start(
                    out=BQ[:, 8192 + p * 4096 + part * 2048: 8192 + p * 4096 + (part + 1) * 2048],
                    in_=i), waits=(lw if part == 0 else {}), inc=(ysem, 16))
            for half in range(2):
                hsem, hpost = h2_posts[half]
                for tc4 in range(4):
                    for oh in range(2):
                        bk = pb[tc4 * 2 + oh]
                        for hcl in range(16):
                            hc = half * 16 + hcl
                            mmw = {}
                            if hcl == 0:
                                mmw[hsem] = hpost
                                if half == 0:
                                    mmw["VE"] = max(w3_ready,
                                                    bank_gate.get((grp - 1, tc4, oh), 0))
                                    if grp == 0:
                                        mmw["SC"] = ph5_sc_total
                            lhs_off = half * 8192 + (hcl // 8) * 4096 + (hcl % 8) * 512 + tc4 * 128
                            rhs = BH[:, hc * 1024 + oh * 512: hc * 1024 + (oh + 1) * 512]
                            S.emit("pe", lambda g, o=bk, lo=lhs_off, r=rhs,
                                   st=(hc == 0): g.matmul(
                                       o[:], BX[:, lo:lo + 128], r,
                                       start=st, stop=False), waits=mmw)
            for tc4 in range(4):
                tc = grp * 4 + tc4
                xpar = tc % 2
                xsem = "XT0" if xpar == 0 else "XT1"
                lw = {}
                if tc < 2:
                    lw["PE"] = ph5_pe_total
                if (tc - 2) in xt_prev:
                    lw["VE"] = xt_prev[tc - 2]
                xpost = S.emit("sy", lambda g, tc=tc, p=xpar: g.dma_start(
                    out=BF[:, 3072 + p * 1024: 3072 + (p + 1) * 1024],
                    in_=x_own[tc * 128:(tc + 1) * 128, :]), waits=lw, inc=(xsem, 16))
                for oh in range(2):
                    bk = pb[tc4 * 2 + oh]
                    pe_fin = None
                    for cc in range(8):
                        mmw = {}
                        if cc == 0:
                            mmw = {ysem: y_post, "WAT": wat_total}
                        lhs_off = 8192 + ygpar * 4096 + (cc // 4) * 2048 + (cc % 4) * 512 + tc4 * 128
                        rhs = BQ[:, cc * 1024 + oh * 512: cc * 1024 + (oh + 1) * 512]
                        pe_fin = S.emit("pe", lambda g, o=bk, lo=lhs_off, r=rhs,
                                        sp2=(cc == 7): g.matmul(
                                            o[:], BQ[:, lo:lo + 128], r,
                                            start=False, stop=sp2), waits=mmw,
                                        inc=("PE", 1) if cc == 7 else None)
                    yg_prev[(grp, ygpar)] = pe_fin
                    h2l_prev[(grp, 0)] = pe_fin
                    h2l_prev[(grp, 1)] = pe_fin
                    obuf = BF[:, 5120 + (tc % 2) * 1024 + oh * 512: 5120 + (tc % 2) * 1024 + (oh + 1) * 512]
                    xsl = BF[:, 3072 + (tc % 2) * 1024 + oh * 512: 3072 + (tc % 2) * 1024 + (oh + 1) * 512]
                    osem = "OS0" if tc % 2 == 0 else "OS1"
                    evw = {"PE": pe_fin, xsem: xpost}
                    if (tc - 2) in outstore:
                        evw[osem] = outstore[tc - 2]
                    ov = S.emit("ve", lambda g, o=obuf, s=bk, x=xsl: g.tensor_add(
                        o, s[:], x), waits=evw, inc=("VE", 1))
                    bank_gate[(grp, tc4, oh)] = ov
                    xt_prev[tc] = ov
                tc = grp * 4 + tc4
                outstore[tc] = S.emit("gp", lambda g, tc=tc: g.dma_start(
                    out=out[tc * 128:(tc + 1) * 128, :],
                    in_=BF[:, 5120 + (tc % 2) * 1024: 5120 + (tc % 2) * 1024 + 1024]),
                    waits={"VE": xt_prev[tc]},
                    inc=("OS0" if tc % 2 == 0 else "OS1", 16))
        S.mark(6)
        if _KPH < 6 and _KPH in S.marks:
            S.recs = S.recs[:S.marks[_KPH]]
        tot = collections.defaultdict(int)
        for (_en, _w, _fn, _inc, _t) in S.recs:
            if _inc:
                tot[_inc[0]] += _inc[1]
        fin = {k: tot[k] for k in ("GS0", "GS1", "KS0", "KS1", "VS0", "VS1",
                                   "YS0", "YS1", "HS0", "HS1", "OS0", "OS1")
               if tot[k] > 0}
        S.recs.append(("gp", fin, None, None, False))

        for k, v in S.cnt.items():
            assert v < 60000, (k, v)

        ksafe_engs = os.environ.get("KSAFE", "ve").replace(",", " ").split()
        selfsem_all = {"pe": "ZPE", "sc": "ZSC", "ve": "ZVE"}
        selfsem = {k: v for k, v in selfsem_all.items() if k in ksafe_engs}

        with nc.Block() as block:
            def replay(eng_name):
                last = {}
                zs = selfsem.get(eng_name)

                def run(g):
                    cnt = collections.defaultdict(int)
                    zn = 0
                    prev = None
                    prev_tiny = False
                    for (en, waits, fn, inc, tiny) in S.recs:
                        if en != eng_name:
                            if inc is not None:
                                cnt[inc[0]] += inc[1]
                            continue
                        for sname, val in waits.items():
                            if last.get(sname, -1) < val:
                                g.wait_ge(sem[sname], val)
                                last[sname] = val
                        if fn is None:
                            continue
                        if (zs is not None and prev is not None
                                and (tiny or prev_tiny)):
                            if last.get(prev[0], -1) < prev[1]:
                                g.wait_ge(sem[prev[0]], prev[1])
                                last[prev[0]] = prev[1]
                        inst = fn(g)
                        if inc is not None:
                            cnt[inc[0]] += inc[1]
                            inst.then_inc(sem[inc[0]], inc[1])
                            prev = (inc[0], cnt[inc[0]]) if inc[1] == 1 else None
                        elif zs is not None:
                            zn += 1
                            inst.then_inc(sem[zs], 1)
                            prev = (zs, zn)
                        else:
                            prev = None
                        prev_tiny = tiny
                return run

            block.tensor(replay("pe"))
            block.scalar(replay("sc"))
            block.vector(replay("ve"))
            block.sync(replay("sy"))
            block.gpsimd(replay("gp"))

    return nc


_NC_CACHE = None


def _get_nc():
    global _NC_CACHE
    if _NC_CACHE is None:
        _NC_CACHE = build_nc()
    return _NC_CACHE


def kernel(L0_attn_x, L0_attn_W_attn, L0_attn_W_proj,
           L0_ffn_w1_w, L0_ffn_w1_s, L0_ffn_w2_w, L0_ffn_w2_s,
           L0_ffn_h1_w, L0_ffn_h1_s, L0_ffn_h2_w, L0_ffn_h2_s,
           L0_ffn_w3_w, L0_ffn_w3_s):
    x = np.ascontiguousarray(np.asarray(L0_attn_x, dtype=np.float32))
    W_attn = np.ascontiguousarray(np.asarray(L0_attn_W_attn, dtype=np.float32))
    W_proj = np.ascontiguousarray(np.asarray(L0_attn_W_proj, dtype=np.float32))
    w1 = np.asarray(L0_ffn_w1_w, dtype=np.float32)
    w2 = np.asarray(L0_ffn_w2_w, dtype=np.float32)
    h1 = np.asarray(L0_ffn_h1_w, dtype=np.float32)
    h2 = np.asarray(L0_ffn_h2_w, dtype=np.float32)
    w3 = np.asarray(L0_ffn_w3_w, dtype=np.float32)
    s1 = np.asarray(L0_ffn_w1_s, dtype=np.float32)
    s2 = np.asarray(L0_ffn_w2_s, dtype=np.float32)
    sh1 = np.asarray(L0_ffn_h1_s, dtype=np.float32)
    sh2 = np.asarray(L0_ffn_h2_s, dtype=np.float32)
    s3 = np.asarray(L0_ffn_w3_s, dtype=np.float32)

    w1t = np.ascontiguousarray(
        w1.T.reshape(8, 128, 32, 128).transpose(2, 1, 0, 3).reshape(32, 128, 1024))
    w2t = np.ascontiguousarray(
        w2.T.reshape(8, 128, 32, 128).transpose(2, 1, 0, 3).reshape(32, 128, 1024))
    h1t = np.ascontiguousarray(
        h1.T.reshape(32, 128, 32, 128).transpose(2, 1, 0, 3).reshape(32, 128, 4096))
    h2t = np.ascontiguousarray(
        h2.T.reshape(32, 128, 32, 128).transpose(2, 1, 0, 3).reshape(32, 128, 4096))
    w3t = np.ascontiguousarray(w3.T.reshape(32, 128, 1024))
    s1c = np.ascontiguousarray(s1.reshape(32, 128).T)
    s2c = np.ascontiguousarray(s2.reshape(32, 128).T)
    sh1c = np.ascontiguousarray(sh1.reshape(32, 128).T)
    sh2c = np.ascontiguousarray(sh2.reshape(32, 128).T)
    s3bc = np.ascontiguousarray(np.broadcast_to(s3, (128, 1024)))

    shared = dict(W_attn=W_attn, W_proj=W_proj, w1t=w1t, w2t=w2t,
                  h1t=h1t, h2t=h2t, w3t=w3t, s1c=s1c, s2c=s2c,
                  sh1c=sh1c, sh2c=sh2c, s3bc=s3bc)

    in_maps = []
    core_spans = []
    p_idx = np.arange(128)[:, None]
    f_idx = np.arange(512)[None, :]
    for c in range(8):
        g = c // 2 + 1
        b = c % 2
        spans = spans_for_class(g)
        core_spans.append((b, spans))
        x_own = np.ascontiguousarray(
            np.concatenate([x[b, sp * SPAN:(sp + 1) * SPAN] for sp in spans], axis=0))
        m = np.zeros((NSLOT, 16, 128, 512), dtype=np.float32)
        for s in range(NSLOT):
            for j in range(16):
                kb = 16 * s + j
                m[s, j] = ((kb * 128 + p_idx) <= (spans[s] * 512 + f_idx))
        in_maps.append(dict(shared, x_own=x_own,
                            x_batch=np.ascontiguousarray(x[b]),
                            masks=m.astype(ml_dtypes.bfloat16)))

    nc = _get_nc()
    res = run_bass_kernel_spmd(nc, in_maps, core_ids=list(range(8)))

    out_full = np.empty((2, T, C), dtype=np.float32)
    for c in range(8):
        b, spans = core_spans[c]
        o = res.results[c]["out"]
        for si, sp in enumerate(spans):
            out_full[b, sp * SPAN:(sp + 1) * SPAN] = o[si * SPAN:(si + 1) * SPAN]
    return out_full


# version 11 (final)
# speedup vs baseline: 1.7280x; 1.0225x over previous
import collections
import os
from contextlib import ExitStack

_KPH = int(os.environ.get("KPH", "9"))
_KP4S = int(os.environ.get("KP4S", "4"))
_KP4Q = int(os.environ.get("KP4Q", "4"))
_KP4L = int(os.environ.get("KP4L", "6"))

import numpy as np
import ml_dtypes

import concourse.bass as bass
import concourse.mybir as mybir
from concourse.bass_utils import run_bass_kernel_spmd

F32 = mybir.dt.float32
BF16 = mybir.dt.bfloat16
AF = mybir.ActivationFunctionType
ALU = mybir.AluOpType
AX = mybir.AxisListType.X

B, T, C = 2, 8192, 1024
DH = 4096
SPAN = 512
NSLOT = 4
OWN = NSLOT * SPAN
ESCH = [16, 32, 48, 64]


def spans_for_class(g):
    return [g - 1, 8 - g, 7 + g, 16 - g]


class Sched:
    ENGS = ("pe", "sc", "ve", "sy", "gp")

    def __init__(self):
        self.recs = []
        self.cnt = collections.defaultdict(int)
        self.marks = {}

    def mark(self, ph):
        self.marks[ph] = len(self.recs)

    def emit(self, eng, fn, waits=None, inc=None, tiny=False):
        assert eng in self.ENGS
        self.recs.append((eng, dict(waits or {}), fn, inc, tiny))
        if inc is not None:
            self.cnt[inc[0]] += inc[1]
            return self.cnt[inc[0]]
        return None

    def c(self, sem):
        return self.cnt[sem]


def build_nc():
    nc = bass.Bass("TRN2", target_bir_lowering=False, debug=False,
                   detect_race_conditions=False)

    dt_in = lambda n, shp, dt=F32: nc.dram_tensor(n, shp, dt, kind="ExternalInput")
    x_own_t = dt_in("x_own", [OWN, C])
    x_batch_t = dt_in("x_batch", [T, C])
    W_attn_t = dt_in("W_attn", [C, 3 * C])
    W_proj_t = dt_in("W_proj", [C, C])
    w1t_t = dt_in("w1t", [32, 128, 1024])
    w2t_t = dt_in("w2t", [32, 128, 1024])
    h1t_t = dt_in("h1t", [32, 128, 4096])
    h2t_t = dt_in("h2t", [32, 128, 4096])
    w3t_t = dt_in("w3t", [32, 128, 1024])
    s1c_t = dt_in("s1c", [128, 32])
    s2c_t = dt_in("s2c", [128, 32])
    sh1c_t = dt_in("sh1c", [128, 32])
    sh2c_t = dt_in("sh2c", [128, 32])
    s3bc_t = dt_in("s3bc", [128, 1024])
    masks_t = dt_in("masks", [NSLOT, 16, 128, 512], BF16)
    out_t = nc.dram_tensor("out", [OWN, C], F32, kind="ExternalOutput")

    _dbg = os.environ.get("KDBG") == "1"
    dt_int = lambda n, shp, dt: nc.dram_tensor(
        n, shp, dt, kind=("ExternalOutput" if _dbg else "Internal"))
    xno_d = dt_int("xno_d", [OWN, C], BF16)
    xn_d = dt_int("xn_d", [T, C], BF16)
    kT_d = dt_int("kT_d", [C, T], BF16)
    v_d = dt_int("v_d", [T, C], BF16)
    y_d = dt_int("y_d", [C, OWN], BF16)
    h2o_d = dt_int("h2o_d", [DH, OWN], BF16)
    lnst_d = dt_int("lnst_d", [80, 128, 8], F32)
    w1b_d = dt_int("w1b_d", [32, 128, 1024], BF16)
    w2b_d = dt_int("w2b_d", [32, 128, 1024], BF16)
    h1b_d = dt_int("h1b_d", [32, 128, 4096], BF16)
    h2b_d = dt_int("h2b_d", [32, 128, 4096], BF16)

    x_own = x_own_t.ap()
    x_batch = x_batch_t.ap()
    W_attn = W_attn_t.ap()
    W_proj = W_proj_t.ap()
    out = out_t.ap()

    S = Sched()

    with ExitStack() as es:
        e = es.enter_context
        BQ = e(nc.sbuf_tensor("BQ", [128, 16384], BF16))
        BX = e(nc.sbuf_tensor("BX", [128, 16384], BF16))
        BH = e(nc.sbuf_tensor("BH", [128, 32768], BF16))
        BF = e(nc.sbuf_tensor("BF", [128, 7680], F32))
        BXW = e(nc.sbuf_tensor("BXW", [128, 4096], BF16))
        P6Y = e(nc.sbuf_tensor("P6Y", [128, 2048], BF16))
        SCL = e(nc.sbuf_tensor("SCL", [128, 152], F32))
        pb = [e(nc.psum_tensor(f"pb{i}", [128, 512], F32)) for i in range(8)]

        sem_names = ["X0", "X1", "XH0", "XH1", "XOT", "WAT", "PRO",
                     "MK0", "MK1", "KV0", "KV1", "WS0", "WS1",
                     "W30", "W31", "H20", "H21", "YG0", "YG1", "XT0", "XT1",
                     "CAST", "PE", "SC", "VE",
                     "GS0", "GS1", "KS0", "KS1", "VS0", "VS1",
                     "YS0", "YS1", "HS0", "HS1", "OS0", "OS1",
                     "ZPE", "ZSC", "ZVE", "DBG"]
        sem = {n: e(nc.semaphore(n)) for n in sem_names}

        COL_S1, COL_S2, COL_SH1, COL_SH2 = 0, 32, 64, 96
        COL_B005, COL_EPS, COL_VS, COL_QS = 128, 129, 130, 131
        COL_SUM, COL_SSQ, COL_MEAN, COL_MSQ = 132, 134, 136, 138
        COL_VAR, COL_TMP, COL_RSTD = 140, 142, 144

        def scl(col):
            return SCL[:, col:col + 1]

        for col, t in ((COL_S1, s1c_t), (COL_S2, s2c_t), (COL_SH1, sh1c_t),
                       (COL_SH2, sh2c_t)):
            S.emit("sy", lambda g, c0=col, tt=t: g.dma_start(
                out=SCL[:, c0:c0 + 32], in_=tt.ap()), inc=("PRO", 16))
        pro_scales = S.c("PRO")
        S.emit("ve", lambda g: g.memset(scl(COL_B005), 0.005), inc=("VE", 1))
        S.emit("ve", lambda g: g.memset(SCL[:, 146:147], 0.25), inc=("VE", 1))
        ve_warm = S.c("VE")
        for fn_ in (AF.Sqrt, AF.Silu, AF.Gelu, AF.Sqrt):
            for rep in range(4):
                S.emit("sc", lambda g, f=fn_: g.activation(
                    SCL[:, 147:148], SCL[:, 146:147], f),
                    waits={"VE": ve_warm}, inc=("SC", 1))
        S.emit("ve", lambda g: g.memset(scl(COL_EPS), 1e-5), inc=("VE", 1))
        S.emit("ve", lambda g: g.memset(scl(COL_VS), 1e-3), inc=("VE", 1))
        S.emit("ve", lambda g: g.memset(scl(COL_QS), 0.125), inc=("VE", 1))

        CH = 16384
        st_post = 0
        for src_t, dst_t, nel in ((w1t_t, w1b_d, 32 * 128 * 1024),
                                  (w2t_t, w2b_d, 32 * 128 * 1024),
                                  (h1t_t, h1b_d, 32 * 128 * 4096),
                                  (h2t_t, h2b_d, 32 * 128 * 4096)):
            per = nel // 128
            for c0 in range(0, per, CH):
                i_ap = bass.AP(src_t, c0, [[per, 128], [1, CH]])
                o_ap = bass.AP(dst_t, c0, [[per, 128], [1, CH]])
                ld_post = S.emit("gp", lambda g, i=i_ap: g.dma_start(
                    out=BX[:, 0:CH], in_=i), waits={"CAST": st_post},
                    inc=("CAST", 16))
                st_post = S.emit("gp", lambda g, o=o_ap: g.dma_start(
                    out=o, in_=BX[:, 0:CH]), waits={"CAST": ld_post},
                    inc=("CAST", 16))
        cast_total = S.c("CAST")
        for ic in range(8):
            S.emit("gp", lambda g, ic=ic: g.dma_start(
                out=BQ[:, ic * 2048:(ic + 1) * 2048],
                in_=W_attn[ic * 128:(ic + 1) * 128, 1024:3072]), inc=("WAT", 16))
        wat_kv = S.c("WAT")

        ln_ve, ln_sc, ln_store, ln_ssq = {}, {}, {}, {}

        def ln_tile(ti, i, src_ap, dst_d):
            par = ti % 2
            xs = BF[:, par * 1024:(par + 1) * 1024]
            xnb = BH[:, par * 1024:(par + 1) * 1024]
            sq = BF[:, 2048:3072]
            xsem = "X0" if par == 0 else "X1"
            waits = {}
            if ti >= 2:
                waits = {"VE": ln_ve[ti - 2], "SC": ln_sc[ti - 2]}
            post = S.emit("sy", lambda g, a=xs, s=src_ap, i=i: g.dma_start(
                out=a, in_=s[i * 128:(i + 1) * 128, :]), waits=waits, inc=(xsem, 16))
            S.emit("ve", lambda g, a=xs, c=par: g.reduce_sum(
                SCL[:, COL_SUM + c:COL_SUM + c + 1], a, axis=AX),
                waits={xsem: post}, inc=("VE", 1), tiny=True)
            sqw = {xsem: post}
            if ti >= 1:
                sqw["VE"] = ln_ssq[ti - 1]
            ln_sc[ti] = S.emit("sc", lambda g, a=xs, c=par: g.activation(
                sq, a, AF.Square),
                waits=sqw, inc=("SC", 1))
            ln_ssq[ti] = S.emit("ve", lambda g, c=par: g.reduce_sum(
                SCL[:, COL_SSQ + c:COL_SSQ + c + 1], sq, axis=AX),
                waits={"SC": ln_sc[ti]}, inc=("VE", 1), tiny=True)
            S.emit("ve", lambda g, c=par: g.tensor_scalar_mul(
                SCL[:, COL_MEAN + c:COL_MEAN + c + 1],
                SCL[:, COL_SUM + c:COL_SUM + c + 1], 1.0 / 1024.0), tiny=True)
            S.emit("ve", lambda g, c=par: g.tensor_mul(
                SCL[:, COL_MSQ + c:COL_MSQ + c + 1],
                SCL[:, COL_MEAN + c:COL_MEAN + c + 1],
                SCL[:, COL_MEAN + c:COL_MEAN + c + 1]), tiny=True)
            ve_var = S.emit("ve", lambda g, c=par: g.scalar_tensor_tensor(
                SCL[:, COL_VAR + c:COL_VAR + c + 1],
                SCL[:, COL_SSQ + c:COL_SSQ + c + 1], 1.0 / 1024.0,
                SCL[:, COL_MSQ + c:COL_MSQ + c + 1], ALU.mult, ALU.subtract),
                inc=("VE", 1), tiny=True)
            sc_sqrt = S.emit("sc", lambda g, c=par: g.activation(
                SCL[:, COL_TMP + c:COL_TMP + c + 1],
                SCL[:, COL_VAR + c:COL_VAR + c + 1], AF.Sqrt,
                bias=scl(COL_EPS)), waits={"VE": ve_var}, inc=("SC", 1))
            S.emit("ve", lambda g, c=par: g.reciprocal(
                SCL[:, COL_RSTD + c:COL_RSTD + c + 1],
                SCL[:, COL_TMP + c:COL_TMP + c + 1]), waits={"SC": sc_sqrt},
                tiny=True)
            st_stage = SCL[:, 148 - par * 0:148]
            ve_dump = S.emit("ve", lambda g, c=par: g.tensor_copy(
                BF[:, 3072 + 8 * (1 + c):3072 + 8 * (1 + c) + 7],
                bass.AP(SCL, 132 + c, [[152, 128], [2, 7]])), inc=("VE", 1),
                tiny=True)
            S.emit("sc", lambda g, ti=ti, c=par: g.dma_start(
                out=bass.AP(lnst_d, ti * 128 * 8, [[8, 128], [1, 7]]),
                in_=BF[:, 3072 + 8 * (1 + c):3072 + 8 * (1 + c) + 7]),
                waits={"VE": ve_dump}, inc=("DBG", 16))
            gsem = "GS0" if par == 0 else "GS1"
            w2 = {gsem: ln_store[ti - 2]} if ti >= 2 else {}
            ln_ve[ti] = S.emit("ve", lambda g, a=xs, o=xnb, c=par: g.tensor_scalar(
                o, a, SCL[:, COL_MEAN + c:COL_MEAN + c + 1],
                SCL[:, COL_RSTD + c:COL_RSTD + c + 1], ALU.subtract, ALU.mult),
                waits=w2, inc=("VE", 1), tiny=True)
            ln_store[ti] = S.emit("sc", lambda g, o=xnb, d=dst_d, i=i: g.dma_start(
                out=d.ap()[i * 128:(i + 1) * 128, :], in_=o),
                waits={"VE": ln_ve[ti]}, inc=(gsem, 16))

        for i in range(16):
            ln_tile(i, i, x_own, xno_d)
        ph1_gs = {"GS0": S.c("GS0"), "GS1": S.c("GS1")}
        for i in range(64):
            ln_tile(16 + i, i, x_batch, xn_d)
        ph2_gs = {"GS0": S.c("GS0"), "GS1": S.c("GS1")}
        S.mark(1)

        prev_use = {}
        kst_prev = {}
        vst_prev = {}
        hg_last_pe = {}
        for hg in range(8):
            par = hg % 2
            xsem = "XH0" if par == 0 else "XH1"
            w = dict(ph2_gs) if hg == 0 else {}
            if hg >= 2:
                w["PE"] = hg_last_pe[hg - 2]
            xh_post = None
            for ic in range(8):
                xh_post = S.emit("sy", lambda g, hg=hg, ic=ic, p=par: g.dma_start_transpose(
                    BH[:, p * 8192 + ic * 1024: p * 8192 + (ic + 1) * 1024],
                    xn_d.ap()[hg * 1024:(hg + 1) * 1024, ic * 128:(ic + 1) * 128]),
                    waits=(w if ic == 0 else {}), inc=(xsem, 16))
            for oc in range(8):
                b0, b1 = pb[(oc % 2) * 2], pb[(oc % 2) * 2 + 1]
                kk = ("k", oc % 2)
                mmw = {xsem: xh_post, "WAT": wat_kv}
                if kk in prev_use:
                    mmw["SC"] = prev_use[kk]
                pe_post = None
                for ic in range(8):
                    lhs = BQ[:, ic * 2048 + oc * 128: ic * 2048 + (oc + 1) * 128]
                    rhs0 = BH[:, par * 8192 + ic * 1024: par * 8192 + ic * 1024 + 512]
                    rhs1 = BH[:, par * 8192 + ic * 1024 + 512: par * 8192 + (ic + 1) * 1024]
                    S.emit("pe", lambda g, o=b0, l=lhs, r=rhs0, st=(ic == 0), sp=(ic == 7):
                           g.matmul(o[:], l, r, start=st, stop=sp),
                           waits=(mmw if ic == 0 else {}))
                    pe_post = S.emit("pe", lambda g, o=b1, l=lhs, r=rhs1, st=(ic == 0), sp=(ic == 7):
                                     g.matmul(o[:], l, r, start=st, stop=sp),
                                     inc=("PE", 1) if ic == 7 else None)
                kst = BH[:, 16384 + (oc % 2) * 1024: 16384 + (oc % 2) * 1024 + 1024]
                ksem = "KS0" if oc % 2 == 0 else "KS1"
                evw = {"PE": pe_post}
                if (oc % 2) in kst_prev:
                    evw[ksem] = kst_prev[oc % 2]
                S.emit("sc", lambda g, o=kst, s=b0: g.copy(o[:, 0:512], s[:]), waits=evw)
                prev_use[kk] = S.emit("sc", lambda g, o=kst, s=b1: g.copy(
                    o[:, 512:1024], s[:]), inc=("SC", 1))
                kst_prev[oc % 2] = S.emit("sc", lambda g, o=kst, hg=hg, oc=oc: g.dma_start(
                    out=kT_d.ap()[oc * 128:(oc + 1) * 128, hg * 1024:(hg + 1) * 1024],
                    in_=o), waits={"SC": prev_use[kk]}, inc=(ksem, 16))
            for tc in range(8):
                b0, b1 = pb[4 + (tc % 2) * 2], pb[4 + (tc % 2) * 2 + 1]
                kk = ("v", tc % 2)
                mmw = {}
                if kk in prev_use:
                    mmw["SC"] = prev_use[kk]
                pe_post = None
                for ic in range(8):
                    lhs = BH[:, par * 8192 + ic * 1024 + tc * 128: par * 8192 + ic * 1024 + (tc + 1) * 128]
                    rhs0 = BQ[:, ic * 2048 + 1024: ic * 2048 + 1536]
                    rhs1 = BQ[:, ic * 2048 + 1536: ic * 2048 + 2048]
                    S.emit("pe", lambda g, o=b0, l=lhs, r=rhs0, st=(ic == 0), sp=(ic == 7):
                           g.matmul(o[:], l, r, start=st, stop=sp),
                           waits=(mmw if ic == 0 else {}))
                    pe_post = S.emit("pe", lambda g, o=b1, l=lhs, r=rhs1, st=(ic == 0), sp=(ic == 7):
                                     g.matmul(o[:], l, r, start=st, stop=sp),
                                     inc=("PE", 1) if ic == 7 else None)
                vst = BH[:, 18432 + (tc % 2) * 1024: 18432 + (tc % 2) * 1024 + 1024]
                vsem = "VS0" if tc % 2 == 0 else "VS1"
                evw = {"PE": pe_post}
                if (tc % 2) in vst_prev:
                    evw[vsem] = vst_prev[tc % 2]
                S.emit("sc", lambda g, o=vst, s=b0: g.activation(
                    o[:, 0:512], s[:], AF.Copy, scale=scl(COL_VS)), waits=evw)
                prev_use[kk] = S.emit("sc", lambda g, o=vst, s=b1: g.activation(
                    o[:, 512:1024], s[:], AF.Copy, scale=scl(COL_VS)), inc=("SC", 1))
                vst_prev[tc % 2] = S.emit("sc", lambda g, o=vst, hg=hg, tc=tc: g.dma_start(
                    out=v_d.ap()[hg * 1024 + tc * 128: hg * 1024 + (tc + 1) * 128, :],
                    in_=o), waits={"SC": prev_use[kk]}, inc=(vsem, 16))
            hg_last_pe[hg] = S.c("PE")
        ph3_pe_total = S.c("PE")
        ph3_sc_total = S.c("SC")
        ph3_kv_st = {s: S.c(s) for s in ("KS0", "KS1", "VS0", "VS1")}
        S.mark(2)

        for ic in range(8):
            S.emit("gp", lambda g, ic=ic: g.dma_start(
                out=BH[:, 16384 + ic * 1024: 16384 + (ic + 1) * 1024],
                in_=W_attn[ic * 128:(ic + 1) * 128, 0:1024]),
                waits=(dict(ph3_kv_st) if ic == 0 else {}), inc=("WAT", 16))
        wat_q = S.c("WAT")
        for ic in range(8):
            xotw = dict(ph1_gs)
            xotw["CAST"] = cast_total
            S.emit("sy", lambda g, ic=ic: g.dma_start_transpose(
                BX[:, ic * 2048:(ic + 1) * 2048],
                xno_d.ap()[:, ic * 128:(ic + 1) * 128]),
                waits=(xotw if ic == 0 else {}), inc=("XOT", 16))
        xot_total = S.c("XOT")
        qevac = {}
        for oc in range(8):
            banks = [pb[0], pb[1], pb[2], pb[3]] if oc % 2 == 0 else [pb[4], pb[5], pb[6], pb[7]]
            mmw = {"XOT": xot_total, "WAT": wat_q,
                   "SC": qevac.get(oc - 2, ph3_sc_total)}
            pe_post = None
            for ic in range(8):
                lhs = BH[:, 16384 + ic * 1024 + oc * 128: 16384 + ic * 1024 + (oc + 1) * 128]
                for j in range(4):
                    rhs = BX[:, ic * 2048 + j * 512: ic * 2048 + (j + 1) * 512]
                    pe_post = S.emit("pe", lambda g, o=banks[j], l=lhs, r=rhs,
                                     st=(ic == 0), sp=(ic == 7): g.matmul(
                                         o[:], l, r, start=st, stop=sp),
                                     waits=(mmw if (ic == 0 and j == 0) else {}),
                                     inc=("PE", 1) if (ic == 7 and j == 3) else None)
            for j in range(4):
                qevac[oc] = S.emit("sc", lambda g, oc=oc, j=j, s=banks[j]: g.activation(
                    BQ[:, oc * 2048 + j * 512: oc * 2048 + (j + 1) * 512],
                    s[:], AF.Copy, scale=scl(COL_QS)),
                    waits={"PE": pe_post}, inc=("SC", 1))
        ph35_sc_total = S.c("SC")
        S.mark(3)

        pm = {}
        kv_prev = {}
        mask_prev = {}
        yev = {0: ph35_sc_total, 1: ph35_sc_total}
        yst_prev = {}
        first_kv = True
        pe_av = None
        for s in range(_KP4S):
            mpar = s % 2
            msem = "MK0" if mpar == 0 else "MK1"
            mw = {}
            if s == 0:
                mw["PE"] = ph3_pe_total
            if mpar in mask_prev:
                mw["VE"] = mask_prev[mpar]
            for j in range(16):
                S.emit("sy", lambda g, s=s, j=j, mp=mpar: g.dma_start(
                    out=BH[:, mp * 8192 + j * 512: mp * 8192 + (j + 1) * 512],
                    in_=bass.AP(masks_t, (s * 16 + j) * 128 * 512,
                                [[512, 128], [1, 512]])),
                    waits=(mw if j == 0 else {}), inc=(msem, 16))
            mask_total = S.c(msem)
            nkb = ESCH[s]
            for q in range(_KP4Q):
                kv_post = {}
                for kb in range(nkb):
                    sup, jj = kb // 4, kb % 4
                    spar = sup % 2
                    ksem = "KV0" if spar == 0 else "KV1"
                    if jj == 0:
                        lw = {}
                        if first_kv:
                            lw.update(ph3_kv_st)
                            first_kv = False
                        if spar in kv_prev:
                            lw["PE"] = kv_prev[spar]
                        for hpc in range(2):
                            S.emit("sy", lambda g, q=q, hpc=hpc, sup=sup, p=spar: g.dma_start(
                                out=BXW[:, p * 1024 + hpc * 512: p * 1024 + (hpc + 1) * 512],
                                in_=kT_d.ap()[(q * 2 + hpc) * 128:(q * 2 + hpc + 1) * 128,
                                              sup * 512:(sup + 1) * 512]),
                                waits=(lw if hpc == 0 else {}), inc=(ksem, 16))
                        for vj in range(4):
                            kv_post[sup] = S.emit("sy", lambda g, q=q, vj=vj, sup=sup, p=spar: g.dma_start(
                                out=BXW[:, 2048 + p * 1024 + vj * 256: 2048 + p * 1024 + (vj + 1) * 256],
                                in_=v_d.ap()[sup * 512 + vj * 128: sup * 512 + (vj + 1) * 128,
                                             q * 256:(q + 1) * 256]), inc=(ksem, 16))
                    for hp in range(2):
                        pp = (kb * 2 + hp) % 2
                        bA, bB = pb[pp * 2], pb[pp * 2 + 1]
                        qoff = (q * 2 + hp) * 2048 + s * 512
                        koff = spar * 1024 + hp * 512 + jj * 128
                        if _KP4L < 1:
                            continue
                        qkw = {ksem: kv_post[sup]}
                        if pp in pm:
                            qkw["SC"] = pm[pp].get("sc", 0)
                            qkw["VE"] = pm[pp].get("ve3", 0)
                        else:
                            qkw["SC"] = ph35_sc_total
                        qkw = {k: v for k, v in qkw.items() if v}
                        S.emit("pe", lambda g, o=bA, ko=koff, qo=qoff: g.matmul(
                            o[:], BXW[0:64, ko:ko + 128], BQ[0:64, qo:qo + 512],
                            start=True, stop=True, tile_position=(0, 0)), waits=qkw)
                        pe_qk = S.emit("pe", lambda g, o=bB, ko=koff, qo=qoff: g.matmul(
                            o[:], BXW[64:128, ko:ko + 128], BQ[64:128, qo:qo + 512],
                            start=True, stop=True, tile_position=(64, 0)), inc=("PE", 1))
                        RA, RB = BF[:, 0:512], BF[:, 512:1024]
                        qA = BF[:, 1024 + pp * 1024: 1536 + pp * 1024]
                        qB = BF[:, 1536 + pp * 1024: 2048 + pp * 1024]
                        tA = BF[:, 3072 + pp * 1024: 3584 + pp * 1024]
                        tB = BF[:, 3584 + pp * 1024: 4096 + pp * 1024]
                        wA = BF[:, 5120 + pp * 1024: 5632 + pp * 1024]
                        wB = BF[:, 5632 + pp * 1024: 6144 + pp * 1024]
                        AA = BH[:, 16384 + (pp * 2) * 512: 16384 + (pp * 2 + 1) * 512]
                        AB = BH[:, 16384 + (pp * 2 + 1) * 512: 16384 + (pp * 2 + 2) * 512]
                        if _KP4L < 2:
                            pm[pp] = {"pe_qk": pe_qk}
                            kv_prev[spar] = pe_qk
                            pe_av = pe_qk
                            continue
                        scw = {"PE": pe_qk}
                        if pp in pm:
                            scw["VE"] = pm[pp].get("ve5", 0)
                        scw = {k: v for k, v in scw.items() if v}
                        S.emit("sc", lambda g, o=RA, i=bA: g.activation(o, i[:], AF.Relu),
                               waits=scw)
                        S.emit("sc", lambda g, o=RB, i=bB: g.activation(o, i[:], AF.Relu))
                        S.emit("sc", lambda g, o=qA, i=RA: g.activation(
                            o, i, AF.Square, bias=scl(COL_B005)))
                        sc_p = S.emit("sc", lambda g, o=qB, i=RB: g.activation(
                            o, i, AF.Square, bias=scl(COL_B005)), inc=("SC", 1))
                        if _KP4L < 3:
                            pm[pp] = {"sc": sc_p}
                            kv_prev[spar] = pe_qk
                            pe_av = pe_qk
                            continue
                        S.emit("ve", lambda g, o=tA, i=bA: g.tensor_scalar(
                            o, i[:], -0.01, 0.999975, ALU.mult, ALU.add),
                            waits={"PE": pe_qk, "SC": sc_p})
                        ve3 = S.emit("ve", lambda g, o=tB, i=bB: g.tensor_scalar(
                            o, i[:], -0.01, 0.999975, ALU.mult, ALU.add), inc=("VE", 1))
                        if _KP4L < 4:
                            pm[pp] = {"sc": sc_p, "ve3": ve3}
                            kv_prev[spar] = pe_qk
                            pe_av = pe_qk
                            continue
                        vew = {"SC": sc_p}
                        if pp in pm:
                            vew["PE"] = pm[pp].get("pe_av", 0)
                        vew = {k: v for k, v in vew.items() if v}
                        S.emit("ve", lambda g, o=wA, a=qA, b=tA: g.tensor_tensor(
                            o, a, b, ALU.add), waits=vew)
                        S.emit("ve", lambda g, o=wB, a=qB, b=tB: g.tensor_tensor(
                            o, a, b, ALU.add))
                        S.emit("ve", lambda g, o=AA, t=tA, w=wA: g.scalar_tensor_tensor(
                            o, t, 1.999975, w, ALU.is_le, ALU.mult))
                        ve5 = S.emit("ve", lambda g, o=AB, t=tB, w=wB: g.scalar_tensor_tensor(
                            o, t, 1.999975, w, ALU.is_le, ALU.mult), inc=("VE", 1))
                        if kb >= 16 * s and _KP4L >= 5:
                            moff = mpar * 8192 + (kb - 16 * s) * 512
                            S.emit("ve", lambda g, o=AA, mo=moff: g.tensor_tensor(
                                o, o, BH[:, mo:mo + 512], ALU.mult),
                                waits={msem: mask_total})
                            ve5 = S.emit("ve", lambda g, o=AB, mo=moff: g.tensor_tensor(
                                o, o, BH[:, mo:mo + 512], ALU.mult), inc=("VE", 1))
                            mask_prev[mpar] = ve5
                        if _KP4L < 6:
                            pm[pp] = {"sc": sc_p, "ve3": ve3, "ve5": ve5}
                            kv_prev[spar] = pe_qk
                            pe_av = pe_qk
                            continue
                        voff = 2048 + spar * 1024 + jj * 256 + hp * 128
                        yb = pb[4 + hp]
                        avw = {"VE": ve5}
                        if kb == 0:
                            avw["SC"] = yev[hp]
                        S.emit("pe", lambda g, o=yb, vo=voff, a=AA, st=(kb == 0),
                               sp=(kb == nkb - 1): g.matmul(
                                   o[0:64, :], BXW[:, vo:vo + 64], a,
                                   start=st, stop=sp, tile_position=(0, 0)), waits=avw)
                        pe_av = S.emit("pe", lambda g, o=yb, vo=voff, a=AB,
                                       st=(kb == 0), sp=(kb == nkb - 1): g.matmul(
                                           o[64:128, :], BXW[:, vo + 64:vo + 128], a,
                                           start=st, stop=sp, tile_position=(0, 64),
                                           skip_group_check=True),
                                       inc=("PE", 1))
                        pm[pp] = {"sc": sc_p, "ve3": ve3, "ve5": ve5, "pe_av": pe_av}
                        kv_prev[spar] = pe_av
                for hp in range(2 if _KP4L >= 6 else 0):
                    yst = BH[:, 18432 + hp * 512: 18432 + (hp + 1) * 512]
                    ysem2 = "YS0" if hp == 0 else "YS1"
                    evw = {"PE": pe_av}
                    if hp in yst_prev:
                        evw[ysem2] = yst_prev[hp]
                    yev[hp] = S.emit("sc", lambda g, o=yst, s=pb[4 + hp]: g.copy(
                        o, s[:]), waits=evw, inc=("SC", 1))
                    yst_prev[hp] = S.emit("sc", lambda g, o=yst, q=q, hp=hp, s=s: g.dma_start(
                        out=y_d.ap()[(q * 2 + hp) * 128:(q * 2 + hp + 1) * 128,
                                     s * 512:(s + 1) * 512], in_=o),
                        waits={"SC": yev[hp]}, inc=(ysem2, 16))
        ph4_y_st = {"YS0": S.c("YS0"), "YS1": S.c("YS1")}
        ph4_pe_total = S.c("PE")
        ph4_sc_total = S.c("SC")
        S.mark(4)

        w_item = 0
        ws_prev = {}
        ffn_gate = {}
        h2_last_pe = ph4_pe_total
        h2o_sst = {}
        for sp in range(NSLOT):
            for rep in range(4):
                S.emit("sc", lambda g: g.activation(
                    SCL[:, 147:148], SCL[:, 146:147], AF.Silu), inc=("SC", 1))
            for oc in range(32):
                par = w_item % 2
                wsem = "WS0" if par == 0 else "WS1"
                lw = {}
                if sp == 0 and oc < 2:
                    lw["CAST"] = cast_total
                    lw["PE"] = ph4_pe_total
                if par in ws_prev:
                    lw["PE"] = max(lw.get("PE", 0), ws_prev[par])
                S.emit("sy", lambda g, oc=oc, p=par: g.dma_start(
                    out=BXW[:, p * 2048: p * 2048 + 1024],
                    in_=bass.AP(w1b_d, oc * 128 * 1024, [[1024, 128], [1, 1024]])),
                    waits=lw, inc=(wsem, 16))
                ws_post = S.emit("sy", lambda g, oc=oc, p=par: g.dma_start(
                    out=BXW[:, p * 2048 + 1024: p * 2048 + 2048],
                    in_=bass.AP(w2b_d, oc * 128 * 1024, [[1024, 128], [1, 1024]])),
                    inc=(wsem, 16))
                b1, b2 = pb[oc % 2], pb[2 + oc % 2]
                mmw = {wsem: ws_post, "PE": h2_last_pe}
                gk = ("w12", oc % 2)
                if gk in ffn_gate:
                    mmw["SC"] = ffn_gate[gk]["sc"]
                    mmw["VE"] = ffn_gate[gk]["ve"]
                elif sp == 0:
                    mmw["SC"] = ph4_sc_total
                for ic in range(8):
                    rhs = BX[:, ic * 2048 + sp * 512: ic * 2048 + (sp + 1) * 512]
                    S.emit("pe", lambda g, o=b1, p=par, ic=ic, r=rhs, st=(ic == 0),
                           sp2=(ic == 7): g.matmul(
                               o[:], BXW[:, p * 2048 + ic * 128: p * 2048 + (ic + 1) * 128],
                               r, start=st, stop=sp2),
                           waits=(mmw if ic == 0 else {}),
                           inc=("PE", 1) if ic == 7 else None)
                pe_w1 = S.c("PE")
                for ic in range(8):
                    rhs = BX[:, ic * 2048 + sp * 512: ic * 2048 + (sp + 1) * 512]
                    S.emit("pe", lambda g, o=b2, p=par, ic=ic, r=rhs, st=(ic == 0),
                           sp2=(ic == 7): g.matmul(
                               o[:], BXW[:, p * 2048 + 1024 + ic * 128: p * 2048 + 1024 + (ic + 1) * 128],
                               r, start=st, stop=sp2),
                           inc=("PE", 1) if ic == 7 else None)
                pe_w2 = S.c("PE")
                ws_prev[par] = pe_w2
                silu = BF[:, (oc % 2) * 512:(oc % 2) * 512 + 512]
                sc_post = S.emit("sc", lambda g, o=silu, s=b1, oc=oc: g.activation(
                    o, s[:], AF.Silu, scale=SCL[:, COL_S1 + oc:COL_S1 + oc + 1]),
                    waits={"PE": pe_w1, "PRO": pro_scales}, inc=("SC", 1))
                ve_post = S.emit("ve", lambda g, s=b2, si=silu, oc=oc: g.scalar_tensor_tensor(
                    BH[:, oc * 512:(oc + 1) * 512], s[:],
                    SCL[:, COL_S2 + oc:COL_S2 + oc + 1], si, ALU.mult, ALU.mult),
                    waits={"PE": pe_w2, "SC": sc_post}, inc=("VE", 1))
                ffn_gate[gk] = {"sc": sc_post, "ve": ve_post}
                w_item += 1
            h_ready = S.c("VE")
            for rep in range(4):
                S.emit("sc", lambda g: g.activation(
                    SCL[:, 147:148], SCL[:, 146:147], AF.Gelu), inc=("SC", 1))
            for oc2 in range(32):
                halves = []
                for half in range(2):
                    par = w_item % 2
                    wsem = "WS0" if par == 0 else "WS1"
                    lw = {}
                    if par in ws_prev:
                        lw["PE"] = ws_prev[par]
                    hp_post = S.emit("sy", lambda g, oc2=oc2, half=half, p=par: g.dma_start(
                        out=BXW[:, p * 2048:(p + 1) * 2048],
                        in_=bass.AP(h1b_d, oc2 * 128 * 4096 + half * 2048,
                                    [[4096, 128], [1, 2048]])),
                        waits=lw, inc=(wsem, 16))
                    halves.append((par, wsem, hp_post))
                    w_item += 1
                bk = pb[4 + oc2 % 2]
                gk = ("h1", oc2 % 2)
                for hc in range(32):
                    par, wsem, hp_post = halves[hc // 16]
                    mmw = {}
                    if hc in (0, 16):
                        mmw[wsem] = hp_post
                        if hc == 0:
                            mmw["VE"] = h_ready
                            mmw["SC"] = ffn_gate.get(gk, ph4_sc_total)
                    rhs = BH[:, hc * 512:(hc + 1) * 512]
                    S.emit("pe", lambda g, o=bk, p=par, hc=hc, r=rhs, st=(hc == 0),
                           sp2=(hc == 31): g.matmul(
                               o[:], BXW[:, p * 2048 + (hc % 16) * 128: p * 2048 + ((hc % 16) + 1) * 128],
                               r, start=st, stop=sp2), waits=mmw,
                           inc=("PE", 1) if hc == 31 else None)
                pe_h1 = S.c("PE")
                ws_prev[halves[0][0]] = pe_h1
                ws_prev[halves[1][0]] = pe_h1
                ffn_gate[gk] = S.emit("sc", lambda g, oc2=oc2, s=bk: g.activation(
                    BH[:, 16384 + oc2 * 512: 16384 + (oc2 + 1) * 512], s[:], AF.Gelu,
                    scale=SCL[:, COL_SH1 + oc2:COL_SH1 + oc2 + 1]),
                    waits={"PE": pe_h1}, inc=("SC", 1))
            g_ready = S.c("SC")
            for oc2 in range(32):
                halves = []
                for half in range(2):
                    par = w_item % 2
                    wsem = "WS0" if par == 0 else "WS1"
                    lw = {}
                    if par in ws_prev:
                        lw["PE"] = ws_prev[par]
                    hp_post = S.emit("sy", lambda g, oc2=oc2, half=half, p=par: g.dma_start(
                        out=BXW[:, p * 2048:(p + 1) * 2048],
                        in_=bass.AP(h2b_d, oc2 * 128 * 4096 + half * 2048,
                                    [[4096, 128], [1, 2048]])),
                        waits=lw, inc=(wsem, 16))
                    halves.append((par, wsem, hp_post))
                    w_item += 1
                bk = pb[6 + oc2 % 2]
                gk = ("h2", oc2 % 2)
                for hc in range(32):
                    par, wsem, hp_post = halves[hc // 16]
                    mmw = {}
                    if hc in (0, 16):
                        mmw[wsem] = hp_post
                        if hc == 0:
                            mmw["SC"] = max(g_ready, ffn_gate.get(gk, 0))
                    rhs = BH[:, 16384 + hc * 512: 16384 + (hc + 1) * 512]
                    S.emit("pe", lambda g, o=bk, p=par, hc=hc, r=rhs, st=(hc == 0),
                           sp2=(hc == 31): g.matmul(
                               o[:], BXW[:, p * 2048 + (hc % 16) * 128: p * 2048 + ((hc % 16) + 1) * 128],
                               r, start=st, stop=sp2), waits=mmw,
                           inc=("PE", 1) if hc == 31 else None)
                pe_h2 = S.c("PE")
                ws_prev[halves[0][0]] = pe_h2
                ws_prev[halves[1][0]] = pe_h2
                h2stg = P6Y[:, (oc2 % 2) * 512:(oc2 % 2 + 1) * 512]
                hsem2 = "HS0" if oc2 % 2 == 0 else "HS1"
                evw = {"PE": pe_h2}
                if (oc2 % 2) in h2o_sst:
                    evw[hsem2] = h2o_sst[oc2 % 2]
                ffn_gate[gk] = S.emit("sc", lambda g, o=h2stg, s=bk, oc2=oc2: g.activation(
                    o, s[:], AF.Gelu, scale=SCL[:, COL_SH2 + oc2:COL_SH2 + oc2 + 1]),
                    waits=evw, inc=("SC", 1))
                h2o_sst[oc2 % 2] = S.emit("sc", lambda g, o=h2stg, oc2=oc2, sp=sp: g.dma_start(
                    out=h2o_d.ap()[oc2 * 128:(oc2 + 1) * 128, sp * 512:(sp + 1) * 512],
                    in_=o), waits={"SC": ffn_gate[gk]}, inc=(hsem2, 16))
            h2_last_pe = S.c("PE")
        ph5_h_st = {"HS0": S.c("HS0"), "HS1": S.c("HS1")}
        ph5_pe_total = S.c("PE")
        ph5_sc_total = S.c("SC")
        ph5_ve_total = S.c("VE")
        S.mark(5)

        for cc in range(8):
            S.emit("gp", lambda g, cc=cc: g.dma_start(
                out=BQ[:, cc * 1024:(cc + 1) * 1024],
                in_=W_proj[cc * 128:(cc + 1) * 128, :]),
                waits=({"PE": ph5_pe_total} if cc == 0 else {}), inc=("WAT", 16))
        wat_total = S.c("WAT")
        S.emit("sy", lambda g: g.dma_start(out=BF[:, 2048:3072], in_=s3bc_t.ap()),
               waits={"PE": ph5_pe_total, "VE": ph5_ve_total}, inc=("PRO", 16))
        pro_total = S.c("PRO")
        w3cv = {}
        for hc in range(32):
            par = hc % 2
            wsem = "W30" if par == 0 else "W31"
            lw = {}
            if hc < 2:
                lw["PE"] = ph5_pe_total
                lw["VE"] = ph5_ve_total
            else:
                lw["VE"] = w3cv[hc - 2]
            w3_post = S.emit("sy", lambda g, hc=hc, p=par: g.dma_start(
                out=BF[:, p * 1024:(p + 1) * 1024],
                in_=bass.AP(w3t_t, hc * 128 * 1024, [[1024, 128], [1, 1024]])),
                waits=lw, inc=(wsem, 16))
            cvw = {wsem: w3_post, "PRO": pro_total}
            if hc < 2:
                cvw["PE"] = ph5_pe_total
            w3cv[hc] = S.emit("ve", lambda g, hc=hc, p=par: g.tensor_tensor(
                BH[:, hc * 1024:(hc + 1) * 1024], BF[:, p * 1024:(p + 1) * 1024],
                BF[:, 2048:3072], ALU.mult), waits=cvw, inc=("VE", 1))
        w3_ready = S.c("VE")

        h2l_prev = {}
        yg_prev = {}
        xt_prev = {}
        outstore = {}
        bank_gate = {}
        for grp in range(4):
            h2_posts = []
            for half in range(2):
                hsem = "H20" if half == 0 else "H21"
                lw = {}
                if grp == 0:
                    lw.update(ph5_h_st)
                if (grp - 1, half) in h2l_prev:
                    lw["PE"] = h2l_prev[(grp - 1, half)]
                hpost = None
                for part in range(2):
                    hcb = half * 16 + part * 8
                    i_ap = bass.AP(h2o_d, hcb * 128 * OWN + grp * 512,
                                   [[OWN, 128], [128 * OWN, 8], [1, 512]])
                    hpost = S.emit("sy", lambda g, i=i_ap, half=half, part=part: g.dma_start(
                        out=BX[:, half * 8192 + part * 4096: half * 8192 + (part + 1) * 4096],
                        in_=i), waits=(lw if part == 0 else {}), inc=(hsem, 16))
                h2_posts.append((hsem, hpost))
            ygpar = grp % 2
            ysem = "YG0" if ygpar == 0 else "YG1"
            lw = {}
            if grp == 0:
                lw.update(ph4_y_st)
            if (grp - 2, ygpar) in yg_prev:
                lw["PE"] = yg_prev[(grp - 2, ygpar)]
            y_post = None
            for part in range(2):
                i_ap = bass.AP(y_d, (part * 4) * 128 * OWN + grp * 512,
                               [[OWN, 128], [128 * OWN, 4], [1, 512]])
                y_post = S.emit("sy", lambda g, i=i_ap, p=ygpar, part=part: g.dma_start(
                    out=BQ[:, 8192 + p * 4096 + part * 2048: 8192 + p * 4096 + (part + 1) * 2048],
                    in_=i), waits=(lw if part == 0 else {}), inc=(ysem, 16))
            for half in range(2):
                hsem, hpost = h2_posts[half]
                for tc4 in range(4):
                    for oh in range(2):
                        bk = pb[tc4 * 2 + oh]
                        for hcl in range(16):
                            hc = half * 16 + hcl
                            mmw = {}
                            if hcl == 0:
                                mmw[hsem] = hpost
                                if half == 0:
                                    mmw["VE"] = max(w3_ready,
                                                    bank_gate.get((grp - 1, tc4, oh), 0))
                                    if grp == 0:
                                        mmw["SC"] = ph5_sc_total
                            lhs_off = half * 8192 + (hcl // 8) * 4096 + (hcl % 8) * 512 + tc4 * 128
                            rhs = BH[:, hc * 1024 + oh * 512: hc * 1024 + (oh + 1) * 512]
                            S.emit("pe", lambda g, o=bk, lo=lhs_off, r=rhs,
                                   st=(hc == 0): g.matmul(
                                       o[:], BX[:, lo:lo + 128], r,
                                       start=st, stop=False), waits=mmw)
            for tc4 in range(4):
                tc = grp * 4 + tc4
                xpar = tc % 2
                xsem = "XT0" if xpar == 0 else "XT1"
                lw = {}
                if tc < 2:
                    lw["PE"] = ph5_pe_total
                if (tc - 2) in xt_prev:
                    lw["VE"] = xt_prev[tc - 2]
                xpost = S.emit("sy", lambda g, tc=tc, p=xpar: g.dma_start(
                    out=BF[:, 3072 + p * 1024: 3072 + (p + 1) * 1024],
                    in_=x_own[tc * 128:(tc + 1) * 128, :]), waits=lw, inc=(xsem, 16))
                for oh in range(2):
                    bk = pb[tc4 * 2 + oh]
                    pe_fin = None
                    for cc in range(8):
                        mmw = {}
                        if cc == 0:
                            mmw = {ysem: y_post, "WAT": wat_total}
                        lhs_off = 8192 + ygpar * 4096 + (cc // 4) * 2048 + (cc % 4) * 512 + tc4 * 128
                        rhs = BQ[:, cc * 1024 + oh * 512: cc * 1024 + (oh + 1) * 512]
                        pe_fin = S.emit("pe", lambda g, o=bk, lo=lhs_off, r=rhs,
                                        sp2=(cc == 7): g.matmul(
                                            o[:], BQ[:, lo:lo + 128], r,
                                            start=False, stop=sp2), waits=mmw,
                                        inc=("PE", 1) if cc == 7 else None)
                    yg_prev[(grp, ygpar)] = pe_fin
                    h2l_prev[(grp, 0)] = pe_fin
                    h2l_prev[(grp, 1)] = pe_fin
                    obuf = BF[:, 5120 + (tc % 2) * 1024 + oh * 512: 5120 + (tc % 2) * 1024 + (oh + 1) * 512]
                    xsl = BF[:, 3072 + (tc % 2) * 1024 + oh * 512: 3072 + (tc % 2) * 1024 + (oh + 1) * 512]
                    osem = "OS0" if tc % 2 == 0 else "OS1"
                    evw = {"PE": pe_fin, xsem: xpost}
                    if (tc - 2) in outstore:
                        evw[osem] = outstore[tc - 2]
                    ov = S.emit("ve", lambda g, o=obuf, s=bk, x=xsl: g.tensor_add(
                        o, s[:], x), waits=evw, inc=("VE", 1))
                    bank_gate[(grp, tc4, oh)] = ov
                    xt_prev[tc] = ov
                tc = grp * 4 + tc4
                outstore[tc] = S.emit("gp", lambda g, tc=tc: g.dma_start(
                    out=out[tc * 128:(tc + 1) * 128, :],
                    in_=BF[:, 5120 + (tc % 2) * 1024: 5120 + (tc % 2) * 1024 + 1024]),
                    waits={"VE": xt_prev[tc]},
                    inc=("OS0" if tc % 2 == 0 else "OS1", 16))
        S.mark(6)
        if _KPH < 6 and _KPH in S.marks:
            S.recs = S.recs[:S.marks[_KPH]]
        tot = collections.defaultdict(int)
        for (_en, _w, _fn, _inc, _t) in S.recs:
            if _inc:
                tot[_inc[0]] += _inc[1]
        fin = {k: tot[k] for k in ("GS0", "GS1", "KS0", "KS1", "VS0", "VS1",
                                   "YS0", "YS1", "HS0", "HS1", "OS0", "OS1")
               if tot[k] > 0}
        S.recs.append(("gp", fin, None, None, False))

        for k, v in S.cnt.items():
            assert v < 60000, (k, v)

        ksafe_engs = os.environ.get("KSAFE", "ve").replace(",", " ").split()
        selfsem_all = {"pe": "ZPE", "sc": "ZSC", "ve": "ZVE"}
        selfsem = {k: v for k, v in selfsem_all.items() if k in ksafe_engs}

        with nc.Block() as block:
            def replay(eng_name):
                last = {}
                zs = selfsem.get(eng_name)

                def run(g):
                    cnt = collections.defaultdict(int)
                    zn = 0
                    prev = None
                    prev_tiny = False
                    for (en, waits, fn, inc, tiny) in S.recs:
                        if en != eng_name:
                            if inc is not None:
                                cnt[inc[0]] += inc[1]
                            continue
                        for sname, val in waits.items():
                            if last.get(sname, -1) < val:
                                g.wait_ge(sem[sname], val)
                                last[sname] = val
                        if fn is None:
                            continue
                        if (zs is not None and prev is not None
                                and (tiny or prev_tiny)):
                            if last.get(prev[0], -1) < prev[1]:
                                g.wait_ge(sem[prev[0]], prev[1])
                                last[prev[0]] = prev[1]
                        inst = fn(g)
                        if inc is not None:
                            cnt[inc[0]] += inc[1]
                            inst.then_inc(sem[inc[0]], inc[1])
                            prev = (inc[0], cnt[inc[0]]) if inc[1] == 1 else None
                        elif zs is not None:
                            zn += 1
                            inst.then_inc(sem[zs], 1)
                            prev = (zs, zn)
                        else:
                            prev = None
                        prev_tiny = tiny
                return run

            block.tensor(replay("pe"))
            block.scalar(replay("sc"))
            block.vector(replay("ve"))
            block.sync(replay("sy"))
            block.gpsimd(replay("gp"))

    return nc


_NC_CACHE = None


def _get_nc():
    global _NC_CACHE
    if _NC_CACHE is None:
        _NC_CACHE = build_nc()
    return _NC_CACHE


def kernel(L0_attn_x, L0_attn_W_attn, L0_attn_W_proj,
           L0_ffn_w1_w, L0_ffn_w1_s, L0_ffn_w2_w, L0_ffn_w2_s,
           L0_ffn_h1_w, L0_ffn_h1_s, L0_ffn_h2_w, L0_ffn_h2_s,
           L0_ffn_w3_w, L0_ffn_w3_s):
    x = np.ascontiguousarray(np.asarray(L0_attn_x, dtype=np.float32))
    W_attn = np.ascontiguousarray(np.asarray(L0_attn_W_attn, dtype=np.float32))
    W_proj = np.ascontiguousarray(np.asarray(L0_attn_W_proj, dtype=np.float32))
    w1 = np.asarray(L0_ffn_w1_w, dtype=np.float32)
    w2 = np.asarray(L0_ffn_w2_w, dtype=np.float32)
    h1 = np.asarray(L0_ffn_h1_w, dtype=np.float32)
    h2 = np.asarray(L0_ffn_h2_w, dtype=np.float32)
    w3 = np.asarray(L0_ffn_w3_w, dtype=np.float32)
    s1 = np.asarray(L0_ffn_w1_s, dtype=np.float32)
    s2 = np.asarray(L0_ffn_w2_s, dtype=np.float32)
    sh1 = np.asarray(L0_ffn_h1_s, dtype=np.float32)
    sh2 = np.asarray(L0_ffn_h2_s, dtype=np.float32)
    s3 = np.asarray(L0_ffn_w3_s, dtype=np.float32)

    w1t = np.ascontiguousarray(
        w1.T.reshape(8, 128, 32, 128).transpose(2, 1, 0, 3).reshape(32, 128, 1024))
    w2t = np.ascontiguousarray(
        w2.T.reshape(8, 128, 32, 128).transpose(2, 1, 0, 3).reshape(32, 128, 1024))
    h1t = np.ascontiguousarray(
        h1.T.reshape(32, 128, 32, 128).transpose(2, 1, 0, 3).reshape(32, 128, 4096))
    h2t = np.ascontiguousarray(
        h2.T.reshape(32, 128, 32, 128).transpose(2, 1, 0, 3).reshape(32, 128, 4096))
    w3t = np.ascontiguousarray(w3.T.reshape(32, 128, 1024))
    s1c = np.ascontiguousarray(s1.reshape(32, 128).T)
    s2c = np.ascontiguousarray(s2.reshape(32, 128).T)
    sh1c = np.ascontiguousarray(sh1.reshape(32, 128).T)
    sh2c = np.ascontiguousarray(sh2.reshape(32, 128).T)
    s3bc = np.ascontiguousarray(np.broadcast_to(s3, (128, 1024)))

    shared = dict(W_attn=W_attn, W_proj=W_proj, w1t=w1t, w2t=w2t,
                  h1t=h1t, h2t=h2t, w3t=w3t, s1c=s1c, s2c=s2c,
                  sh1c=sh1c, sh2c=sh2c, s3bc=s3bc)

    in_maps = []
    core_spans = []
    p_idx = np.arange(128)[:, None]
    f_idx = np.arange(512)[None, :]
    for c in range(8):
        g = c // 2 + 1
        b = c % 2
        spans = spans_for_class(g)
        core_spans.append((b, spans))
        x_own = np.ascontiguousarray(
            np.concatenate([x[b, sp * SPAN:(sp + 1) * SPAN] for sp in spans], axis=0))
        m = np.zeros((NSLOT, 16, 128, 512), dtype=np.float32)
        for s in range(NSLOT):
            for j in range(16):
                kb = 16 * s + j
                m[s, j] = ((kb * 128 + p_idx) <= (spans[s] * 512 + f_idx))
        in_maps.append(dict(shared, x_own=x_own,
                            x_batch=np.ascontiguousarray(x[b]),
                            masks=m.astype(ml_dtypes.bfloat16)))

    nc = _get_nc()
    res = run_bass_kernel_spmd(nc, in_maps, core_ids=list(range(8)))

    out_full = np.empty((2, T, C), dtype=np.float32)
    for c in range(8):
        b, spans = core_spans[c]
        o = res.results[c]["out"]
        for si, sp in enumerate(spans):
            out_full[b, sp * SPAN:(sp + 1) * SPAN] = o[si * SPAN:(si + 1) * SPAN]
    return out_full
